# version 05
# baseline (speedup 1.0000x reference)
import math
import numpy as np
import concourse.bass as bass
import concourse.mybir as mybir
from concourse.bass_utils import run_bass_kernel_spmd

F32 = mybir.dt.float32
BF16 = mybir.dt.bfloat16
I32 = mybir.dt.int32
AF = mybir.ActivationFunctionType
ALU = mybir.AluOpType
AX = mybir.AxisListType

B, S, D = 4, 2048, 2048
TOK = 1024
NT = TOK // 128
ND = D // 128
NCORES = 8


_CM_METHODS = {"register", "If_eq", "If_ne", "If_lt", "Else", "While", "Fori"}


class _DummyVal:
    def __call__(self, *a, **k):
        return self


class _DummyCM:
    def __enter__(self):
        return _DummyVal()

    def __exit__(self, *a):
        return False


class _Ins:
    __slots__ = ("E", "real")

    def __init__(self, E, real):
        self.E = E
        self.real = real

    def then_inc(self, sem, n=1):
        if self.E.counting:
            self.E.counts[id(sem)] = self.E.counts.get(id(sem), 0) + n
        if self.real is not None:
            self.real.then_inc(sem, n)
        return self


_AUTO_OPS = {
    "tensor_scalar", "tensor_tensor", "tensor_copy", "tensor_reduce",
    "reduce_max", "reduce_sum", "reciprocal", "scalar_tensor_tensor",
    "memset", "memzero", "select", "tensor_add", "tensor_mul", "tensor_sub",
    "tensor_max", "tensor_relu", "bn_stats", "bn_aggr", "activation", "copy",
    "mul", "add", "square", "sqrt", "sign", "iota", "affine_select",
    "tensor_scalar_mul", "tensor_scalar_add", "tensor_single_scalar",
    "tensor_tensor_scan", "stream_shuffle", "transpose",
}
_AUTO_ENGS = {"vector", "scalar", "gpsimd"}


class _Eng:
    def __init__(self, E, name):
        self._E = E
        self._name = name

    @property
    def _active(self):
        return self._E.cur == self._name

    def __getattr__(self, attr):
        E = self._E
        name = self._name
        active = self._active
        if attr in _CM_METHODS:
            if active:
                return getattr(E.eng, attr)
            return lambda *a, **k: _DummyCM()
        if attr in ("alloc_register", "value_load", "scalar_reg_alu"):
            if active:
                return getattr(E.eng, attr)
            return lambda *a, **k: _DummyVal()
        auto = name in _AUTO_ENGS and attr in _AUTO_OPS
        if active:
            real = getattr(E.eng, attr)

            def call(*a, **k):
                r = real(*a, **k)
                if auto:
                    sem = E.selfsem[name]
                    c = E.c(sem)
                    if c > 0:
                        r._wait_ge(sem, c)
                    ins = _Ins(E, r)
                    ins.then_inc(sem, 1)
                    return ins
                return _Ins(E, r)

            return call

        def dummy(*a, **k):
            ins = _Ins(E, None)
            if auto:
                ins.then_inc(E.selfsem[name], 1)
            return ins

        return dummy


class Emit:
    def __init__(self, nc, block):
        self.nc = nc
        self.block = block
        self.cur = None
        self.eng = None
        self.counting = True
        self.counts = {}
        self.selfsem = {}
        self.sync = _Eng(self, "sync")
        self.tensor = _Eng(self, "tensor")
        self.vector = _Eng(self, "vector")
        self.scalar = _Eng(self, "scalar")
        self.gpsimd = _Eng(self, "gpsimd")

    def c(self, sem):
        return self.counts.get(id(sem), 0)

    def bump(self, sem, n):
        if self.counting:
            self.counts[id(sem)] = self.counts.get(id(sem), 0) + n

    class _NoCount:
        def __init__(self, E):
            self.E = E

        def __enter__(self):
            self.E.counting = False

        def __exit__(self, *a):
            self.E.counting = True
            return False

    def nocount(self):
        return Emit._NoCount(self)

    def build(self, fn):
        order = ["sync", "tensor", "vector", "scalar", "gpsimd"]
        for name in order:
            deco = getattr(self.block, name)

            def body(eng, name=name):
                self.cur = name
                self.eng = eng
                self.counts = {}
                fn(self)

            deco(body)




def emit_load_x(E, r, x_in):
    E.sync.dma_start(out=r.x_sb[:, :, :],
                     in_=x_in.rearrange("(t p) d -> p t d", p=128)).then_inc(r.s_x, 16)


def emit_store_x(E, r, x_out):
    E.sync.wait_ge(r.s_v, E.c(r.s_v))
    E.sync.wait_ge(r.s_s, E.c(r.s_s))
    E.sync.dma_start(out=x_out.rearrange("(t p) d -> p t d", p=128),
                     in_=r.x_sb[:, :, :]).then_inc(r.s_x, 16)


def emit_ln_T(E, r, eps=1e-5, out_tok=None):
    v, sc, t_ = E.vector, E.scalar, E.tensor
    xw = E.c(r.s_x)
    ww0 = E.c(r.s_w)
    s0 = E.c(r.s_s)
    for t in range(NT):
        if t == 0:
            v.wait_ge(r.s_x, xw)
            sc.wait_ge(r.s_x, xw)
        v.tensor_scalar(r.scr_bf[:, :], r.x_sb[:, t, :], 1.0, 0.0, ALU.mult,
                        ALU.add,
                        accum_out=r.stat[:, 4 * t + 0: 4 * t + 1])
        sc.activation(r.scr2_bf[:, :], r.x_sb[:, t, :], AF.Square,
                      accum_out=r.stat[:, 4 * t + 1: 4 * t + 2])
    for t in range(NT):
        v.tensor_scalar(r.stat[:, 4 * t + 0: 4 * t + 1],
                        r.stat[:, 4 * t + 0: 4 * t + 1], -1.0 / D, None,
                        ALU.mult)
        v.wait_ge(r.s_s, s0 + t + 1)
        v.tensor_scalar(r.stat[:, 4 * t + 1: 4 * t + 2],
                        r.stat[:, 4 * t + 1: 4 * t + 2], 1.0 / D, None,
                        ALU.mult)
        v.tensor_tensor(r.stat[:, 4 * t + 2: 4 * t + 3],
                        r.stat[:, 4 * t + 0: 4 * t + 1],
                        r.stat[:, 4 * t + 0: 4 * t + 1], ALU.mult)
        v.tensor_tensor(r.stat[:, 4 * t + 1: 4 * t + 2],
                        r.stat[:, 4 * t + 1: 4 * t + 2],
                        r.stat[:, 4 * t + 2: 4 * t + 3], ALU.subtract)
        v.tensor_scalar(r.stat[:, 4 * t + 1: 4 * t + 2],
                        r.stat[:, 4 * t + 1: 4 * t + 2], eps, None,
                        ALU.add)
        sc.wait_ge(r.s_v, E.c(r.s_v))
        sc.activation(r.stat[:, 4 * t + 3: 4 * t + 4],
                      r.stat[:, 4 * t + 1: 4 * t + 2], AF.Ln)
        v.wait_ge(r.s_s, E.c(r.s_s))
        v.tensor_scalar(r.stat[:, 4 * t + 3: 4 * t + 4],
                        r.stat[:, 4 * t + 3: 4 * t + 4], -0.5, None,
                        ALU.mult)
        sc.wait_ge(r.s_v, E.c(r.s_v))
        sc.activation(r.stat[:, 4 * t + 3: 4 * t + 4],
                      r.stat[:, 4 * t + 3: 4 * t + 4], AF.Exp)
        v.wait_ge(r.s_s, E.c(r.s_s))
        if t > 0:
            v.wait_ge(r.s_t, E.c(r.s_t))
        v.tensor_scalar(r.xstage[:, :], r.x_sb[:, t, :],
                        r.stat[:, 4 * t + 0: 4 * t + 1],
                        r.stat[:, 4 * t + 3: 4 * t + 4],
                        ALU.add, ALU.mult)
        if out_tok is not None:
            v.tensor_scalar(out_tok[:, t, :], r.x_sb[:, t, :],
                            r.stat[:, 4 * t + 0: 4 * t + 1],
                            r.stat[:, 4 * t + 3: 4 * t + 4],
                            ALU.add, ALU.mult)
        t_.wait_ge(r.s_v, E.c(r.s_v))
        if t == 0:
            t_.wait_ge(r.s_w, ww0)
        for dc in range(ND):
            i = dc % 2
            if dc >= 2:
                t_.wait_ge(r.s_v, E.c(r.s_v) - 1)
            t_.transpose(r.ps_tr[i][:, :], r.xstage[:, dc * 128:(dc + 1) * 128],
                         r.ident[:, :]).then_inc(r.s_t, 1)
            v.wait_ge(r.s_t, E.c(r.s_t))
            v.tensor_copy(r.xnT[:, dc, t * 128:(t + 1) * 128],
                          r.ps_tr[i][:, :])



FS = 2816
FE = 1408
NFE = FE // 128
NEXP = 8
NWK = 2
NWD = 2
ARN_BYTES = 71296


def emit_projT(E, r, wap, nf, dst, dst_off=0, act=None, bias_col=None,
               scale_imm=None, rhsT=None, bias_off=0):
    t_, v, sc, g = E.tensor, E.vector, E.scalar, E.gpsimd
    rhsT = r.xnT if rhsT is None else rhsT
    t0 = E.c(r.s_t)
    evsem = r.s_s if act == "silu" else r.s_v
    e0 = E.c(evsem)
    wv = []
    for m in range(nf):
        need = r.wk_lastuse.get(m % NWK, 0)
        if m >= NWK:
            need = max(need, t0 + 2 * (m - NWK + 1))
        if need > 0:
            g.wait_ge(r.s_t, need)
        sem = r.s_wp[m % NWK]
        g.dma_start(out=r.wk[m % NWK][:, :, :],
                    in_=wap[:, :, m * 128:(m + 1) * 128]).then_inc(sem, 16)
        wv.append(E.c(sem))
    t_.wait_ge(r.s_s, E.c(r.s_s))
    t_.wait_ge(r.s_v, E.c(r.s_v))
    for m in range(nf):
        t_.wait_ge(r.s_wp[m % NWK], wv[m])
        for half in range(2):
            i = 2 * m + half
            pb = i % 2
            if i >= 2:
                t_.wait_ge(evsem, e0 + i - 1)
            for kc in range(ND):
                ins = t_.matmul(r.ps_a[pb][:, :], r.wk[m % NWK][:, kc, :],
                                rhsT[:, kc, half * 512:(half + 1) * 512],
                                start=(kc == 0), stop=(kc == ND - 1))
                if kc == ND - 1:
                    ins.then_inc(r.s_t, 1)
            r.wk_lastuse[m % NWK] = E.c(r.s_t)
            dslice = dst[:, dst_off + m, half * 512:(half + 1) * 512]
            bc = None if bias_col is None else \
                bias_col[:, bias_off + m:bias_off + m + 1]
            if act == "silu":
                sc.wait_ge(r.s_t, E.c(r.s_t))
                if bc is None:
                    sc.activation(dslice, r.ps_a[pb][:, :], AF.Silu)
                else:
                    sc.activation(dslice, r.ps_a[pb][:, :], AF.Silu, bias=bc)
            else:
                v.wait_ge(r.s_t, E.c(r.s_t))
                if bc is not None and scale_imm is not None:
                    v.tensor_scalar(dslice, r.ps_a[pb][:, :], bc,
                                    scale_imm, ALU.add, ALU.mult)
                elif bc is not None:
                    v.tensor_scalar(dslice, r.ps_a[pb][:, :], bc, None,
                                    ALU.add)
                elif scale_imm is not None:
                    v.tensor_scalar(dslice, r.ps_a[pb][:, :], scale_imm, None,
                                    ALU.mult)
                else:
                    v.tensor_copy(dslice, r.ps_a[pb][:, :])


def emit_moe_up(E, r, wap, nf, silu, dst):
    emit_projT(E, r, wap, nf, dst, act="silu" if silu else None)


def emit_proj_tok(E, r, wap, F, dst, bias_row=None, rhsT=None, scale_imm=None,
                  pre_evac_wait=(), lora=None, act=None, epilogue=None,
                  epilogue_sem=None):
    t_, v, g = E.tensor, E.vector, E.gpsimd
    rhsT = r.xnT if rhsT is None else rhsT
    nchunks = F // 512
    t0 = E.c(r.s_t)
    wv = []
    if lora is not None:
        t_.wait_ge(r.s_w, E.c(r.s_w))
    evsem = epilogue_sem if epilogue_sem is not None else (
        r.s_s if (act == "sigmoid" or epilogue is not None) else r.s_v)
    nwt = len(r.wtok)
    e0 = E.c(evsem)
    for nchunk in range(nchunks):
        need = r.wtok_lastuse.get(nchunk % nwt, 0)
        if nchunk >= nwt:
            need = max(need, t0 + NT * (nchunk - nwt + 1))
        if need > 0:
            g.wait_ge(r.s_t, need)
        sem = r.s_wtp[nchunk % len(r.wtok)]
        g.dma_start(out=r.wtok[nchunk % len(r.wtok)][:, :, :],
                    in_=wap.rearrange("p c (n f) -> p c n f", f=512)[:, :, nchunk, :]
                    ).then_inc(sem, 16)
        wv.append(E.c(sem))
    t_.wait_ge(r.s_s, E.c(r.s_s))
    t_.wait_ge(r.s_v, E.c(r.s_v))
    for nchunk in range(nchunks):
        t_.wait_ge(r.s_wtp[nchunk % len(r.wtok)], wv[nchunk])
        for tt in range(NT):
            i = nchunk * NT + tt
            pb = i % 2
            if i >= 2:
                t_.wait_ge(evsem, e0 + i - 1)
            for kc in range(ND):
                last = (kc == ND - 1) and bias_row is None and lora is None
                ins = t_.matmul(r.ps_a[pb][:, :],
                                rhsT[:, kc, tt * 128:(tt + 1) * 128],
                                r.wtok[nchunk % len(r.wtok)][:, kc, :],
                                start=(kc == 0), stop=last)
                if last:
                    ins.then_inc(r.s_t, 1)
            if lora is not None:
                l1T, wBt = lora
                t_.matmul(r.ps_a[pb][:, :], l1T[:, tt * 128:(tt + 1) * 128],
                          wBt[:, nchunk * 512:(nchunk + 1) * 512],
                          start=False, stop=(bias_row is None)
                          ).then_inc(r.s_t, 1 if bias_row is None else 0)
                if bias_row is None:
                    E.bump(r.s_t, 0)
            if bias_row is not None:
                t_.matmul(r.ps_a[pb][:, :], r.ones_row[0:1, 0:128],
                          bias_row[0:1, nchunk * 512:(nchunk + 1) * 512],
                          start=False, stop=True).then_inc(r.s_t, 1)
            r.wtok_lastuse[nchunk % len(r.wtok)] = E.c(r.s_t)
            if nchunk == 0 and tt == 0:
                for sem_, val_ in pre_evac_wait:
                    v.wait_ge(sem_, val_)
            if epilogue is not None:
                epilogue(tt, nchunk, pb)
            elif act == "sigmoid":
                sc = E.scalar
                sc.wait_ge(r.s_t, E.c(r.s_t))
                sc.activation(dst[:, tt, nchunk * 512:(nchunk + 1) * 512],
                              r.ps_a[pb][:, :], AF.Sigmoid)
            elif scale_imm is not None:
                v.wait_ge(r.s_t, E.c(r.s_t))
                v.tensor_scalar(dst[:, tt, nchunk * 512:(nchunk + 1) * 512],
                                r.ps_a[pb][:, :], scale_imm, None, ALU.mult)
            else:
                v.wait_ge(r.s_t, E.c(r.s_t))
                v.tensor_copy(dst[:, tt, nchunk * 512:(nchunk + 1) * 512],
                              r.ps_a[pb][:, :])


def emit_moe_down(E, r, wap, nf, hbuf):
    t_, v, g = E.tensor, E.vector, E.gpsimd
    t0 = E.c(r.s_t)
    e0 = E.c(r.s_v)
    wv = []
    for nchunk in range(4):
        nwd = len(r.wd)
        need = r.wd_lastuse.get(nchunk % nwd, 0)
        if nchunk >= nwd:
            need = max(need, t0 + NT * (nchunk - nwd + 1))
        if need > 0:
            g.wait_ge(r.s_t, need)
        sem = r.s_wdp[nchunk % len(r.wd)]
        g.dma_start(out=r.wd[nchunk % nwd][:, :nf, :],
                    in_=wap[:, :, nchunk * 512:(nchunk + 1) * 512]).then_inc(sem, 16)
        wv.append(E.c(sem))
    t_.wait_ge(r.s_s, E.c(r.s_s))
    t_.wait_ge(r.s_v, e0)
    for nchunk in range(4):
        t_.wait_ge(r.s_wdp[nchunk % len(r.wd)], wv[nchunk])
        for tt in range(NT):
            i = nchunk * NT + tt
            pb = i % 2
            if i >= 2:
                t_.wait_ge(r.s_v, e0 + i - 1)
            for kc in range(nf):
                ins = t_.matmul(r.ps_a[pb][:, :],
                                hbuf[:, kc, tt * 128:(tt + 1) * 128],
                                r.wd[nchunk % len(r.wd)][:, kc, :],
                                start=(kc == 0), stop=(kc == nf - 1))
                if kc == nf - 1:
                    ins.then_inc(r.s_t, 1)
            r.wd_lastuse[nchunk % len(r.wd)] = E.c(r.s_t)
            v.wait_ge(r.s_t, E.c(r.s_t))
            v.tensor_tensor(r.x_sb[:, tt, nchunk * 512:(nchunk + 1) * 512],
                            r.x_sb[:, tt, nchunk * 512:(nchunk + 1) * 512],
                            r.ps_a[pb][:, :], ALU.add)


def emit_moe_routing(E, r, w):
    t_, v, sc, g = E.tensor, E.vector, E.scalar, E.gpsimd
    g.dma_start(out=r.wrT_sb[:, :, :],
                in_=w["wrT"].rearrange("(c p) e -> p c e", p=128)).then_inc(r.s_w, 16)
    g.wait_ge(r.s_w, E.c(r.s_w))
    ww = E.c(r.s_w)
    vxn = E.c(r.s_v)
    s0 = E.c(r.s_s)
    for tt in range(NT):
        t_.wait_ge(r.s_w, ww)
        if tt == 0:
            t_.wait_ge(r.s_v, vxn)
        if tt >= 2:
            t_.wait_ge(r.s_s, s0 + tt - 1)
        for kc in range(ND):
            ins = t_.matmul(r.ps_a[tt % 2][:, 0:8],
                            r.xnT[:, kc, tt * 128:(tt + 1) * 128],
                            r.wrT_sb[:, kc, :],
                            start=(kc == 0), stop=(kc == ND - 1))
            if kc == ND - 1:
                ins.then_inc(r.s_t, 1)
        st = r.mstat
        v.wait_ge(r.s_t, E.c(r.s_t))
        v.tensor_reduce(st[:, 0:1], r.ps_a[tt % 2][:, 0:8], AX.X, ALU.max,
                        negate=True)
        sc.wait_ge(r.s_v, E.c(r.s_v))
        sc.activation(r.wsm[:, 0:8], r.ps_a[tt % 2][:, 0:8], AF.Exp,
                      bias=st[:, 0:1],
                      accum_out=st[:, 1:2])
        v.wait_ge(r.s_s, E.c(r.s_s))
        v.reciprocal(st[:, 2:3], st[:, 1:2])
        v.tensor_scalar(r.wsm[:, 0:8], r.wsm[:, 0:8], st[:, 2:3], None,
                        ALU.mult)
        v.tensor_reduce(st[:, 3:4], r.wsm[:, 0:8], AX.X, ALU.max)
        v.tensor_scalar(r.mtmp[:, 0:8], r.wsm[:, 0:8], st[:, 3:4], None,
                        ALU.is_ge)
        v.scalar_tensor_tensor(r.mtmp[:, 8:16], r.mtmp[:, 0:8], -2.0,
                               r.wsm[:, 0:8], ALU.mult, ALU.add)
        v.tensor_reduce(st[:, 4:5], r.mtmp[:, 8:16], AX.X, ALU.max)
        v.tensor_scalar(r.mtmp[:, 8:16], r.mtmp[:, 8:16], st[:, 4:5], None,
                        ALU.is_ge)
        v.tensor_tensor(r.mtmp[:, 0:8], r.mtmp[:, 0:8], r.mtmp[:, 8:16],
                        ALU.add)
        v.tensor_tensor(r.wfb[:, tt, 0:8], r.wsm[:, 0:8],
                        r.mtmp[:, 0:8], ALU.mult)
        t_.wait_ge(r.s_v, E.c(r.s_v))
        t_.transpose(r.ps_w[0:8, :], r.wfb[:, tt, 0:8],
                     r.ident[:, :]).then_inc(r.s_t, 1)
        v.wait_ge(r.s_t, E.c(r.s_t))
        v.tensor_copy(r.wT8[0:8, tt * 128:(tt + 1) * 128],
                      r.ps_w[0:8, :])


def emit_moe(E, r, w):
    t_, v, sc, g = E.tensor, E.vector, E.scalar, E.gpsimd
    emit_moe_routing(E, r, w)
    wsg = w["wsgT"].rearrange("(c p) f -> p c f", p=128)
    wsu = w["wsuT"].rearrange("(c p) f -> p c f", p=128)
    wsd = w["wsdT"].rearrange("(c p) d -> p c d", p=128)
    wg = w["wgT"].rearrange("e (c p) f -> p e c f", p=128)
    wu = w["wuT"].rearrange("e (c p) f -> p e c f", p=128)
    wd = w["wdT"].rearrange("e (c p) d -> p e c d", p=128)
    for o in range(2):
        emit_moe_up(E, r, wsg[:, :, o * FE:(o + 1) * FE], NFE, True, r.gbuf)
        emit_moe_up(E, r, wsu[:, :, o * FE:(o + 1) * FE], NFE, False, r.ubuf)
        v0 = E.c(r.s_s)
        for m in range(NFE):
            if m == 0:
                v.wait_ge(r.s_s, v0)
            v.tensor_tensor(r.gbuf[:, m, :], r.gbuf[:, m, :], r.ubuf[:, m, :],
                            ALU.mult)
        t_.wait_ge(r.s_v, E.c(r.s_v))
        emit_moe_down(E, r, wsd[:, o * NFE:(o + 1) * NFE, :], NFE, r.gbuf)
    for e in range(NEXP):
        emit_moe_up(E, r, wg[:, e, :, :], NFE, True, r.gbuf)
        emit_moe_up(E, r, wu[:, e, :, :], NFE, False, r.ubuf)
        t_.wait_ge(r.s_v, E.c(r.s_v))
        for half in range(2):
            ins = t_.matmul(r.ps_g[half][:, :], r.sel8_sb[:, e * 128:(e + 1) * 128],
                            r.wT8[:, half * 512:(half + 1) * 512],
                            start=True, stop=True)
            if half == 1:
                ins.then_inc(r.s_t, 1)
        v.wait_ge(r.s_t, E.c(r.s_t))
        for half in range(2):
            v.tensor_copy(r.gatebc[:, half * 512:(half + 1) * 512],
                          r.ps_g[half][:, :])
        v0 = E.c(r.s_s)
        for m in range(NFE):
            if m == 0:
                v.wait_ge(r.s_s, v0)
            v.tensor_tensor(r.gbuf[:, m, :], r.gbuf[:, m, :], r.ubuf[:, m, :],
                            ALU.mult)
            v.tensor_tensor(r.gbuf[:, m, :], r.gbuf[:, m, :], r.gatebc[:, :],
                            ALU.mult)
        t_.wait_ge(r.s_v, E.c(r.s_v))
        emit_moe_down(E, r, wd[:, e, :, :], NFE, r.gbuf)



NH1B = 16
HG = 4
NHG = NH1B // HG


def emit_oproj_add(E, r, wap, bias_col, rhsT, nf=ND, bias_off=0, lora=None):
    t_, v, g = E.tensor, E.vector, E.gpsimd
    if lora is not None:
        t_.wait_ge(r.s_w, E.c(r.s_w))
    t0 = E.c(r.s_t)
    wv = []
    for m in range(nf):
        need = r.wk_lastuse.get(m % NWK, 0)
        if m >= NWK:
            need = max(need, t0 + (2 + NT) * (m - NWK) + 2)
        if need > 0:
            g.wait_ge(r.s_t, need)
        sem = r.s_wp[m % NWK]
        g.dma_start(out=r.wk[m % NWK][:, :, :],
                    in_=wap[:, :, m * 128:(m + 1) * 128]).then_inc(sem, 16)
        wv.append(E.c(sem))
    t_.wait_ge(r.s_s, E.c(r.s_s))
    t_.wait_ge(r.s_v, E.c(r.s_v))
    for m in range(nf):
        t_.wait_ge(r.s_wp[m % NWK], wv[m])
        for half in range(2):
            i = 2 * m + half
            pb = i % 2
            if i >= 2:
                t_.wait_ge(r.s_v, E.c(r.s_v) - NT - 1)
            for kc in range(ND):
                last = (kc == ND - 1) and lora is None
                ins = t_.matmul(r.ps_a[pb][:, :], r.wk[m % NWK][:, kc, :],
                                rhsT[:, kc, half * 512:(half + 1) * 512],
                                start=(kc == 0), stop=last)
                if last:
                    ins.then_inc(r.s_t, 1)
            if lora is not None:
                l1T, wBt = lora
                t_.matmul(r.ps_a[pb][:, :],
                          wBt[:, m * 128:(m + 1) * 128],
                          l1T[:, half * 512:(half + 1) * 512],
                          start=False, stop=True).then_inc(r.s_t, 1)
            v.wait_ge(r.s_t, E.c(r.s_t))
            if bias_col is None:
                v.tensor_copy(r.ystage[:, half * 512:(half + 1) * 512],
                              r.ps_a[pb][:, :])
            else:
                v.tensor_scalar(r.ystage[:, half * 512:(half + 1) * 512],
                                r.ps_a[pb][:, :],
                                bias_col[:, bias_off + m:bias_off + m + 1],
                                None, ALU.add)
        r.wk_lastuse[m % NWK] = E.c(r.s_t)
        t_.wait_ge(r.s_v, E.c(r.s_v))
        for tt in range(NT):
            pb = tt % 2
            if tt >= 2:
                t_.wait_ge(r.s_v, E.c(r.s_v) - 1)
            t_.transpose(r.ps_tr[pb][:, :], r.ystage[:, tt * 128:(tt + 1) * 128],
                         r.ident[:, :]).then_inc(r.s_t, 1)
            v.wait_ge(r.s_t, E.c(r.s_t))
            v.tensor_tensor(r.x_sb[:, tt, m * 128:(m + 1) * 128],
                            r.x_sb[:, tt, m * 128:(m + 1) * 128],
                            r.ps_tr[pb][:, :], ALU.add)


def emit_proj_tok64(E, r, wap, F, dst64, bias_row=None, rhsT=None):
    t_, v, g = E.tensor, E.vector, E.gpsimd
    rhsT = r.xnT if rhsT is None else rhsT
    assert F == 512
    sem = r.s_wtp[0]
    need = r.wtok_lastuse.get(0, 0)
    if need > 0:
        g.wait_ge(r.s_t, need)
    g.dma_start(out=r.wtok[0][:, :, :], in_=wap).then_inc(sem, 16)
    wv = E.c(sem)
    e0 = E.c(r.s_v)
    t_.wait_ge(sem, wv)
    t_.wait_ge(r.s_s, E.c(r.s_s))
    t_.wait_ge(r.s_v, e0)
    for cc in range(2 * NT):
        tt, sub = cc // 2, cc % 2
        pb = cc % 2
        if cc >= 2:
            t_.wait_ge(r.s_v, e0 + cc - 1)
        for kc in range(ND):
            ins = t_.matmul(r.ps_a[pb][0:64, :],
                            rhsT[:, kc, tt * 128 + sub * 64:
                                 tt * 128 + sub * 64 + 64],
                            r.wtok[0][:, kc, :],
                            start=(kc == 0), stop=(kc == ND - 1 and bias_row is None))
            if kc == ND - 1 and bias_row is None:
                ins.then_inc(r.s_t, 1)
        if bias_row is not None:
            t_.matmul(r.ps_a[pb][0:64, :], r.ones_row[0:1, 0:64],
                      bias_row[0:1, 0:512],
                      start=False, stop=True).then_inc(r.s_t, 1)
        r.wtok_lastuse[0] = E.c(r.s_t)
        v.wait_ge(r.s_t, E.c(r.s_t))
        v.tensor_copy(dst64[0:64, cc, :], r.ps_a[pb][0:64, :])


def emit_l1ab(E, r, w):
    t_, v, sc, g = E.tensor, E.vector, E.scalar, E.gpsimd
    qw = w["l1ab_qwT"].rearrange("(c p) f -> p c f", p=128)
    kw = w["l1ab_kwT"].rearrange("(c p) f -> p c f", p=128)
    vw = w["l1ab_vwT"].rearrange("(c p) f -> p c f", p=128)
    ogw = w["l1ab_ogwT"].rearrange("(c p) f -> p c f", p=128)
    ow = w["l1ab_owT"].rearrange("(c p) f -> p c f", p=128)
    qb = r.l1ab_cols
    kb = r.l1ab_cols
    ob = r.l1ab_cols
    vb = r.l1ab_vbrow[0:1, :]
    bw = E.c(r.s_w)
    v0beta = E.c(r.s_v)
    for cc in range(2 * NT):
        tt, sub = cc // 2, cc % 2
        t_.wait_ge(r.s_w, bw)
        if cc == 0:
            t_.wait_ge(r.s_v, v0beta)
        if cc >= 2:
            t_.wait_ge(r.s_v, v0beta + 5 * (cc - 1))
        for kc in range(ND):
            ins = t_.matmul(r.ps_a[cc % 2][0:64, 0:16],
                            r.xnT[:, kc, tt * 128 + sub * 64:
                                  tt * 128 + sub * 64 + 64],
                            r.bwT_sb[:, kc, :],
                            start=(kc == 0), stop=False)
        t_.matmul(r.ps_a[cc % 2][0:64, 0:16], r.ones_row[0:1, 0:64],
                  r.bb16[0:1, :], start=False, stop=True).then_inc(r.s_t, 1)
        v.wait_ge(r.s_t, E.c(r.s_t))
        v.tensor_scalar(r.mtmp[0:64, 16:32], r.ps_a[cc % 2][0:64, 0:16],
                        -1.0, None, ALU.mult)
        v.tensor_tensor(r.mtmp[0:64, 0:16], r.ps_a[cc % 2][0:64, 0:16],
                        r.mtmp[0:64, 16:32], ALU.min)
        sc.wait_ge(r.s_v, E.c(r.s_v))
        sc.activation(r.mtmp[0:64, 0:16], r.mtmp[0:64, 0:16], AF.Exp)
        v.wait_ge(r.s_s, E.c(r.s_s))
        v.tensor_scalar(r.mtmp[0:64, 0:16], r.mtmp[0:64, 0:16], 1.0, None,
                        ALU.add)
        sc.wait_ge(r.s_v, E.c(r.s_v))
        sc.activation(r.mtmp[0:64, 0:16], r.mtmp[0:64, 0:16], AF.Ln)
        v.wait_ge(r.s_s, E.c(r.s_s))
        v.tensor_scalar(r.mtmp[0:64, 16:32],
                        r.ps_a[cc % 2][0:64, 0:16], 0.0, None, ALU.max)
        v.tensor_tensor(r.betab[0:64, cc, :], r.mtmp[0:64, 16:32],
                        r.mtmp[0:64, 0:16], ALU.add)
    SCALE = float(128) ** 0.5
    for hg in range(NHG):
        emit_projT(E, r, qw[:, :, hg * 512:(hg + 1) * 512], HG, r.gbuf,
                   dst_off=0, bias_col=qb, bias_off=0 + hg * 4)
        emit_projT(E, r, kw[:, :, hg * 512:(hg + 1) * 512], HG, r.gbuf,
                   dst_off=4, bias_col=kb, bias_off=16 + hg * 4)
        emit_proj_tok64(E, r, vw[:, :, hg * 512:(hg + 1) * 512], 512, r.vtok,
                        bias_row=vb[0:1, hg * 512:(hg + 1) * 512])
        vv0 = E.c(r.s_v)
        for cc in range(2 * NT):
            if cc == 0:
                v.wait_ge(r.s_v, vv0)
            for hl in range(HG):
                h = hg * HG + hl
                v.tensor_scalar(r.vtok[0:64, cc, hl * 128:(hl + 1) * 128],
                                r.vtok[0:64, cc, hl * 128:(hl + 1) * 128],
                                r.betab[0:64, cc, h:h + 1], None, ALU.mult)
        for qk in range(2):
            for hl in range(HG):
                ch = qk * 4 + hl
                sc.wait_ge(r.s_v, E.c(r.s_v))
                sc.activation(r.scr2_bf[:, 0:1024], r.gbuf[:, ch, :], AF.Square)
                t_.wait_ge(r.s_s, E.c(r.s_s))
                t_.wait_ge(r.s_v, E.c(r.s_v))
                for half in range(2):
                    t_.matmul(r.ps_g[half][0:1, :], r.ones_col[:, 0:1],
                              r.scr2_bf[:, half * 512:(half + 1) * 512],
                              start=True, stop=True).then_inc(r.s_t, 1)
                for half in range(2):
                    v.wait_ge(r.s_t, E.c(r.s_t))
                    v.tensor_copy(r.nrm[0:1, :], r.ps_g[half][0:1, :])
                    sc.wait_ge(r.s_v, E.c(r.s_v))
                    sc.activation(r.nrm[0:1, :], r.nrm[0:1, :], AF.Sqrt)
                    v.wait_ge(r.s_s, E.c(r.s_s))
                    v.tensor_scalar(r.nrm[0:1, :], r.nrm[0:1, :],
                                    SCALE * 1e-12, None, ALU.max)
                    sc.wait_ge(r.s_v, E.c(r.s_v))
                    sc.activation(r.nrm[0:1, :], r.nrm[0:1, :], AF.Ln)
                    v.wait_ge(r.s_s, E.c(r.s_s))
                    v.tensor_scalar(r.nrm[0:1, :], r.nrm[0:1, :], -1.0, None,
                                    ALU.mult)
                    sc.wait_ge(r.s_v, E.c(r.s_v))
                    sc.activation(r.nrm[0:1, :], r.nrm[0:1, :], AF.Exp)
                    v.wait_ge(r.s_s, E.c(r.s_s))
                    v.tensor_copy(r.nrm_bf[0:1, :], r.nrm[0:1, :])
                    t_.wait_ge(r.s_v, E.c(r.s_v))
                    t_.matmul(r.ps_g[half][:, :], r.ones_row[0:1, 0:128],
                              r.nrm_bf[0:1, :],
                              start=True, stop=True).then_inc(r.s_t, 1)
                    v.wait_ge(r.s_t, E.c(r.s_t))
                    v.tensor_tensor(r.gbuf[:, ch, half * 512:(half + 1) * 512],
                                    r.gbuf[:, ch, half * 512:(half + 1) * 512],
                                    r.ps_g[half][:, :], ALU.mult)
        av0 = E.c(r.s_v)
        for hl in range(HG):
            h = hg * HG + hl
            for cc in range(16):
                i = hl * 16 + cc
                pb = i % 2
                if i == 0:
                    t_.wait_ge(r.s_v, av0)
                if i >= 2:
                    t_.wait_ge(r.s_v, E.c(r.s_v) - 1)
                t_.matmul(r.ps_s[pb][0:64, 0:64],
                          r.gbuf[:, 4 + hl, cc * 64:(cc + 1) * 64],
                          r.gbuf[:, hl, cc * 64:(cc + 1) * 64],
                          start=True, stop=True).then_inc(r.s_t, 1)
                v.wait_ge(r.s_t, E.c(r.s_t))
                v.tensor_tensor(r.attn64[0:64, :], r.ps_s[pb][0:64, 0:64],
                                r.decayT[0:64, :], ALU.mult)
                t_.wait_ge(r.s_v, E.c(r.s_v))
                t_.matmul(r.ps_o[pb][:, 0:64],
                          r.vtok[0:64, cc, hl * 128:(hl + 1) * 128],
                          r.attn64[0:64, :],
                          start=True, stop=True).then_inc(r.s_t, 1)
                v.wait_ge(r.s_t, E.c(r.s_t))
                v.tensor_copy(r.attT[:, h, cc * 64:(cc + 1) * 64],
                              r.ps_o[pb][:, 0:64])
        if getattr(r, "DBG_NOGATE", False):
            continue
        emit_projT(E, r, ogw[:, :, hg * 512:(hg + 1) * 512], HG, r.gbuf,
                   dst_off=4, act="silu")
        gv0 = E.c(r.s_s)
        for hl in range(HG):
            h = hg * HG + hl
            if hl == 0:
                v.wait_ge(r.s_s, gv0)
            v.tensor_tensor(r.attT[:, h, :], r.attT[:, h, :],
                            r.gbuf[:, 4 + hl, :], ALU.mult)
    if not getattr(r, "DBG_NOGATE", False):
        emit_oproj_add(E, r, ow, ob, r.attT, bias_off=32)



def emit_loraT(E, r, wAt_ap, dst, scale, rhsT=None, nf=ND):
    t_, v, g = E.tensor, E.vector, E.gpsimd
    rhsT = r.xnT if rhsT is None else rhsT
    g.wait_ge(r.s_t, E.c(r.s_t))
    g.dma_start(out=r.lA_sb[:, :, :], in_=wAt_ap).then_inc(r.s_w, 16)
    g.wait_ge(r.s_w, E.c(r.s_w))
    ww = E.c(r.s_w)
    t_.wait_ge(r.s_w, ww)
    t_.wait_ge(r.s_v, E.c(r.s_v))
    for half in range(2):
        for kc in range(nf):
            ins = t_.matmul(r.ps_d[half][0:8, :], r.lA_sb[:, kc, :],
                            rhsT[:, kc, half * 512:(half + 1) * 512],
                            start=(kc == 0), stop=(kc == nf - 1))
            if kc == nf - 1:
                ins.then_inc(r.s_t, 1)
    v.wait_ge(r.s_t, E.c(r.s_t))
    for half in range(2):
        v.tensor_scalar(dst[0:8, half * 512:(half + 1) * 512],
                        r.ps_d[half][0:8, :], scale, None, ALU.mult)


def emit_rope_interleaved(E, r, stage, ctab, stab, width):
    v, g = E.vector, E.gpsimd
    nh = width // 128
    for t in range(NT):
        pb = t % len(r.ropeCw)
        need = r.rope_lastuse.get(pb, 0)
        if need > 0:
            g.wait_ge(r.s_v, need)
        sem = r.s_wp[pb]
        sem = r.s_wp[pb]
        g.dma_start(out=r.ropeCw[pb][:, 0:width],
                    in_=ctab.rearrange("(t p) w -> p t w", p=128)[:, t, 0:width]
                    ).then_inc(sem, 16)
        g.dma_start(out=r.ropeSw[pb][:, 0:width],
                    in_=stab.rearrange("(t p) w -> p t w", p=128)[:, t, 0:width]
                    ).then_inc(sem, 16)
        wv = E.c(sem)
        qv = stage[:, t, 0:width].rearrange("p (d two) -> p d two", two=2)
        tmp = r.ropetmpw[:, 0:width].rearrange("p (d two) -> p d two", two=2)
        v.wait_ge(sem, wv)
        v.tensor_scalar(tmp[:, :, 0:1], qv[:, :, 1:2], -1.0, None, ALU.mult)
        v.tensor_scalar(tmp[:, :, 1:2], qv[:, :, 0:1], 1.0, None, ALU.mult)
        v.tensor_tensor(stage[:, t, 0:width], stage[:, t, 0:width],
                        r.ropeCw[pb][:, 0:width], ALU.mult)
        v.tensor_tensor(r.ropetmpw[:, 0:width], r.ropetmpw[:, 0:width],
                        r.ropeSw[pb][:, 0:width], ALU.mult)
        v.tensor_tensor(stage[:, t, 0:width], stage[:, t, 0:width],
                        r.ropetmpw[:, 0:width], ALU.add)
        r.rope_lastuse[pb] = E.c(r.s_v)


def emit_l0aa(E, r, w, ex):
    t_, v, sc, g, sy = E.tensor, E.vector, E.scalar, E.gpsimd, E.sync
    qw = w["l0aa_qwT"].rearrange("(c p) f -> p c f", p=128)
    kw = w["l0aa_kwT"].rearrange("(c p) f -> p c f", p=128)
    vw = w["l0aa_vwT"].rearrange("(c p) f -> p c f", p=128)
    ow = w["l0aa_owT"].rearrange("(c p) f -> p c f", p=128)
    SC = 1.0 / math.sqrt(128.0)
    emit_loraT(E, r, w["l0aa_qAt"].rearrange("(c p) e -> p c e", p=128),
               r.l1q, 2.0)
    emit_loraT(E, r, w["l0aa_kAt"].rearrange("(c p) e -> p c e", p=128),
               r.l1k, 2.0)
    emit_loraT(E, r, w["l0aa_vAt"].rearrange("(c p) e -> p c e", p=128),
               r.l1v, 2.0)
    emit_proj_tok(E, r, qw, D, r.qstage, scale_imm=SC,
                  lora=(r.l1q, w2sb(E, r, "l0aa_qBt", w)))
    emit_rope_interleaved(E, r, r.qstage, w["l0aa_ropeCw"], w["l0aa_ropeSw"], D)
    sy.wait_ge(r.s_v, E.c(r.s_v))
    sy.dma_start(out=r.qscr.ap().rearrange("(t p) d -> p t d", p=128),
                 in_=r.qstage[:, :, :]).then_inc(r.s_x, 16)
    qscr_done = E.c(r.s_x)
    emit_proj_tok(E, r, kw, 1024, r.kstage, pre_evac_wait=[(r.s_x, qscr_done)],
                  lora=(r.l1k, w2sb(E, r, "l0aa_kBt", w)))
    emit_rope_interleaved(E, r, r.kstage, w["l0aa_ropeCw"], w["l0aa_ropeSw"], 1024)
    sy.wait_ge(r.s_v, E.c(r.s_v))
    emit_halved_dma(E, r,
                    r.ksh_aa.ap().rearrange("(t p) d -> p t d", p=128)[:, 0:NT, :],
                    r.ksh_aa.ap().rearrange("(t p) d -> p t d", p=128)[:, NT:2 * NT, :],
                    r.kstage[:, :, 0:1024], r.kstage[:, :, 0:1024], r.s_kv)
    emit_proj_tok(E, r, vw, 1024, r.kstage, pre_evac_wait=[(r.s_kv, E.c(r.s_kv))],
                  lora=(r.l1v, w2sb(E, r, "l0aa_vBt", w)))
    sy.wait_ge(r.s_v, E.c(r.s_v))
    for h in range(8):
        vsh_h0 = r.vsh_aa.ap()[h, 0, :, :].rearrange("(t p) d -> p t d", p=128)
        vsh_h1 = r.vsh_aa.ap()[h, 1, :, :].rearrange("(t p) d -> p t d", p=128)
        emit_halved_dma(E, r, vsh_h0, vsh_h1,
                        r.kstage[:, :, h * 128:(h + 1) * 128],
                        r.kstage[:, :, h * 128:(h + 1) * 128], r.s_kv)
    emit_flag_write(E, r, ex, [(r.s_kv, E.c(r.s_kv))])
    emit_flag_poll(E, r, ex)
    sy.wait_ge(r.s_x, qscr_done)
    sy.wait_ge(r.s_t, E.c(r.s_t))
    for c in range(ND):
        sy.dma_start(out=r.qT[:, c, :], in_=r.qscr.ap()[:, c * 128:(c + 1) * 128],
                     transpose=True).then_inc(r.s_x, 16)
    qT_done = E.c(r.s_x)
    for kvh in range(8):
        hb = kvh % 2
        if kvh >= 2:
            sy.wait_ge(r.s_t, r.kv_lastuse.get(hb, 0))
        ksh_m = r.ksh_aa.ap()[:, kvh * 128:(kvh + 1) * 128]
        with sy.If_eq(r.rg_parity, 0):
            with E.nocount():
                sy.dma_start(out=r.kt[hb][:, 0:1024], in_=ksh_m[0:1024, :],
                             transpose=True).then_inc(r.s_kv, 16)
                sy.dma_start(out=r.kt[hb][:, 1024:2048], in_=ksh_m[1024:2048, :],
                             transpose=True).then_inc(r.s_kv, 16)
                sy.dma_start(out=r.vt[hb][:, 0:8, :],
                             in_=r.vsh_aa.ap()[kvh, 0, :, :].rearrange(
                                 "(c p) d -> p c d", p=128)).then_inc(r.s_kv, 16)
                sy.dma_start(out=r.vt[hb][:, 8:16, :],
                             in_=r.vsh_aa.ap()[kvh, 1, :, :].rearrange(
                                 "(c p) d -> p c d", p=128)).then_inc(r.s_kv, 16)
        with sy.Else():
            sy.dma_start(out=r.kt[hb][:, 0:1024], in_=ksh_m[1024:2048, :],
                         transpose=True).then_inc(r.s_kv, 16)
            sy.dma_start(out=r.kt[hb][:, 1024:2048], in_=ksh_m[0:1024, :],
                         transpose=True).then_inc(r.s_kv, 16)
            sy.dma_start(out=r.vt[hb][:, 0:8, :],
                         in_=r.vsh_aa.ap()[kvh, 1, :, :].rearrange(
                             "(c p) d -> p c d", p=128)).then_inc(r.s_kv, 16)
            sy.dma_start(out=r.vt[hb][:, 8:16, :],
                         in_=r.vsh_aa.ap()[kvh, 0, :, :].rearrange(
                             "(c p) d -> p c d", p=128)).then_inc(r.s_kv, 16)
        kvhc = E.c(r.s_kv)
        t_.wait_ge(r.s_kv, kvhc)
        for h in (2 * kvh, 2 * kvh + 1):
            t_.wait_ge(r.s_v, E.c(r.s_v))
            if kvh == 0 and h == 0:
                t_.wait_ge(r.s_x, qT_done)
            e0 = E.c(r.s_s)
            pv_done = {}
            fix_done = {}
            for kc in range(ND):
                for half in range(2):
                    i = kc * 2 + half
                    pb = i % 2
                    if i >= 2:
                        t_.wait_ge(r.s_s, e0 + i - 1)
                    ins = t_.matmul(r.ps_a[pb][:, :],
                                    r.kt[hb][:, kc * 128:(kc + 1) * 128],
                                    r.qT[:, h, half * 512:(half + 1) * 512],
                                    start=True, stop=True)
                    ins.then_inc(r.s_t, 1)
                    if kc >= 2:
                        sc.wait_ge(r.s_t, pv_done[kc - 2])
                        if kc - 2 in fix_done:
                            sc.wait_ge(r.s_v, fix_done[kc - 2])
                    sc.wait_ge(r.s_t, E.c(r.s_t))
                    sc.activation(r.at_sb[:, kc % 2, half * 512:(half + 1) * 512],
                                  r.ps_a[pb][:, :], AF.Exp,
                                  bias=r.l0ab_mb[:, kc:kc + 1])
                if kc < 8:
                    v.wait_ge(r.s_s, E.c(r.s_s))
                    if kc > 0:
                        v.memset(r.at_sb[:, kc % 2, 0:kc * 128], 0.0)
                    v.tensor_tensor(r.at_sb[:, kc % 2, kc * 128:(kc + 1) * 128],
                                    r.at_sb[:, kc % 2, kc * 128:(kc + 1) * 128],
                                    r.triu[:, :], ALU.mult)
                    fix_done[kc] = E.c(r.s_v)
                t_.wait_ge(r.s_v, E.c(r.s_v))
                t_.wait_ge(r.s_s, E.c(r.s_s))
                for half in range(2):
                    t_.matmul(r.ps_d[half][0:1, :], r.ones_col[:, 0:1],
                              r.at_sb[:, kc % 2, half * 512:(half + 1) * 512],
                              start=(kc == 0), stop=(kc == ND - 1))
                    ins = t_.matmul(r.ps_g[half][:, :],
                                    r.vt[hb][:, kc, :],
                                    r.at_sb[:, kc % 2, half * 512:(half + 1) * 512],
                                    start=(kc == 0), stop=(kc == ND - 1))
                    ins.then_inc(r.s_t, 1)
                pv_done[kc] = E.c(r.s_t)
            r.kv_lastuse[hb] = E.c(r.s_t)
            emit_attn_norm(E, r, h)
    emit_loraT(E, r, w["l0aa_oAt"].rearrange("(c p) e -> p c e", p=128),
               r.l1q, 2.0, rhsT=r.attT)
    emit_oproj_add(E, r, ow, None, r.attT,
                   lora=(r.l1q, w2sb(E, r, "l0aa_oBt", w)))


def emit_attn_norm(E, r, h):
    t_, v, sc = E.tensor, E.vector, E.scalar
    v.wait_ge(r.s_t, E.c(r.s_t))
    for half, dstb in ((0, r.nrm_bf), (1, r.nrm_bf2)):
        v.tensor_copy(r.nrm[0:1, 0:512], r.ps_d[half][0:1, :])
        sc.wait_ge(r.s_v, E.c(r.s_v))
        sc.activation(r.nrm[0:1, 0:512], r.nrm[0:1, 0:512], AF.Ln)
        v.wait_ge(r.s_s, E.c(r.s_s))
        v.tensor_scalar(r.nrm[0:1, 0:512], r.nrm[0:1, 0:512], -1.0, None, ALU.mult)
        sc.wait_ge(r.s_v, E.c(r.s_v))
        sc.activation(r.nrm[0:1, 0:512], r.nrm[0:1, 0:512], AF.Exp)
        v.wait_ge(r.s_s, E.c(r.s_s))
        v.tensor_copy(dstb[0:1, 0:512], r.nrm[0:1, 0:512])
    t_.wait_ge(r.s_v, E.c(r.s_v))
    t_.matmul(r.ps_a[0][:, :], r.ones_row[0:1, 0:128],
              r.nrm_bf[0:1, 0:512], start=True, stop=True)
    t_.matmul(r.ps_a[1][:, :], r.ones_row[0:1, 0:128],
              r.nrm_bf2[0:1, 0:512], start=True, stop=True).then_inc(r.s_t, 1)
    v.wait_ge(r.s_t, E.c(r.s_t))
    for half in range(2):
        v.tensor_copy(r.bcbuf[:, half * 512:(half + 1) * 512],
                      r.ps_a[half][:, :])
    for half in range(2):
        v.tensor_tensor(r.attT[:, h, half * 512:(half + 1) * 512],
                        r.ps_g[half][:, :],
                        r.bcbuf[:, half * 512:(half + 1) * 512], ALU.mult)


def w2sb(E, r, name, w):
    F = w[name].shape[1]
    E.gpsimd.wait_ge(r.s_t, E.c(r.s_t))
    E.gpsimd.dma_start(out=r.lB_sb[:, 0:F], in_=w[name]).then_inc(r.s_w, 16)
    E.gpsimd.wait_ge(r.s_w, E.c(r.s_w))
    return r.lB_sb[:, 0:F]



H0F = 8192


def emit_l0ffn_prep(E, r, w):
    g, sy = E.gpsimd, E.sync
    for src, dstd, nrow, ncol in ((w["l0ffn_d1h"], r.d1c, D, 4096),
                                  (w["l0ffn_d2h"], r.d2c, 4096, D)):
        sap = src.rearrange("(c p) f -> p c f", p=128)
        dap = dstd.ap().rearrange("(c p) f -> p c f", p=128)
        nc_ = nrow // 128
        for c0 in range(0, nc_, 8):
            for f0 in range(0, ncol, 512):
                g.wait_ge(r.s_x, E.c(r.s_x))
                g.dma_start(out=r.wrow[:, :, :],
                            in_=sap[:, c0:c0 + 8, f0:f0 + 512]).then_inc(r.s_w, 16)
                g.wait_ge(r.s_w, E.c(r.s_w))
                sy.wait_ge(r.s_w, E.c(r.s_w))
                sy.dma_start(out=dap[:, c0:c0 + 8, f0:f0 + 512],
                             in_=r.wrow[:, :, :]).then_inc(r.s_x, 16)
    sy.wait_ge(r.s_x, E.c(r.s_x))
    r.d1c_ready = True


def _row_to_colT(E, r, row_bf, ncol, dst):
    t_, v = E.tensor, E.vector
    t_.wait_ge(r.s_v, E.c(r.s_v))
    for c in range(ncol // 128):
        pb = c % 2
        if c >= 2:
            t_.wait_ge(r.s_v, E.c(r.s_v) - 1)
        t_.transpose(r.ps_tr[pb][:, 0:1], row_bf[0:1, c * 128:(c + 1) * 128],
                     r.ident[0:1, 0:1]).then_inc(r.s_t, 1)
        v.wait_ge(r.s_t, E.c(r.s_t))
        v.tensor_copy(dst[:, c:c + 1], r.ps_tr[pb][:, 0:1])


def _row_ln(E, r, src_row, dst_row, width, eps, gam=None, bet=None):
    v, sc = E.vector, E.scalar
    st = r.mtmp
    v.tensor_scalar(r.scr2_bf[0:1, 0:width], src_row, 1.0, 0.0, ALU.mult,
                    ALU.add, accum_out=st[0:1, 0:1])
    v.tensor_scalar(st[0:1, 0:1], st[0:1, 0:1], -1.0 / width, None, ALU.mult)
    v.tensor_scalar(dst_row, src_row, st[0:1, 0:1], None, ALU.add)
    sc.wait_ge(r.s_v, E.c(r.s_v))
    sc.activation(r.scr2_bf[0:1, 0:width], dst_row, AF.Square,
                  accum_out=st[0:1, 1:2])
    v.wait_ge(r.s_s, E.c(r.s_s))
    v.tensor_scalar(st[0:1, 1:2], st[0:1, 1:2], 1.0 / width, None, ALU.mult)
    v.tensor_scalar(st[0:1, 1:2], st[0:1, 1:2], eps, None, ALU.add)
    sc.wait_ge(r.s_v, E.c(r.s_v))
    sc.activation(st[0:1, 2:3], st[0:1, 1:2], AF.Ln)
    v.wait_ge(r.s_s, E.c(r.s_s))
    v.tensor_scalar(st[0:1, 2:3], st[0:1, 2:3], -0.5, None, ALU.mult)
    sc.wait_ge(r.s_v, E.c(r.s_v))
    sc.activation(st[0:1, 2:3], st[0:1, 2:3], AF.Exp)
    v.wait_ge(r.s_s, E.c(r.s_s))
    v.tensor_scalar(dst_row, dst_row, st[0:1, 2:3], None, ALU.mult)
    if gam is not None:
        gr = gam()
        v.tensor_tensor(dst_row, dst_row, gr, ALU.mult)
        br = bet()
        v.tensor_tensor(dst_row, dst_row, br, ALU.add)


def _load_row(E, r, w, seg, which):
    g = E.gpsimd
    g.wait_ge(r.s_v, E.c(r.s_v))
    g.dma_start(out=r.rowc[which][0:1, :],
                in_=w["l0ffn_rows_in"][0:1, seg * D:(seg + 1) * D]).then_inc(r.s_w, 16)
    g.wait_ge(r.s_w, E.c(r.s_w))
    ww = E.c(r.s_w)
    E.vector.wait_ge(r.s_w, ww)
    return r.rowc[which][0:1, :]


def _row_stream_mm(E, r, wap3, colT, col0, ngrp, psum, first, last_extra=None):
    t_, g = E.tensor, E.gpsimd
    for grp in range(ngrp):
        g.wait_ge(r.s_t, r.wrow_lastuse)
        g.dma_start(out=r.wrow[:, :, :],
                    in_=wap3[:, grp * 8:(grp + 1) * 8, :]).then_inc(r.s_wtp[1 % len(r.s_wtp)], 16)
        sem = r.s_wtp[1 % len(r.s_wtp)]
        t_.wait_ge(sem, E.c(sem))
        t_.wait_ge(r.s_v, E.c(r.s_v))
        for kc in range(8):
            lastmm = (grp == ngrp - 1 and kc == 7 and last_extra is None)
            ins = t_.matmul(psum, colT[:, col0 + grp * 8 + kc:col0 + grp * 8 + kc + 1],
                            r.wrow[:, kc, :],
                            start=(first and grp == 0 and kc == 0), stop=lastmm)
            if lastmm:
                ins.then_inc(r.s_t, 1)
            elif kc == 7:
                ins.then_inc(r.s_t, 1)
        r.wrow_lastuse = E.c(r.s_t)
    if last_extra is not None:
        t_.matmul(psum, r.ones_row[0:1, 0:1], last_extra,
                  start=False, stop=True).then_inc(r.s_t, 1)
        r.wrow_lastuse = E.c(r.s_t)


def _row_proj_full(E, r, wap, out_row, bias_row):
    t_, v, g = E.tensor, E.vector, E.gpsimd
    for nchunk in range(4):
        _row_stream_mm(E, r,
                       wap.rearrange("p c (n f) -> p c n f", f=512)[:, :, nchunk, :],
                       r.tcolT, 0, 2, r.ps_d[0][0:1, :], True)
        v.wait_ge(r.s_t, E.c(r.s_t))
        v.tensor_copy(out_row[0:1, nchunk * 512:(nchunk + 1) * 512],
                      r.ps_d[0][0:1, :])
    v.tensor_tensor(out_row, out_row, bias_row, ALU.add)


def emit_l0ffn(E, r, w, ex, xntok):
    t_, v, sc, g, sy = E.tensor, E.vector, E.scalar, E.gpsimd, E.sync
    if not getattr(r, "d1c_ready", False):
        emit_l0ffn_prep(E, r, w)
    sy.wait_ge(r.s_v, E.c(r.s_v))
    sy.dma_start(out=r.rowscr.ap()[0:1, :],
                 in_=xntok[127:128, NT - 1, :]).then_inc(r.s_x, 16)
    sy.wait_ge(r.s_x, E.c(r.s_x))
    emit_halved_dma(E, r, r.lastrow_sh.ap()[0:1, :], r.lastrow_sh.ap()[1:2, :],
                    r.rowscr.ap()[0:1, :], r.rowscr.ap()[0:1, :], r.s_kv)
    emit_flag_write(E, r, ex, [(r.s_kv, E.c(r.s_kv))])
    emit_flag_poll(E, r, ex)
    sy.dma_start(out=r.row_bfin[0:1, :], in_=r.lastrow_sh.ap()[1:2, :]).then_inc(r.s_x, 16)
    v.wait_ge(r.s_x, E.c(r.s_x))
    _row_ln(E, r, r.row_bfin[0:1, :], r.row_b[0:1, :], D, 1e-6,
            lambda: _load_row(E, r, w, 0, 0), lambda: _load_row(E, r, w, 1, 0))
    v.tensor_copy(r.row_bf[0:1, :], r.row_b[0:1, :])
    _row_to_colT(E, r, r.row_bf[0:1, :], D, r.tcolT)
    aggb = _load_row(E, r, w, 2, 0)
    _row_proj_full(E, r, w["l0ffn_aggw"].rearrange("(c p) f -> p c f", p=128),
                   r.thought[0:1, :], aggb)
    d1 = r.d1c.ap().rearrange("(c p) f -> p c f", p=128)
    d2 = r.d2c.ap().rearrange("(c p) f -> p c f", p=128)
    for step in range(4):
        _row_ln(E, r, r.thought[0:1, :], r.row_b[0:1, :], D, 1e-6,
                lambda: _load_row(E, r, w, 3, 0), lambda: _load_row(E, r, w, 4, 0))
        v.tensor_copy(r.row_bf[0:1, :], r.row_b[0:1, :])
        _row_to_colT(E, r, r.row_bf[0:1, :], D, r.tcolT)
        for m in range(8):
            t_.wait_ge(r.s_s, E.c(r.s_s))
            _row_stream_mm(E, r, d1[:, :, m * 512:(m + 1) * 512], r.tcolT, 0, 2,
                           r.ps_d[0][0:1, :], True,
                           last_extra=r.b1h[0:1, m * 512:(m + 1) * 512])
            sc.wait_ge(r.s_t, E.c(r.s_t))
            sc.activation(r.t1h_bf[0:1, m * 512:(m + 1) * 512], r.ps_d[0][0:1, :],
                          AF.Gelu)
        sc_done = E.c(r.s_s)
        v.wait_ge(r.s_s, sc_done)
        _row_to_colT(E, r, r.t1h_bf[0:1, :], 4096, r.t1colT)
        for nchunk in range(4):
            _row_stream_mm(E, r, d2[:, :, nchunk * 512:(nchunk + 1) * 512],
                           r.t1colT, 0, 4, r.ps_d[0][0:1, :], True)
            v.wait_ge(r.s_t, E.c(r.s_t))
            v.tensor_copy(r.row_b[0:1, nchunk * 512:(nchunk + 1) * 512],
                          r.ps_d[0][0:1, :])
        sy.wait_ge(r.s_v, E.c(r.s_v))
        dsh = r.delta_sh.ap()
        emit_halved_dma(E, r, dsh[2 * step:2 * step + 1, :],
                        dsh[2 * step + 1:2 * step + 2, :],
                        r.row_b[0:1, :], r.row_b[0:1, :], r.s_kv)
        emit_flag_write(E, r, ex + 1 + step, [(r.s_kv, E.c(r.s_kv))])
        emit_flag_poll(E, r, ex + 1 + step)
        with sy.If_eq(r.rg_parity, 0):
            with E.nocount():
                sy.dma_start(out=r.row_a[0:1, :],
                             in_=dsh[2 * step + 1:2 * step + 2, :]).then_inc(r.s_x, 16)
        with sy.Else():
            sy.dma_start(out=r.row_a[0:1, :],
                         in_=dsh[2 * step:2 * step + 1, :]).then_inc(r.s_x, 16)
        v.wait_ge(r.s_x, E.c(r.s_x))
        d2b = _load_row(E, r, w, 5, 0)
        v.tensor_tensor(r.thought[0:1, :], r.thought[0:1, :], r.row_b[0:1, :],
                        ALU.add)
        v.tensor_tensor(r.thought[0:1, :], r.thought[0:1, :], r.row_a[0:1, :],
                        ALU.add)
        v.tensor_tensor(r.thought[0:1, :], r.thought[0:1, :], d2b, ALU.add)
    v.tensor_copy(r.row_bf[0:1, :], r.thought[0:1, :])
    _row_to_colT(E, r, r.row_bf[0:1, :], D, r.tcolT)
    bcb_row = _load_row(E, r, w, 6, 0)
    _row_proj_full(E, r, w["l0ffn_bcw"].rearrange("(c p) f -> p c f", p=128),
                   r.row_b[0:1, :], bcb_row)
    v.tensor_copy(r.row_bf[0:1, :], r.row_b[0:1, :])
    for nchunk in range(4):
        t_.wait_ge(r.s_v, E.c(r.s_v))
        t_.matmul(r.ps_g[nchunk % 2][:, :], r.ones_row[0:1, 0:128],
                  r.row_bf[0:1, nchunk * 512:(nchunk + 1) * 512],
                  start=True, stop=True).then_inc(r.s_t, 1)
        v.wait_ge(r.s_t, E.c(r.s_t))
        v.tensor_copy(r.bcb[:, nchunk * 512:(nchunk + 1) * 512],
                      r.ps_g[nchunk % 2][:, :])
    emit_gate_out(E, r, w["l0ffn_gatew"].rearrange("(c p) f -> p c f", p=128),
                  xntok)
    emit_l0ffn_outln(E, r, w, xntok)


def emit_gate_out(E, r, wap, xntok):
    t_, v, sc, g = E.tensor, E.vector, E.gpsimd if False else E.scalar, E.gpsimd
    sc = E.scalar
    g = E.gpsimd
    t0 = E.c(r.s_t)
    wv = []
    for m in range(ND):
        need = r.wk_lastuse.get(m % NWK, 0)
        if m >= NWK:
            need = max(need, t0 + (2 + NT) * (m - NWK) + 2)
        if need > 0:
            g.wait_ge(r.s_t, need)
        sem = r.s_wp[m % NWK]
        g.dma_start(out=r.wk[m % NWK][:, :, :],
                    in_=wap[:, :, m * 128:(m + 1) * 128]).then_inc(sem, 16)
        wv.append(E.c(sem))
    t_.wait_ge(r.s_s, E.c(r.s_s))
    t_.wait_ge(r.s_v, E.c(r.s_v))
    for m in range(ND):
        t_.wait_ge(r.s_wp[m % NWK], wv[m])
        for half in range(2):
            i = 2 * m + half
            pb = i % 2
            if i >= 2:
                t_.wait_ge(r.s_s, E.c(r.s_s) - NT * 0 - 1)
            for kc in range(ND):
                ins = t_.matmul(r.ps_a[pb][:, :], r.wk[m % NWK][:, kc, :],
                                r.xnT[:, kc, half * 512:(half + 1) * 512],
                                start=(kc == 0), stop=(kc == ND - 1))
                if kc == ND - 1:
                    ins.then_inc(r.s_t, 1)
            sc.wait_ge(r.s_t, E.c(r.s_t))
            sc.activation(r.ystage[:, half * 512:(half + 1) * 512],
                          r.ps_a[pb][:, :], AF.Sigmoid,
                          bias=r.l0ffn_gcol[:, m:m + 1])
        r.wk_lastuse[m % NWK] = E.c(r.s_t)
        t_.wait_ge(r.s_s, E.c(r.s_s))
        for tt in range(NT):
            pb = tt % 2
            if tt >= 2:
                t_.wait_ge(r.s_v, E.c(r.s_v) - 3)
            t_.transpose(r.ps_tr[pb][:, :], r.ystage[:, tt * 128:(tt + 1) * 128],
                         r.ident[:, :]).then_inc(r.s_t, 1)
            v.wait_ge(r.s_t, E.c(r.s_t))
            v.tensor_tensor(r.scr2_bf[:, 0:128], r.ps_tr[pb][:, :],
                            r.bcb[:, m * 128:(m + 1) * 128], ALU.mult)
            v.tensor_tensor(xntok[:, tt, m * 128:(m + 1) * 128],
                            xntok[:, tt, m * 128:(m + 1) * 128],
                            r.scr2_bf[:, 0:128], ALU.add)


def emit_l0ffn_outln(E, r, w, xntok, eps=1e-6):
    t_, v, sc = E.tensor, E.vector, E.scalar
    row = _load_row(E, r, w, 7, 0)
    v.tensor_copy(r.row_bf[0:1, :], row)
    for nchunk in range(4):
        t_.wait_ge(r.s_v, E.c(r.s_v))
        t_.matmul(r.ps_g[nchunk % 2][:, :],
                  r.ones_row[0:1, 0:128],
                  r.row_bf[0:1, nchunk * 512:(nchunk + 1) * 512],
                  start=True, stop=True).then_inc(r.s_t, 1)
        v.wait_ge(r.s_t, E.c(r.s_t))
        v.tensor_copy(r.bcb[:, nchunk * 512:(nchunk + 1) * 512],
                      r.ps_g[nchunk % 2][:, :])
    row = _load_row(E, r, w, 8, 0)
    v.tensor_copy(r.row_bf[0:1, :], row)
    t_.wait_ge(r.s_v, E.c(r.s_v))
    bbanks = [r.ps_a[0], r.ps_a[1], r.ps_g[0], r.ps_g[1]]
    for nchunk in range(4):
        t_.matmul(bbanks[nchunk][:, :], r.ones_row[0:1, 0:128],
                  r.row_bf[0:1, nchunk * 512:(nchunk + 1) * 512],
                  start=True, stop=True).then_inc(r.s_t, 1)
    s0 = E.c(r.s_s)
    v.wait_ge(r.s_t, E.c(r.s_t))
    for t in range(NT):
        v.tensor_scalar(r.scr_bf[:, :], xntok[:, t, :], 1.0, 0.0, ALU.mult,
                        ALU.add, accum_out=r.stat[:, 4 * t + 0: 4 * t + 1])
        if t == 0:
            sc.wait_ge(r.s_v, E.c(r.s_v))
        sc.activation(r.scr2_bf[:, :], xntok[:, t, :], AF.Square,
                      accum_out=r.stat[:, 4 * t + 1: 4 * t + 2])
    for t in range(NT):
        v.tensor_scalar(r.stat[:, 4 * t + 0: 4 * t + 1],
                        r.stat[:, 4 * t + 0: 4 * t + 1], -1.0 / D, None, ALU.mult)
        v.wait_ge(r.s_s, s0 + t + 1)
        v.tensor_scalar(r.stat[:, 4 * t + 1: 4 * t + 2],
                        r.stat[:, 4 * t + 1: 4 * t + 2], 1.0 / D, None, ALU.mult)
        v.tensor_tensor(r.stat[:, 4 * t + 2: 4 * t + 3],
                        r.stat[:, 4 * t + 0: 4 * t + 1],
                        r.stat[:, 4 * t + 0: 4 * t + 1], ALU.mult)
        v.tensor_tensor(r.stat[:, 4 * t + 1: 4 * t + 2],
                        r.stat[:, 4 * t + 1: 4 * t + 2],
                        r.stat[:, 4 * t + 2: 4 * t + 3], ALU.subtract)
        v.tensor_scalar(r.stat[:, 4 * t + 1: 4 * t + 2],
                        r.stat[:, 4 * t + 1: 4 * t + 2], eps, None, ALU.add)
        sc.wait_ge(r.s_v, E.c(r.s_v))
        sc.activation(r.stat[:, 4 * t + 3: 4 * t + 4],
                      r.stat[:, 4 * t + 1: 4 * t + 2], AF.Ln)
        v.wait_ge(r.s_s, E.c(r.s_s))
        v.tensor_scalar(r.stat[:, 4 * t + 3: 4 * t + 4],
                        r.stat[:, 4 * t + 3: 4 * t + 4], -0.5, None, ALU.mult)
        sc.wait_ge(r.s_v, E.c(r.s_v))
        sc.activation(r.stat[:, 4 * t + 3: 4 * t + 4],
                      r.stat[:, 4 * t + 3: 4 * t + 4], AF.Exp)
        v.wait_ge(r.s_s, E.c(r.s_s))
        v.tensor_scalar(r.scr2_bf[:, :], xntok[:, t, :],
                        r.stat[:, 4 * t + 0: 4 * t + 1],
                        r.stat[:, 4 * t + 3: 4 * t + 4], ALU.add, ALU.mult)
        v.tensor_tensor(r.scr2_bf[:, :], r.scr2_bf[:, :], r.bcb[:, :], ALU.mult)
        for nchunk in range(4):
            v.tensor_tensor(r.scr2_bf[:, nchunk * 512:(nchunk + 1) * 512],
                            r.scr2_bf[:, nchunk * 512:(nchunk + 1) * 512],
                            bbanks[nchunk][:, :], ALU.add)
        v.tensor_tensor(r.x_sb[:, t, :], r.x_sb[:, t, :], r.scr2_bf[:, :], ALU.add)



SC128 = 1.0 / math.sqrt(128.0)


def emit_l1aa(E, r, w, ex):
    t_, v, sc, g, sy = E.tensor, E.vector, E.scalar, E.gpsimd, E.sync
    attT_ap = r.attT.ap() if callable(getattr(r.attT, "ap", None)) else r.attT
    av = attT_ap.rearrange("p j (g h m) -> p g h j m", g=2, h=16, m=32)
    qw = w["l1aa_qw"].rearrange("(c p) f -> p c f", p=128)
    kw = w["l1aa_kw"].rearrange("(c p) f -> p c f", p=128)
    vw = w["l1aa_vw"].rearrange("(c p) f -> p c f", p=128)
    ow = w["l1aa_ow"].rearrange("(c p) f -> p c f", p=128)
    v.wait_ge(r.s_w, E.c(r.s_w))
    sc.wait_ge(r.s_w, E.c(r.s_w))
    sc.wait_ge(r.s_v, E.c(r.s_v))
    sc.activation(r.betcol[:, :], r.l1aa_bet_sb[:, :], AF.Sigmoid)
    v.wait_ge(r.s_s, E.c(r.s_s))
    v.tensor_scalar(r.ombet[:, :], r.betcol[:, :], -1.0, 1.0, ALU.mult, ALU.add)
    for hg in range(NHG):
        exi = ex + hg
        emit_projT(E, r, qw[:, :, hg * 512:(hg + 1) * 512], HG, r.gbuf, dst_off=0)
        emit_projT(E, r, kw[:, :, hg * 512:(hg + 1) * 512], HG, r.gbuf, dst_off=4)
        r.wtok_lastuse[0] = max(r.wtok_lastuse.get(0, 0), E.c(r.s_t))
        emit_proj_tok(E, r, vw[:, :, hg * 512:(hg + 1) * 512], 512, r.vtk)
        pv0 = E.c(r.s_v)
        for ch in range(8):
            if ch == 0:
                v.wait_ge(r.s_v, pv0)
                v.wait_ge(r.s_s, E.c(r.s_s))
            v.tensor_scalar(r.xstage[:, 0:1024], r.gbuf[:, ch, :], 0.0, None,
                            ALU.min)
            sc.wait_ge(r.s_v, E.c(r.s_v))
            sc.activation(r.xstage[:, 1024:2048], r.xstage[:, 0:1024], AF.Exp)
            v.tensor_scalar(r.sgbuf[:, ch, :], r.gbuf[:, ch, :], 0.0, None,
                            ALU.max)
            v.wait_ge(r.s_s, E.c(r.s_s))
            v.tensor_tensor(r.sgbuf[:, ch, :], r.sgbuf[:, ch, :],
                            r.xstage[:, 1024:2048], ALU.add)
        t_.wait_ge(r.s_v, E.c(r.s_v))
        for hl in range(HG):
            for kc in range(8):
                i = hl * 8 + kc
                pb = i % 2
                if i >= 2:
                    t_.wait_ge(r.s_v, E.c(r.s_v) - 1)
                t_.transpose(r.ps_tr[pb][:, :],
                             r.sgbuf[:, 4 + hl, kc * 128:(kc + 1) * 128],
                             r.ident[:, :]).then_inc(r.s_t, 1)
                v.wait_ge(r.s_t, E.c(r.s_t))
                v.tensor_copy(r.sktok[:, kc, hl * 128:(hl + 1) * 128],
                              r.ps_tr[pb][:, :])
        for hl in range(HG):
            h = hg * HG + hl
            for seg in range(2):
                q0 = seg * 512
                e0 = E.c(r.s_s)
                pvd = {}
                fxd = {}
                t_.wait_ge(r.s_v, E.c(r.s_v))
                t_.wait_ge(r.s_s, E.c(r.s_s))
                for kc in range(4):
                    pb = kc % 2
                    if kc >= 2:
                        t_.wait_ge(r.s_s, e0 + kc - 1)
                    t_.matmul(r.ps_a[pb][:, :],
                              r.gbuf[:, 4 + hl, q0 + kc * 128:q0 + (kc + 1) * 128],
                              r.gbuf[:, hl, q0:q0 + 512],
                              start=True, stop=True).then_inc(r.s_t, 1)
                    if kc >= 2:
                        sc.wait_ge(r.s_t, pvd[kc - 2])
                        sc.wait_ge(r.s_v, fxd[kc - 2])
                    sc.wait_ge(r.s_t, E.c(r.s_t))
                    sc.activation(r.aat[:, kc % 2, :], r.ps_a[pb][:, :],
                                  AF.Exp, scale=SC128)
                    v.wait_ge(r.s_s, E.c(r.s_s))
                    if kc > 0:
                        v.memset(r.aat[:, kc % 2, 0:kc * 128], 0.0)
                    v.tensor_tensor(r.aat[:, kc % 2, kc * 128:(kc + 1) * 128],
                                    r.aat[:, kc % 2, kc * 128:(kc + 1) * 128],
                                    r.triu[:, :], ALU.mult)
                    fxd[kc] = E.c(r.s_v)
                    t_.wait_ge(r.s_v, E.c(r.s_v))
                    t_.wait_ge(r.s_s, E.c(r.s_s))
                    t_.matmul(r.ps_d[0][0:1, :], r.ones_col[:, 0:1],
                              r.aat[:, kc % 2, :],
                              start=(kc == 0), stop=(kc == 3))
                    ins = t_.matmul(r.ps_g[0][:, :],
                                    r.vtk[:, seg * 4 + kc, hl * 128:(hl + 1) * 128],
                                    r.aat[:, kc % 2, :],
                                    start=(kc == 0), stop=(kc == 3))
                    ins.then_inc(r.s_t, 1)
                    pvd[kc] = E.c(r.s_t)
                v.wait_ge(r.s_t, E.c(r.s_t))
                v.tensor_copy(r.nrm[0:1, 0:512], r.ps_d[0][0:1, :])
                sc.wait_ge(r.s_v, E.c(r.s_v))
                sc.activation(r.nrm[0:1, 0:512], r.nrm[0:1, 0:512], AF.Ln)
                v.wait_ge(r.s_s, E.c(r.s_s))
                v.tensor_scalar(r.nrm[0:1, 0:512], r.nrm[0:1, 0:512], -1.0,
                                None, ALU.mult)
                sc.wait_ge(r.s_v, E.c(r.s_v))
                sc.activation(r.nrm[0:1, 0:512], r.nrm[0:1, 0:512], AF.Exp)
                v.wait_ge(r.s_s, E.c(r.s_s))
                v.tensor_copy(r.nrm_bf[0:1, 0:512], r.nrm[0:1, 0:512])
                t_.wait_ge(r.s_v, E.c(r.s_v))
                t_.matmul(r.ps_g[1][:, :], r.ones_row[0:1, 0:128],
                          r.nrm_bf[0:1, 0:512], start=True,
                          stop=True).then_inc(r.s_t, 1)
                v.wait_ge(r.s_t, E.c(r.s_t))
                v.tensor_copy(r.scr2_bf[:, 1024:1536], r.ps_g[1][:, :])
                v.tensor_tensor(
                    av[:, seg, h, :, :],
                    r.ps_g[0][:, :].rearrange("p (m j) -> p j m", m=32),
                    r.scr2_bf[:, 1024:1536].rearrange("p (m j) -> p j m", m=32),
                    ALU.mult)
        for ph in range(2):
            if ph == 0:
                v.wait_ge(r.s_t, E.c(r.s_t))
                v.memset(r.mem_f[:, :], 0.0)
                v.memset(r.z_f[:, 0:4], 1.0 / 128.0)
            else:
                sy.wait_ge(r.s_v, E.c(r.s_v))
                sy.dma_start(out=r.mem_f[:, :],
                             in_=r.state_sh.ap()[hg, 0, :, 0:512]).then_inc(r.s_x, 16)
                sy.dma_start(out=r.z_f[:, 0:4],
                             in_=r.state_sh.ap()[hg, 0, :, 512:516]).then_inc(r.s_x, 16)
                v.wait_ge(r.s_x, E.c(r.s_x))
            v.tensor_copy(r.mem_b[:, :], r.mem_f[:, :])
            v.tensor_copy(r.z_b[:, 0:4], r.z_f[:, 0:4])
            for hl in range(HG):
                h = hg * HG + hl
                for seg in range(2):
                    q0 = seg * 512
                    t_.wait_ge(r.s_v, E.c(r.s_v))
                    t_.matmul(r.ps_a[0][:, :], r.mem_b[:, hl * 128:(hl + 1) * 128],
                              r.sgbuf[:, hl, q0:q0 + 512],
                              start=True, stop=True)
                    t_.matmul(r.ps_d[0][0:1, :], r.z_b[:, hl:hl + 1],
                              r.sgbuf[:, hl, q0:q0 + 512],
                              start=True, stop=True).then_inc(r.s_t, 1)
                    v.wait_ge(r.s_t, E.c(r.s_t))
                    v.tensor_copy(r.nrm[0:1, 0:512], r.ps_d[0][0:1, :])
                    sc.wait_ge(r.s_v, E.c(r.s_v))
                    sc.activation(r.nrm[0:1, 0:512], r.nrm[0:1, 0:512], AF.Ln)
                    v.wait_ge(r.s_s, E.c(r.s_s))
                    v.tensor_scalar(r.nrm[0:1, 0:512], r.nrm[0:1, 0:512], -1.0,
                                    None, ALU.mult)
                    sc.wait_ge(r.s_v, E.c(r.s_v))
                    sc.activation(r.nrm[0:1, 0:512], r.nrm[0:1, 0:512], AF.Exp)
                    v.wait_ge(r.s_s, E.c(r.s_s))
                    v.tensor_copy(r.nrm_bf[0:1, 0:512], r.nrm[0:1, 0:512])
                    t_.wait_ge(r.s_v, E.c(r.s_v))
                    t_.matmul(r.ps_g[1][:, :], r.ones_row[0:1, 0:128],
                              r.nrm_bf[0:1, 0:512], start=True,
                              stop=True).then_inc(r.s_t, 1)
                    v.wait_ge(r.s_t, E.c(r.s_t))
                    v.tensor_copy(r.scr2_bf[:, 1024:1536], r.ps_g[1][:, :])
                    slot = hl * 2 + seg
                    if ph == 0:
                        v.tensor_tensor(r.amem[:, slot, :], r.ps_a[0][:, :],
                                        r.scr2_bf[:, 1024:1536], ALU.mult)
                    else:
                        v.tensor_tensor(r.scr2_bf[:, 1536:2048], r.ps_a[0][:, :],
                                        r.scr2_bf[:, 1024:1536], ALU.mult)
                        v.tensor_scalar(r.amem[:, slot, :], r.amem[:, slot, :],
                                        r.selcol[:, 0:1], None, ALU.mult)
                        v.tensor_scalar(r.scr2_bf[:, 1536:2048],
                                        r.scr2_bf[:, 1536:2048],
                                        r.selcol[:, 1:2], None, ALU.mult)
                        v.tensor_tensor(r.amem[:, slot, :], r.amem[:, slot, :],
                                        r.scr2_bf[:, 1536:2048], ALU.add)
                    for c in range(4):
                        t_.wait_ge(r.s_v, E.c(r.s_v))
                        t_.matmul(r.ps_g[0][:, 0:128],
                                  r.sgbuf[:, 4 + hl, q0 + c * 128:q0 + (c + 1) * 128],
                                  r.mem_b[:, hl * 128:(hl + 1) * 128],
                                  start=True, stop=True)
                        t_.matmul(r.ps_a[1][:, 0:1],
                                  r.sgbuf[:, 4 + hl, q0 + c * 128:q0 + (c + 1) * 128],
                                  r.z_b[:, hl:hl + 1],
                                  start=True, stop=True).then_inc(r.s_t, 1)
                        v.wait_ge(r.s_t, E.c(r.s_t))
                        v.reciprocal(r.mtmp[:, 0:1], r.ps_a[1][:, 0:1])
                        v.tensor_scalar(r.scr2_bf[:, 1536:1664], r.ps_g[0][:, 0:128],
                                        r.mtmp[:, 0:1], None, ALU.mult)
                        v.tensor_tensor(r.delta[:, c, :],
                                        r.vtk[:, seg * 4 + c, hl * 128:(hl + 1) * 128],
                                        r.scr2_bf[:, 1536:1664], ALU.subtract)
                    t_.wait_ge(r.s_v, E.c(r.s_v))
                    for c in range(4):
                        t_.matmul(r.ps_g[0][:, 0:128],
                                  r.sktok[:, seg * 4 + c, hl * 128:(hl + 1) * 128],
                                  r.delta[:, c, :],
                                  start=(c == 0), stop=(c == 3))
                        ins = t_.matmul(r.ps_a[1][:, 0:1],
                                        r.sktok[:, seg * 4 + c, hl * 128:(hl + 1) * 128],
                                        r.ones_col[:, 0:1],
                                        start=(c == 0), stop=(c == 3))
                        if c == 3:
                            ins.then_inc(r.s_t, 1)
                    v.wait_ge(r.s_t, E.c(r.s_t))
                    v.tensor_tensor(r.mem_f[:, hl * 128:(hl + 1) * 128],
                                    r.mem_f[:, hl * 128:(hl + 1) * 128],
                                    r.ps_g[0][:, 0:128], ALU.add)
                    v.tensor_tensor(r.z_f[:, hl:hl + 1], r.z_f[:, hl:hl + 1],
                                    r.ps_a[1][:, 0:1], ALU.add)
                    v.tensor_copy(r.mem_b[:, hl * 128:(hl + 1) * 128],
                                  r.mem_f[:, hl * 128:(hl + 1) * 128])
                    v.tensor_copy(r.z_b[:, hl:hl + 1], r.z_f[:, hl:hl + 1])
            if ph == 0:
                sy.wait_ge(r.s_v, E.c(r.s_v))
                emit_halved_dma(E, r, r.state_sh.ap()[hg, 0, :, 0:512],
                                r.state_sh.ap()[hg, 1, :, 0:512],
                                r.mem_f[:, :], r.mem_f[:, :], r.s_kv)
                emit_halved_dma(E, r, r.state_sh.ap()[hg, 0, :, 512:516],
                                r.state_sh.ap()[hg, 1, :, 512:516],
                                r.z_f[:, 0:4], r.z_f[:, 0:4], r.s_kv)
                emit_flag_write(E, r, exi, [(r.s_kv, E.c(r.s_kv))])
                emit_flag_poll(E, r, exi)
        for hl in range(HG):
            h = hg * HG + hl
            for seg in range(2):
                slot = hl * 2 + seg
                v.tensor_scalar(av[:, seg, h, :, :], av[:, seg, h, :, :],
                                r.ombet[:, h:h + 1], None, ALU.mult)
                v.tensor_scalar(r.scr2_bf[:, 1024:1536], r.amem[:, slot, :],
                                r.betcol[:, h:h + 1], None, ALU.mult)
                v.tensor_tensor(
                    av[:, seg, h, :, :], av[:, seg, h, :, :],
                    r.scr2_bf[:, 1024:1536].rearrange("p (m j) -> p j m", m=32),
                    ALU.add)
    r.wtok_lastuse[0] = max(r.wtok_lastuse.get(0, 0), E.c(r.s_t))

    def oproj_ep(tt, nchunk, pb):
        v.wait_ge(r.s_t, E.c(r.s_t))
        v.tensor_tensor(r.x_sb[:, tt, nchunk * 512:(nchunk + 1) * 512],
                        r.x_sb[:, tt, nchunk * 512:(nchunk + 1) * 512],
                        r.ps_a[pb][:, :], ALU.add)
    emit_proj_tok(E, r, ow, D, None, rhsT=r.attT, epilogue=oproj_ep,
                  epilogue_sem=r.s_v)



NFLAGS = 64


def emit_exch_init(E, r, meta_in):
    sy = E.sync
    sy.dma_start(out=r.meta_sb[:, :], in_=meta_in).then_inc(r.s_x, 16)
    sy.wait_ge(r.s_x, E.c(r.s_x))
    r.rg_parity = sy.alloc_register("rg_parity")
    sy.reg_load(r.rg_parity, r.meta_sb[0:1, 0:1])


def emit_flag_write(E, r, idx, wait_sems):
    sy = E.sync
    for sem, val in wait_sems:
        sy.wait_ge(sem, val)
    with sy.If_eq(r.rg_parity, 0):
        with E.nocount():
            sy.dma_start(out=r.flags_sh.ap()[2 * idx:2 * idx + 1, :],
                         in_=r.meta_sb[0:1, 2 + idx:3 + idx]).then_inc(r.s_f, 16)
    with sy.Else():
        sy.dma_start(out=r.flags_sh.ap()[2 * idx + 1:2 * idx + 2, :],
                     in_=r.meta_sb[0:1, 2 + idx:3 + idx]).then_inc(r.s_f, 16)


def emit_flag_poll(E, r, idx):
    sy = E.sync
    sy.wait_ge(r.s_f, E.c(r.s_f))
    u = f"_{idx}"
    with sy.register("tok" + u) as tok, sy.register("f0" + u) as f0, \
         sy.register("f1" + u) as f1, sy.register("cond" + u) as cond:
        sy.reg_load(tok, r.meta_sb[0:1, 2 + idx:3 + idx])
        sy.reg_mov(cond, 1)
        with sy.While(cond):
            sy.reg_load(f0, r.flags_sh.ap()[2 * idx:2 * idx + 1, :])
            sy.reg_load(f1, r.flags_sh.ap()[2 * idx + 1:2 * idx + 2, :])
            sy.reg_alu(f0, f0, tok, ALU.subtract)
            sy.reg_alu(f1, f1, tok, ALU.subtract)
            sy.reg_alu(cond, f0, f1, ALU.bitwise_or)
    sy.sem_inc(r.s_p, 1)
    E.bump(r.s_p, 1)


def emit_halved_dma(E, r, dst_my, dst_sib, src_my, src_sib, sem):
    sy = E.sync
    with sy.If_eq(r.rg_parity, 0):
        with E.nocount():
            sy.dma_start(out=dst_my, in_=src_my).then_inc(sem, 16)
    with sy.Else():
        sy.dma_start(out=dst_sib, in_=src_sib).then_inc(sem, 16)



def emit_rope_tok(E, r, stage, ctab, stab, nheads=16, drope=32):
    v, g = E.vector, E.gpsimd
    half = drope // 2
    for t in range(NT):
        pb = t % len(r.ropeC)
        need = r.rope_lastuse.get(pb, 0)
        if need > 0:
            g.wait_ge(r.s_v, need)
        sem = r.s_wp[pb]
        g.dma_start(out=r.ropeC[pb][:, :],
                    in_=ctab.rearrange("(t p) w -> p t w", p=128)[:, t, :]
                    ).then_inc(sem, 16)
        g.dma_start(out=r.ropeS[pb][:, :],
                    in_=stab.rearrange("(t p) w -> p t w", p=128)[:, t, :]
                    ).then_inc(sem, 16)
        wv = E.c(sem)
        qv = stage[:, t, :].rearrange("p (h d) -> p h d", d=128)
        qa = qv[:, :, 0:half]
        qb_ = qv[:, :, half:drope]
        qr = qv[:, :, 0:drope]
        tmp = r.ropetmp[:, :].rearrange("p (h d) -> p h d", d=drope)
        cp = r.ropeC[pb][:, :].rearrange("p (h d) -> p h d", d=drope)
        sp = r.ropeS[pb][:, :].rearrange("p (h d) -> p h d", d=drope)
        v.wait_ge(sem, wv)
        v.tensor_scalar(tmp[:, :, 0:half], qb_, -1.0, None, ALU.mult)
        v.tensor_scalar(tmp[:, :, half:drope], qa, 1.0, None, ALU.mult)
        v.tensor_tensor(qr, qr, cp, ALU.mult)
        v.tensor_tensor(tmp[:, :, :], tmp[:, :, :], sp, ALU.mult)
        v.tensor_tensor(qr, qr, tmp[:, :, :], ALU.add)
        r.rope_lastuse[pb] = E.c(r.s_v)


def emit_l0ab(E, r, w, ex):
    t_, v, sc, g, sy = E.tensor, E.vector, E.scalar, E.gpsimd, E.sync
    qw = w["l0ab_qwT"].rearrange("(c p) f -> p c f", p=128)
    kw = w["l0ab_kwT"].rearrange("(c p) f -> p c f", p=128)
    vw = w["l0ab_vwT"].rearrange("(c p) f -> p c f", p=128)
    ow = w["l0ab_owT"].rearrange("(c p) f -> p c f", p=128)
    SC = 1.0 / math.sqrt(128.0)
    emit_proj_tok(E, r, qw, D, r.qstage, bias_row=r.l0ab_qbrow[0:1, :],
                  scale_imm=SC)
    emit_rope_tok(E, r, r.qstage, w["l0ab_ropeC"], w["l0ab_ropeS"])
    sy.wait_ge(r.s_v, E.c(r.s_v))
    sy.dma_start(out=r.qscr.ap().rearrange("(t p) d -> p t d", p=128),
                 in_=r.qstage[:, :, :]).then_inc(r.s_x, 16)
    qscr_done = E.c(r.s_x)
    emit_proj_tok(E, r, kw, D, r.qstage, bias_row=r.l0ab_kbrow[0:1, :],
                  pre_evac_wait=[(r.s_x, qscr_done)])
    emit_rope_tok(E, r, r.qstage, w["l0ab_ropeC"], w["l0ab_ropeS"])
    sy.wait_ge(r.s_v, E.c(r.s_v))
    emit_halved_dma(E, r,
                    r.ksh.ap().rearrange("(t p) d -> p t d", p=128)[:, 0:NT, :],
                    r.ksh.ap().rearrange("(t p) d -> p t d", p=128)[:, NT:2 * NT, :],
                    r.qstage[:, :, :], r.qstage[:, :, :], r.s_kv)
    emit_proj_tok(E, r, vw, D, r.qstage, bias_row=r.l0ab_vbrow[0:1, :],
                  pre_evac_wait=[(r.s_kv, E.c(r.s_kv))])
    sy.wait_ge(r.s_v, E.c(r.s_v))
    for h in range(NH1B):
        vsh_h0 = r.vsh.ap()[h, 0, :, :].rearrange("(t p) d -> p t d", p=128)
        vsh_h1 = r.vsh.ap()[h, 1, :, :].rearrange("(t p) d -> p t d", p=128)
        emit_halved_dma(E, r, vsh_h0, vsh_h1,
                        r.qstage[:, :, h * 128:(h + 1) * 128],
                        r.qstage[:, :, h * 128:(h + 1) * 128], r.s_kv)
    emit_flag_write(E, r, ex, [(r.s_kv, E.c(r.s_kv))])
    emit_flag_poll(E, r, ex)
    sy.wait_ge(r.s_x, qscr_done)
    sy.wait_ge(r.s_t, E.c(r.s_t))
    for c in range(ND):
        sy.dma_start(out=r.qT[:, c, :], in_=r.qscr.ap()[:, c * 128:(c + 1) * 128],
                     transpose=True).then_inc(r.s_x, 16)
    qT_done = E.c(r.s_x)
    pol = E.c(r.s_p)
    for h in range(NH1B):
        hb = h % 2
        if h >= 2:
            sy.wait_ge(r.s_t, r.kv_lastuse.get(hb, 0))
        ksh_m = r.ksh.ap()[:, h * 128:(h + 1) * 128]
        with sy.If_eq(r.rg_parity, 0):
            with E.nocount():
                sy.dma_start(out=r.kt[hb][:, 0:1024], in_=ksh_m[0:1024, :],
                             transpose=True).then_inc(r.s_kv, 16)
                sy.dma_start(out=r.kt[hb][:, 1024:2048], in_=ksh_m[1024:2048, :],
                             transpose=True).then_inc(r.s_kv, 16)
                sy.dma_start(out=r.vt[hb][:, 0:8, :],
                             in_=r.vsh.ap()[h, 0, :, :].rearrange(
                                 "(c p) d -> p c d", p=128)).then_inc(r.s_kv, 16)
                sy.dma_start(out=r.vt[hb][:, 8:16, :],
                             in_=r.vsh.ap()[h, 1, :, :].rearrange(
                                 "(c p) d -> p c d", p=128)).then_inc(r.s_kv, 16)
        with sy.Else():
            sy.dma_start(out=r.kt[hb][:, 0:1024], in_=ksh_m[1024:2048, :],
                         transpose=True).then_inc(r.s_kv, 16)
            sy.dma_start(out=r.kt[hb][:, 1024:2048], in_=ksh_m[0:1024, :],
                         transpose=True).then_inc(r.s_kv, 16)
            sy.dma_start(out=r.vt[hb][:, 0:8, :],
                         in_=r.vsh.ap()[h, 1, :, :].rearrange(
                             "(c p) d -> p c d", p=128)).then_inc(r.s_kv, 16)
            sy.dma_start(out=r.vt[hb][:, 8:16, :],
                         in_=r.vsh.ap()[h, 0, :, :].rearrange(
                             "(c p) d -> p c d", p=128)).then_inc(r.s_kv, 16)
        kvh = E.c(r.s_kv)
        t_.wait_ge(r.s_kv, kvh)
        t_.wait_ge(r.s_v, E.c(r.s_v))
        if h == 0:
            t_.wait_ge(r.s_x, qT_done)
        e0 = E.c(r.s_s)
        pv_done = {}
        fix_done = {}
        for kc in range(ND):
            for half in range(2):
                i = kc * 2 + half
                pb = i % 2
                if i >= 2:
                    t_.wait_ge(r.s_s, e0 + i - 1)
                ins = t_.matmul(r.ps_a[pb][:, :],
                                r.kt[hb][:, kc * 128:(kc + 1) * 128],
                                r.qT[:, h, half * 512:(half + 1) * 512],
                                start=True, stop=True)
                ins.then_inc(r.s_t, 1)
                if kc >= 2:
                    sc.wait_ge(r.s_t, pv_done[kc - 2])
                    if kc - 2 in fix_done:
                        sc.wait_ge(r.s_v, fix_done[kc - 2])
                sc.wait_ge(r.s_t, E.c(r.s_t))
                sc.activation(r.at_sb[:, kc % 2, half * 512:(half + 1) * 512],
                              r.ps_a[pb][:, :], AF.Exp,
                              bias=r.l0ab_mb[:, kc:kc + 1])
            if kc < 8:
                v.wait_ge(r.s_s, E.c(r.s_s))
                if kc > 0:
                    v.memset(r.at_sb[:, kc % 2, 0:kc * 128], 0.0)
                v.tensor_tensor(r.at_sb[:, kc % 2, kc * 128:(kc + 1) * 128],
                                r.at_sb[:, kc % 2, kc * 128:(kc + 1) * 128],
                                r.triu[:, :], ALU.mult)
                fix_done[kc] = E.c(r.s_v)
            t_.wait_ge(r.s_v, E.c(r.s_v))
            t_.wait_ge(r.s_s, E.c(r.s_s))
            for half in range(2):
                t_.matmul(r.ps_d[half][0:1, :], r.ones_col[:, 0:1],
                          r.at_sb[:, kc % 2, half * 512:(half + 1) * 512],
                          start=(kc == 0), stop=(kc == ND - 1))
                ins = t_.matmul(r.ps_g[half][:, :],
                                r.vt[hb][:, kc, :],
                                r.at_sb[:, kc % 2, half * 512:(half + 1) * 512],
                                start=(kc == 0), stop=(kc == ND - 1))
                ins.then_inc(r.s_t, 1)
            pv_done[kc] = E.c(r.s_t)
        r.kv_lastuse[hb] = E.c(r.s_t)
        v.wait_ge(r.s_t, E.c(r.s_t))
        v.tensor_copy(r.nrm[0:1, 0:512], r.ps_d[0][0:1, :])
        sc.wait_ge(r.s_v, E.c(r.s_v))
        sc.activation(r.nrm[0:1, 0:512], r.nrm[0:1, 0:512], AF.Ln)
        v.wait_ge(r.s_s, E.c(r.s_s))
        v.tensor_scalar(r.nrm[0:1, 0:512], r.nrm[0:1, 0:512], -1.0, None, ALU.mult)
        sc.wait_ge(r.s_v, E.c(r.s_v))
        sc.activation(r.nrm[0:1, 0:512], r.nrm[0:1, 0:512], AF.Exp)
        v.wait_ge(r.s_s, E.c(r.s_s))
        v.tensor_copy(r.nrm_bf[0:1, 0:512], r.nrm[0:1, 0:512])
        v.tensor_copy(r.nrm[0:1, 0:512], r.ps_d[1][0:1, :])
        sc.wait_ge(r.s_v, E.c(r.s_v))
        sc.activation(r.nrm[0:1, 0:512], r.nrm[0:1, 0:512], AF.Ln)
        v.wait_ge(r.s_s, E.c(r.s_s))
        v.tensor_scalar(r.nrm[0:1, 0:512], r.nrm[0:1, 0:512], -1.0, None, ALU.mult)
        sc.wait_ge(r.s_v, E.c(r.s_v))
        sc.activation(r.nrm[0:1, 0:512], r.nrm[0:1, 0:512], AF.Exp)
        v.wait_ge(r.s_s, E.c(r.s_s))
        v.tensor_copy(r.nrm_bf2[0:1, 0:512], r.nrm[0:1, 0:512])
        t_.wait_ge(r.s_v, E.c(r.s_v))
        t_.matmul(r.ps_a[0][:, :], r.ones_row[0:1, 0:128],
                  r.nrm_bf[0:1, 0:512], start=True, stop=True)
        t_.matmul(r.ps_a[1][:, :], r.ones_row[0:1, 0:128],
                  r.nrm_bf2[0:1, 0:512], start=True, stop=True).then_inc(r.s_t, 1)
        v.wait_ge(r.s_t, E.c(r.s_t))
        for half in range(2):
            v.tensor_copy(r.bcbuf[:, half * 512:(half + 1) * 512],
                          r.ps_a[half][:, :])
        for half in range(2):
            v.tensor_tensor(r.attT[:, h, half * 512:(half + 1) * 512],
                            r.ps_g[half][:, :],
                            r.bcbuf[:, half * 512:(half + 1) * 512], ALU.mult)
    emit_oproj_add(E, r, ow, r.l0ab_ocol, r.attT)



def _shard_x(x):
    out = []
    for c in range(NCORES):
        b, h = c // 2, c % 2
        out.append(np.ascontiguousarray(x[b, h * TOK:(h + 1) * TOK, :], dtype=np.float32))
    return out


def _ident128():
    return np.eye(128, dtype=np.float32)


def _sel8():
    m = np.zeros((8, 8 * 128), dtype=np.float32)
    for e in range(8):
        m[e, e * 128:(e + 1) * 128] = 1.0
    return m


def l0ab_host_weights(inputs, core):
    half = core % 2
    Wqkv = inputs["L0_attn_b_W_qkv"]
    bqkv = inputs["L0_attn_b_b_qkv"]
    hh = np.arange(16)
    qi = (hh[:, None] * 384 + np.arange(128)[None, :]).ravel()
    ki = qi + 128
    vi = qi + 256
    t = (np.arange(TOK) + half * TOK).astype(np.float32)
    theta = 1.0 / (10000.0 ** (np.arange(0, 32, 2, dtype=np.float32) / 32.0))
    idx = t[:, None] * theta[None, :]
    idx2 = np.concatenate([idx, idx], axis=1)
    cosw = np.tile(np.cos(idx2), (1, 16)).astype(np.float32)
    sinw = np.tile(np.sin(idx2), (1, 16)).astype(np.float32)
    mb = np.zeros((128, 16), np.float32)
    if half == 0:
        mb[:, 8:] = -30000.0
    triu = (np.arange(128)[:, None] <= np.arange(128)[None, :]).astype(np.float32)
    ocol = inputs["L0_attn_b_b_out"].reshape(16, 128).T
    return {
        "l0ab_qwT": np.ascontiguousarray(Wqkv[qi, :].T),
        "l0ab_kwT": np.ascontiguousarray(Wqkv[ki, :].T),
        "l0ab_vwT": np.ascontiguousarray(Wqkv[vi, :].T),
        "l0ab_owT": np.ascontiguousarray(inputs["L0_attn_b_W_out"].T),
        "l0ab_qb": np.ascontiguousarray(bqkv[qi][None, :]),
        "l0ab_kb": np.ascontiguousarray(bqkv[ki][None, :]),
        "l0ab_vb": np.ascontiguousarray(bqkv[vi][None, :]),
        "l0ab_ocol_in": np.ascontiguousarray(ocol),
        "l0ab_ropeC": cosw, "l0ab_ropeS": sinw,
        "l0ab_mb_in": mb, "triu_in": triu,
    }


def l0aa_host_weights(inputs, core):
    half = core % 2
    t = (np.arange(TOK) + half * TOK).astype(np.float64)
    invf = 1.0 / (10000.0 ** (np.arange(0, 128, 2).astype(np.float64) / 128.0))
    ang = t[:, None] * invf[None, :]
    cosI = np.repeat(np.cos(ang), 2, axis=1)
    sinI = np.repeat(np.sin(ang), 2, axis=1)
    cosW = np.tile(cosI, (1, 16)).astype(np.float32)
    sinW = np.tile(sinI, (1, 16)).astype(np.float32)
    mb = np.zeros((128, 16), np.float32)
    if half == 0:
        mb[:, 8:] = -30000.0
    triu = (np.arange(128)[:, None] <= np.arange(128)[None, :]).astype(np.float32)
    return {
        "l0aa_qwT": np.ascontiguousarray(inputs["L0_attn_a_Wq"].T),
        "l0aa_kwT": np.ascontiguousarray(inputs["L0_attn_a_Wk"].T),
        "l0aa_vwT": np.ascontiguousarray(inputs["L0_attn_a_Wv"].T),
        "l0aa_owT": np.ascontiguousarray(inputs["L0_attn_a_Wo"].T),
        "l0aa_qAt": np.ascontiguousarray(inputs["L0_attn_a_qA"].T),
        "l0aa_kAt": np.ascontiguousarray(inputs["L0_attn_a_kA"].T),
        "l0aa_vAt": np.ascontiguousarray(inputs["L0_attn_a_vA"].T),
        "l0aa_oAt": np.ascontiguousarray(inputs["L0_attn_a_oA"].T),
        "l0aa_qBt": np.ascontiguousarray(inputs["L0_attn_a_qB"].T),
        "l0aa_kBt": np.ascontiguousarray(inputs["L0_attn_a_kB"].T),
        "l0aa_vBt": np.ascontiguousarray(inputs["L0_attn_a_vB"].T),
        "l0aa_oBt": np.ascontiguousarray(inputs["L0_attn_a_oB"].T),
        "l0aa_ropeCw": cosW, "l0aa_ropeSw": sinW,
        "l0ab_mb_in": mb, "triu_in": triu,
    }


def l0ffn_host_weights(inputs, core):
    half = core % 2
    d1 = inputs["L0_ffn_d1_w"]
    d2 = inputs["L0_ffn_d2_w"]
    rows = np.concatenate([
        inputs["L0_ffn_in_gamma"], inputs["L0_ffn_in_beta"],
        inputs["L0_ffn_agg_b"], inputs["L0_ffn_pn_gamma"],
        inputs["L0_ffn_pn_beta"], inputs["L0_ffn_d2_b"],
        inputs["L0_ffn_bc_b"], inputs["L0_ffn_out_gamma"],
        inputs["L0_ffn_out_beta"]])[None, :]
    return {
        "l0ffn_aggw": np.ascontiguousarray(inputs["L0_ffn_agg_w"]),
        "l0ffn_bcw": np.ascontiguousarray(inputs["L0_ffn_bc_w"]),
        "l0ffn_gatew": np.ascontiguousarray(inputs["L0_ffn_gate_w"]),
        "l0ffn_d1h": np.ascontiguousarray(d1[:, half * 4096:(half + 1) * 4096]),
        "l0ffn_d2h": np.ascontiguousarray(d2[half * 4096:(half + 1) * 4096, :]),
        "l0ffn_b1h_in": np.ascontiguousarray(
            inputs["L0_ffn_d1_b"][half * 4096:(half + 1) * 4096][None, :]),
        "l0ffn_rows_in": np.ascontiguousarray(rows),
        "l0ffn_gb": np.ascontiguousarray(inputs["L0_ffn_gate_b"].reshape(16, 128).T),
    }


def moe_host_weights(inputs):
    return {
        "wsgT": np.ascontiguousarray(inputs["L1_ffn_Wsg"].T),
        "wsuT": np.ascontiguousarray(inputs["L1_ffn_Wsu"].T),
        "wsdT": np.ascontiguousarray(inputs["L1_ffn_Wsd"].T),
        "wrT": np.ascontiguousarray(inputs["L1_ffn_Wr"].T),
        "wgT": np.ascontiguousarray(inputs["L1_ffn_Wg"].transpose(0, 2, 1)),
        "wuT": np.ascontiguousarray(inputs["L1_ffn_Wu"].transpose(0, 2, 1)),
        "wdT": np.ascontiguousarray(inputs["L1_ffn_Wd"].transpose(0, 2, 1)),
    }



class Res:
    pass


MOE_W_NAMES = ["wsgT", "wsuT", "wsdT", "wrT", "wgT", "wuT", "wdT"]


L1AB_W = ["l1ab_qwT", "l1ab_kwT", "l1ab_vwT", "l1ab_ogwT", "l1ab_owT",
          "l1ab_bwT", "l1ab_cols_in", "l1ab_vbrow_in", "l1ab_bb", "decayT_in"]


def l1ab_host_weights(inputs):
    qw = inputs["L1_attn_b_q_w"]
    kw = inputs["L1_attn_b_k_w"]
    vw = inputs["L1_attn_b_v_w"]
    ogw = inputs["L1_attn_b_og_w"]
    ow = inputs["L1_attn_b_o_w"]
    bw = inputs["L1_attn_b_b_w"]
    j = np.arange(64)[:, None].astype(np.float32)
    i = np.arange(64)[None, :].astype(np.float32)
    decay = np.where(j <= i, 1.0, 0.0).astype(np.float32)
    return {
        "l1ab_qwT": np.ascontiguousarray(qw.T),
        "l1ab_kwT": np.ascontiguousarray(kw.T),
        "l1ab_vwT": np.ascontiguousarray(vw.T),
        "l1ab_ogwT": np.ascontiguousarray(ogw.T),
        "l1ab_owT": np.ascontiguousarray(ow.T),
        "l1ab_bwT": np.ascontiguousarray(bw.T),
        "l1ab_cols_in": np.ascontiguousarray(np.concatenate([
            inputs["L1_attn_b_q_b"].reshape(16, 128).T,
            inputs["L1_attn_b_k_b"].reshape(16, 128).T,
            inputs["L1_attn_b_o_b"].reshape(16, 128).T], axis=1)),
        "l1ab_vbrow_in": np.ascontiguousarray(inputs["L1_attn_b_v_b"][None, :]),
        "l1ab_bb": np.ascontiguousarray(inputs["L1_attn_b_b_b"][None, :]),
        "decayT_in": decay,
    }


def l1aa_host_weights(inputs, core):
    par = core % 2
    bet = np.ascontiguousarray(
        inputs["L1_attn_a_betas"].reshape(16, 128).T.astype(np.float32))
    sel = np.zeros((128, 2), np.float32)
    sel[:, 0] = 1.0 if par == 0 else 0.0
    sel[:, 1] = 1.0 - sel[0, 0]
    return {
        "l1aa_qw": np.ascontiguousarray(inputs["L1_attn_a_Wq"]),
        "l1aa_kw": np.ascontiguousarray(inputs["L1_attn_a_Wk"]),
        "l1aa_vw": np.ascontiguousarray(inputs["L1_attn_a_Wv"]),
        "l1aa_ow": np.ascontiguousarray(inputs["L1_attn_a_Wo"]),
        "l1aa_bet_in": bet,
        "l1aa_sel_in": sel,
        "triu_in": np.triu(np.ones((128, 128), np.float32)),
    }


def build_core(stages):
    import contextlib
    nc = bass.Bass("TRN2", target_bir_lowering=False, debug=False,
                   detect_race_conditions=False)
    r = Res()
    x_in = nc.dram_tensor("x_in", [TOK, D], F32, kind="ExternalInput").ap()
    ident = nc.dram_tensor("ident_in", [128, 128], F32, kind="ExternalInput").ap()
    x_out = nc.dram_tensor("x_out", [TOK, D], F32, kind="ExternalOutput").ap()
    w = {}
    if "moe" in stages:
        w["sel8"] = nc.dram_tensor("sel8", [8, 8 * 128], F32, kind="ExternalInput").ap()
        w["wsgT"] = nc.dram_tensor("wsgT", [D, FS], F32, kind="ExternalInput").ap()
        w["wsuT"] = nc.dram_tensor("wsuT", [D, FS], F32, kind="ExternalInput").ap()
        w["wsdT"] = nc.dram_tensor("wsdT", [FS, D], F32, kind="ExternalInput").ap()
        w["wrT"] = nc.dram_tensor("wrT", [D, 8], F32, kind="ExternalInput").ap()
        w["wgT"] = nc.dram_tensor("wgT", [NEXP, D, FE], F32, kind="ExternalInput").ap()
        w["wuT"] = nc.dram_tensor("wuT", [NEXP, D, FE], F32, kind="ExternalInput").ap()
        w["wdT"] = nc.dram_tensor("wdT", [NEXP, FE, D], F32, kind="ExternalInput").ap()
    if "l0aa" in stages:
        w["l0aa_qwT"] = nc.dram_tensor("l0aa_qwT", [D, D], F32, kind="ExternalInput").ap()
        w["l0aa_kwT"] = nc.dram_tensor("l0aa_kwT", [D, 1024], F32, kind="ExternalInput").ap()
        w["l0aa_vwT"] = nc.dram_tensor("l0aa_vwT", [D, 1024], F32, kind="ExternalInput").ap()
        w["l0aa_owT"] = nc.dram_tensor("l0aa_owT", [D, D], F32, kind="ExternalInput").ap()
        for nm, sh in [("l0aa_qAt", [D, 8]), ("l0aa_kAt", [D, 8]), ("l0aa_vAt", [D, 8]),
                       ("l0aa_oAt", [D, 8]), ("l0aa_qBt", [8, D]), ("l0aa_kBt", [8, 1024]),
                       ("l0aa_vBt", [8, 1024]), ("l0aa_oBt", [8, D]),
                       ("l0aa_ropeCw", [TOK, D]), ("l0aa_ropeSw", [TOK, D])]:
            w[nm] = nc.dram_tensor(nm, sh, F32, kind="ExternalInput").ap()
    if ("l0ab" in stages) or ("l0aa" in stages):
        w["l0ab_mb_in"] = nc.dram_tensor("l0ab_mb_in", [128, 16], F32, kind="ExternalInput").ap()
    if ("l0ab" in stages) or ("l0aa" in stages) or ("l1aa" in stages):
        w["triu_in"] = nc.dram_tensor("triu_in", [128, 128], F32, kind="ExternalInput").ap()
    if "l1aa" in stages:
        for nm in ["l1aa_qw", "l1aa_kw", "l1aa_vw", "l1aa_ow"]:
            w[nm] = nc.dram_tensor(nm, [D, D], F32, kind="ExternalInput").ap()
        w["l1aa_bet_in"] = nc.dram_tensor("l1aa_bet_in", [128, 16], F32, kind="ExternalInput").ap()
        w["l1aa_sel_in"] = nc.dram_tensor("l1aa_sel_in", [128, 2], F32, kind="ExternalInput").ap()
        r.state_sh = nc.dram_tensor("state_sh", [NHG, 2, 128, 516], F32, addr_space="Shared")
    if "l0ab" in stages:
        for nm in ["l0ab_qwT", "l0ab_kwT", "l0ab_vwT", "l0ab_owT"]:
            w[nm] = nc.dram_tensor(nm, [D, D], F32, kind="ExternalInput").ap()
        for nm in ["l0ab_qb", "l0ab_kb", "l0ab_vb"]:
            w[nm] = nc.dram_tensor(nm, [1, D], F32, kind="ExternalInput").ap()
        w["l0ab_ocol_in"] = nc.dram_tensor("l0ab_ocol_in", [128, 16], F32, kind="ExternalInput").ap()
        w["l0ab_ropeC"] = nc.dram_tensor("l0ab_ropeC", [TOK, 512], F32, kind="ExternalInput").ap()
        w["l0ab_ropeS"] = nc.dram_tensor("l0ab_ropeS", [TOK, 512], F32, kind="ExternalInput").ap()
    if "l1ab" in stages:
        for nm in ["l1ab_qwT", "l1ab_kwT", "l1ab_vwT", "l1ab_ogwT", "l1ab_owT"]:
            w[nm] = nc.dram_tensor(nm, [D, D], F32, kind="ExternalInput").ap()
        w["l1ab_bwT"] = nc.dram_tensor("l1ab_bwT", [D, 16], F32, kind="ExternalInput").ap()
        w["l1ab_cols_in"] = nc.dram_tensor("l1ab_cols_in", [128, 48], F32, kind="ExternalInput").ap()
        w["l1ab_vbrow_in"] = nc.dram_tensor("l1ab_vbrow_in", [1, D], F32, kind="ExternalInput").ap()
        w["l1ab_bb"] = nc.dram_tensor("l1ab_bb", [1, 16], F32, kind="ExternalInput").ap()
        w["decayT_in"] = nc.dram_tensor("decayT_in", [64, 64], F32, kind="ExternalInput").ap()

    need_exch = any(st_ in ("l0ab", "l0aa", "l0ffn", "l1aa") for st_ in stages)
    if need_exch:
        meta_in = nc.dram_tensor("meta_in", [1, 2 + NFLAGS], I32, kind="ExternalInput").ap()
        r.flags_sh = nc.dram_tensor("flags_sh", [2 * NFLAGS, 1], I32, addr_space="Shared")
    need_attn = ("l0ab" in stages) or ("l0aa" in stages)
    if need_attn:
        r.qscr = nc.dram_tensor("qscr", [TOK, D], BF16)
    if "l0ab" in stages:
        r.ksh = nc.dram_tensor("ksh_l0ab", [S, D], BF16, addr_space="Shared")
        r.vsh = nc.dram_tensor("vsh_l0ab", [16, 2, TOK, 128], BF16, addr_space="Shared")
    if "l0aa" in stages:
        r.ksh_aa = nc.dram_tensor("ksh_l0aa", [S, 1024], BF16, addr_space="Shared")
        r.vsh_aa = nc.dram_tensor("vsh_l0aa", [8, 2, TOK, 128], BF16, addr_space="Shared")
    if "l0ffn" in stages:
        r.d1c = nc.dram_tensor("d1c", [D, 4096], BF16)
        r.d2c = nc.dram_tensor("d2c", [4096, D], BF16)
        r.rowscr = nc.dram_tensor("rowscr", [1, D], BF16)
        r.lastrow_sh = nc.dram_tensor("lastrow_sh", [2, D], BF16, addr_space="Shared")
        r.delta_sh = nc.dram_tensor("delta_sh", [8, D], F32, addr_space="Shared")
        w["l0ffn_aggw"] = nc.dram_tensor("l0ffn_aggw", [D, D], F32, kind="ExternalInput").ap()
        w["l0ffn_bcw"] = nc.dram_tensor("l0ffn_bcw", [D, D], F32, kind="ExternalInput").ap()
        w["l0ffn_gatew"] = nc.dram_tensor("l0ffn_gatew", [D, D], F32, kind="ExternalInput").ap()
        w["l0ffn_d1h"] = nc.dram_tensor("l0ffn_d1h", [D, 4096], F32, kind="ExternalInput").ap()
        w["l0ffn_d2h"] = nc.dram_tensor("l0ffn_d2h", [4096, D], F32, kind="ExternalInput").ap()
        w["l0ffn_b1h_in"] = nc.dram_tensor("l0ffn_b1h_in", [1, 4096], F32, kind="ExternalInput").ap()
        w["l0ffn_rows_in"] = nc.dram_tensor("l0ffn_rows_in", [1, 9 * D], F32, kind="ExternalInput").ap()
        w["l0ffn_gb"] = nc.dram_tensor("l0ffn_gb", [128, 16], F32, kind="ExternalInput").ap()
    es = contextlib.ExitStack()
    with es:
        def sb(name, shape, dt):
            h = es.enter_context(nc.sbuf_tensor(name, shape, dt))
            setattr(r, name, h)
            return h

        def ps(name, shape, dt=F32):
            h = es.enter_context(nc.psum_tensor(name, shape, dt))
            return h

        sb("x_sb", [128, NT, D], F32)
        sb("xnT", [128, ND, TOK], BF16)
        sb("xstage", [128, D], BF16)
        sb("scr2_bf", [128, D], BF16)
        sb("stat", [128, 4 * NT], F32)
        sb("ident_sb", [128, 128], BF16)
        sb("ones_row", [1, 512], BF16)
        sb("ones_col", [128, 1], BF16)
        sb("mtmp", [128, 32], F32)
        sb("big1", [128, 16384], BF16)
        if need_exch:
            sb("meta_sb", [1, 2 + NFLAGS], I32)
        if "triu_in" in w:
            sb("triu", [128, 128], BF16)
        if need_attn:
            sb("l0ab_mb", [128, 16], F32)
        if "l1aa" in stages:
            sb("csc", [128, 1], F32)
            nc.const_aps.register(F32, SC128, r.csc.ap()[:, 0:1])
        r.ident = r.ident_sb
        r.scr_bf = r.xstage
        r.ystage = r.xstage
        r.qT = r.xnT
        r.qstage = r.big1.ap().rearrange("p (t d) -> p t d", d=D)
        r.kstage = r.qstage
        r.attT = r.big1.ap().rearrange("p (c t) -> p c t", t=TOK)
        r.ps_tr = [ps("ps_tr0", [128, 128], BF16), ps("ps_tr1", [128, 128], BF16)]
        r.ps_a = [ps("ps_a0", [128, 512], F32), ps("ps_a1", [128, 512], F32)]
        r.ps_g = [ps("ps_g0", [128, 512], F32), ps("ps_g1", [128, 512], F32)]
        r.ps_d = [ps("ps_d0", [8, 512], F32), ps("ps_d1", [8, 512], F32)]
        r.ps_s = r.ps_a
        r.ps_o = r.ps_g
        r.ps_w = r.ps_tr[0]

        sb("ARN", [128, ARN_BYTES // 2], BF16)
        arn_bf = r.ARN.ap()
        arn_f32 = arn_bf.bitcast(F32)

        def AV(off, shape, dt=BF16):
            base = arn_f32 if dt == F32 else arn_bf
            esz = 4 if dt == F32 else 2
            assert off % esz == 0
            n = 1
            for d_ in shape[1:]:
                n *= d_
            v = base[0:shape[0], off // esz:off // esz + n]
            if len(shape) == 3:
                v = v.rearrange("p (a b) -> p a b", b=shape[2])
            return v

        def _views(st):
            d = {}
            if st == "moe":
                d["gbuf"] = AV(0, [128, NFE, TOK])
                d["wd"] = [AV(22528, [128, ND, 512])]
                d["wtok"] = d["wd"]
                d["wk"] = [AV(38912, [128, ND, 128]), AV(43008, [128, ND, 128])]
                d["wrT_sb"] = AV(47104, [128, ND, 8])
                d["wT8"] = AV(47360, [8, TOK])
                d["sel8_sb"] = AV(49408, [8, 8 * 128])
                d["gatebc"] = AV(51456, [128, TOK])
                d["wsm"] = AV(53504, [128, 8], F32)
                d["mstat"] = AV(53568, [128, 8], F32)
                d["wfb"] = AV(53632, [128, NT, 8])
                d["ubuf"] = r.big1.ap()[:, 0:NFE * TOK].rearrange(
                    "p (a b) -> p a b", b=TOK)
            elif st == "l1ab":
                d["gbuf"] = AV(0, [128, 8, TOK])
                d["vtok"] = AV(16384, [64, 2 * NT, 512])
                d["wd"] = [AV(32768, [128, ND, 512])]
                d["wtok"] = d["wd"]
                d["wk"] = [AV(49152, [128, ND, 128]), AV(53248, [128, ND, 128])]
                d["betab"] = AV(57344, [64, 2 * NT, 16], F32)
                d["bwT_sb"] = AV(58368, [128, ND, 16])
                d["bb16"] = AV(58880, [1, 16])
                d["l1ab_cols"] = AV(58944, [128, 48], F32)
                d["l1ab_vbrow"] = AV(59200, [1, D])
                d["decayT"] = AV(63296, [64, 64])
                d["attn64"] = AV(63424, [64, 64])
                d["nrm"] = AV(63552, [1, 512], F32)
                d["nrm_bf"] = AV(65600, [1, 512])
            elif st == "l1aa":
                d["gbuf"] = AV(0, [128, 8, TOK])
                d["amem"] = AV(0, [128, 16, 512])
                d["sgbuf"] = AV(16384, [128, 8, TOK])
                d["wtok"] = [AV(16384, [128, ND, 512])]
                d["vtk"] = AV(32768, [128, NT, 512])
                d["sktok"] = AV(40960, [128, NT, 512])
                d["wk"] = [AV(49152, [128, ND, 128]), AV(53248, [128, ND, 128])]
                d["mem_f"] = AV(57344, [128, 512], F32)
                d["mem_b"] = AV(59392, [128, 512])
                d["z_f"] = AV(60416, [128, 4], F32)
                d["z_b"] = AV(60480, [128, 4])
                d["delta"] = AV(60544, [128, 4, 128])
                d["betcol"] = AV(61568, [128, 16], F32)
                d["ombet"] = AV(61632, [128, 16], F32)
                d["l1aa_bet_sb"] = AV(61696, [128, 16], F32)
                d["selcol"] = AV(61760, [128, 2], F32)
                d["nrm"] = AV(61824, [1, 512], F32)
                d["nrm_bf"] = AV(63872, [1, 512])
                d["aat"] = r.scr2_bf.ap()[:, 0:1024].rearrange(
                    "p (c t) -> p c t", t=512)
            elif st in ("l0ab", "l0aa"):
                d["kt"] = [AV(0, [128, S]), AV(4096, [128, S])]
                d["vt"] = [AV(8192, [128, ND, 128]), AV(12288, [128, ND, 128])]
                d["at_sb"] = AV(16384, [128, 2, TOK])
                d["bcbuf"] = AV(20480, [128, TOK])
                d["nrm_bf2"] = AV(22528, [1, 512])
                d["wtok"] = [AV(23552, [128, ND, 512])]
                d["wk"] = [AV(39936, [128, ND, 128]), AV(44032, [128, ND, 128])]
                if st == "l0ab":
                    d["l0ab_qbrow"] = AV(48128, [1, D])
                    d["l0ab_kbrow"] = AV(52224, [1, D])
                    d["l0ab_vbrow"] = AV(56320, [1, D])
                    d["l0ab_ocol"] = AV(60416, [128, 16], F32)
                    d["ropeC"] = [AV(60544, [128, 512]), AV(61568, [128, 512])]
                    d["ropeS"] = [AV(62592, [128, 512]), AV(63616, [128, 512])]
                    d["ropetmp"] = AV(64640, [128, 512])
                    d["nrm"] = AV(65664, [1, 512], F32)
                    d["nrm_bf"] = AV(67712, [1, 512])
                else:
                    d["ropeCw"] = [AV(48128, [128, D])]
                    d["ropeSw"] = [AV(52224, [128, D])]
                    d["lA_sb"] = AV(56320, [128, ND, 8])
                    l1all = AV(56576, [8, 3 * TOK])
                    d["l1all"] = l1all
                    d["l1q"] = l1all[:, 0:TOK]
                    d["l1k"] = l1all[:, TOK:2 * TOK]
                    d["l1v"] = l1all[:, 2 * TOK:3 * TOK]
                    d["lB_sb"] = AV(62720, [8, D])
                    d["ropetmpw"] = r.scr2_bf
                    d["nrm"] = AV(66816, [1, 512], F32)
                    d["nrm_bf"] = AV(68864, [1, 512])
            elif st == "l0ffn":
                d["wk"] = [AV(0, [128, ND, 128]), AV(4096, [128, ND, 128])]
                d["wrow"] = AV(8192, [128, 8, 512])
                d["bcb"] = AV(16384, [128, D])
                d["rowc"] = [AV(20480, [1, D])]
                rbf = AV(24576, [1, D])
                d["row_bf"] = rbf
                d["row_bfin"] = rbf
                d["row_a"] = AV(28672, [1, D], F32)
                d["row_b"] = AV(36864, [1, D], F32)
                d["thought"] = AV(45056, [1, D], F32)
                d["t1h_bf"] = AV(53248, [1, 4096])
                d["b1h"] = AV(61440, [1, 4096])
                d["tcolT"] = AV(69632, [128, ND])
                d["t1colT"] = AV(69696, [128, 32])
                d["l0ffn_gcol"] = AV(69760, [128, 16], F32)
            return d

        stage_views = {st_: _views(st_) for st_ in set(stages)}

        sems = {}
        for sname in ["s_x", "s_v", "s_s", "s_t", "s_w", "s_g", "s_f", "s_p", "s_kv"]:
            sems[sname] = es.enter_context(nc.semaphore(sname))
            setattr(r, sname, sems[sname])
        r.s_wp = [es.enter_context(nc.semaphore(f"s_wp{i}")) for i in range(NWK)]
        r.s_wdp = [es.enter_context(nc.semaphore("s_wdp0"))]
        r.s_wtp = r.s_wdp
        block = es.enter_context(nc.Block())

        def emit_stage_barrier(E):
            allsems = [r.s_v, r.s_s, r.s_t, r.s_g, r.s_x, r.s_w, r.s_f,
                       r.s_kv, r.s_p] + r.s_wp + r.s_wdp
            for eng in (E.tensor, E.vector, E.scalar, E.gpsimd, E.sync):
                for sem_ in allsems:
                    c_ = E.c(sem_)
                    if c_ > 0:
                        eng.wait_ge(sem_, c_)

        def emit_stage_consts(E, st):
            g = E.gpsimd
            if st == "moe":
                g.dma_start(out=r.sel8_sb[:, :], in_=w["sel8"]).then_inc(r.s_w, 16)
            elif st == "l0ab":
                g.dma_start(out=r.l0ab_qbrow[0:1, :], in_=w["l0ab_qb"]).then_inc(r.s_w, 16)
                g.dma_start(out=r.l0ab_kbrow[0:1, :], in_=w["l0ab_kb"]).then_inc(r.s_w, 16)
                g.dma_start(out=r.l0ab_vbrow[0:1, :], in_=w["l0ab_vb"]).then_inc(r.s_w, 16)
                g.dma_start(out=r.l0ab_ocol[:, :], in_=w["l0ab_ocol_in"]).then_inc(r.s_w, 16)
            elif st == "l0ffn":
                g.dma_start(out=r.b1h[0:1, :], in_=w["l0ffn_b1h_in"]).then_inc(r.s_w, 16)
                g.dma_start(out=r.l0ffn_gcol[:, :], in_=w["l0ffn_gb"]).then_inc(r.s_w, 16)
            elif st == "l1ab":
                g.dma_start(out=r.bwT_sb[:, :, :],
                            in_=w["l1ab_bwT"].rearrange("(c p) e -> p c e", p=128)
                            ).then_inc(r.s_w, 16)
                g.dma_start(out=r.bb16[0:1, :], in_=w["l1ab_bb"]).then_inc(r.s_w, 16)
                g.dma_start(out=r.l1ab_cols[:, :], in_=w["l1ab_cols_in"]).then_inc(r.s_w, 16)
                g.dma_start(out=r.l1ab_vbrow[0:1, :], in_=w["l1ab_vbrow_in"]).then_inc(r.s_w, 16)
                g.dma_start(out=r.decayT[:, :], in_=w["decayT_in"]).then_inc(r.s_w, 16)
            elif st == "l1aa":
                g.dma_start(out=r.l1aa_bet_sb[:, :], in_=w["l1aa_bet_in"]).then_inc(r.s_w, 16)
                g.dma_start(out=r.selcol[:, :], in_=w["l1aa_sel_in"]).then_inc(r.s_w, 16)

        def prog(E):
            import os as _os
            r.DBG_NOGATE = bool(_os.environ.get("KDBG_NOGATE"))
            E.selfsem = {"vector": r.s_v, "scalar": r.s_s, "gpsimd": r.s_g}
            r.wk_lastuse = {}
            r.wd_lastuse = {}
            r.wrow_lastuse = 0
            r.d1c_ready = False
            r.wtok_lastuse = r.wd_lastuse
            r.rope_lastuse = {}
            r.kv_lastuse = {}
            E.vector.memset(r.ones_row[0:1, :], 1.0)
            E.vector.memset(r.ones_col[:, 0:1], 1.0)
            if "l1aa" in stages:
                E.vector.memset(r.csc[:, 0:1], SC128)
            if need_exch:
                emit_exch_init(E, r, meta_in)
            E.gpsimd.dma_start(out=r.ident[:, :], in_=ident).then_inc(r.s_w, 16)
            if "triu_in" in w:
                E.gpsimd.dma_start(out=r.triu[:, :], in_=w["triu_in"]).then_inc(r.s_w, 16)
            if need_attn:
                E.gpsimd.dma_start(out=r.l0ab_mb[:, :], in_=w["l0ab_mb_in"]).then_inc(r.s_w, 16)
            E.gpsimd.wait_ge(r.s_w, E.c(r.s_w))
            r.exch_idx = 0
            r.w2sb_done = {}
            emit_load_x(E, r, x_in)
            for st_ in stages:
                emit_stage_barrier(E)
                for k_, v_ in stage_views[st_].items():
                    setattr(r, k_, v_)
                emit_stage_consts(E, st_)
                if st_ == "l0ffn":
                    emit_ln_T(E, r, out_tok=r.qstage)
                    emit_l0ffn(E, r, w, r.exch_idx, r.qstage)
                    r.exch_idx += 5
                    continue
                emit_ln_T(E, r)
                if st_ == "moe":
                    emit_moe(E, r, w)
                elif st_ == "l1ab":
                    emit_l1ab(E, r, w)
                elif st_ == "l0ab":
                    emit_l0ab(E, r, w, r.exch_idx)
                    r.exch_idx += 1
                elif st_ == "l0aa":
                    emit_l0aa(E, r, w, r.exch_idx)
                    r.exch_idx += 1
                elif st_ == "l1aa":
                    emit_l1aa(E, r, w, r.exch_idx)
                    r.exch_idx += NHG
            emit_store_x(E, r, x_out)
            E.sync.wait_ge(r.s_x, E.c(r.s_x))

        Emit(nc, block).build(prog)

    return nc


_prog_cache = {}


def _get_prog(stages):
    key = tuple(stages)
    if key not in _prog_cache:
        _prog_cache[key] = build_core(list(stages))
    return _prog_cache[key]


STAGE_ORDER = ["l0aa", "l0ffn", "l0ab", "l0ffn", "l1aa", "moe", "l1ab", "moe"]
EXCH_STAGES = ("l0aa", "l0ffn", "l0ab", "l1aa")


def _stage_weights(inputs, st, c, cache):
    if st in ("moe", "l1ab"):
        key = (st, 0)
    elif st in ("l0ffn", "l1aa"):
        key = (st, c % 2)
    else:
        key = (st, c)
    if key not in cache:
        if st == "moe":
            d = dict(moe_host_weights(inputs))
            d["sel8"] = _sel8()
        elif st == "l1ab":
            d = l1ab_host_weights(inputs)
        elif st == "l0ab":
            d = l0ab_host_weights(inputs, c)
        elif st == "l0aa":
            d = l0aa_host_weights(inputs, c)
        elif st == "l0ffn":
            d = l0ffn_host_weights(inputs, c)
        elif st == "l1aa":
            d = l1aa_host_weights(inputs, c)
        cache[key] = d
    return cache[key]


def kernel(**inputs):
    x = np.asarray(inputs["L0_attn_a_x"], np.float32)
    shards = _shard_x(x)
    ident = _ident128()
    wcache = {}
    rng = np.random.default_rng()
    nc = _get_prog(STAGE_ORDER)
    toks = rng.integers(1, 2 ** 30, size=NFLAGS).astype(np.int32)
    ins = []
    for c in range(NCORES):
        m = {"x_in": shards[c], "ident_in": ident}
        for st in ("l0aa", "l0ffn", "l0ab", "l1aa", "moe", "l1ab"):
            m.update(_stage_weights(inputs, st, c, wcache))
        meta = np.zeros((1, 2 + NFLAGS), np.int32)
        meta[0, 0] = c % 2
        meta[0, 1] = c
        meta[0, 2:] = toks
        m["meta_in"] = meta
        ins.append(m)
    res = run_bass_kernel_spmd(nc, ins, core_ids=list(range(NCORES)))
    out = np.empty((B, S, D), np.float32)
    for c in range(NCORES):
        b, h = c // 2, c % 2
        out[b, h * TOK:(h + 1) * TOK, :] = \
            np.ascontiguousarray(res.results[c]["x_out"], dtype=np.float32)
    return out


# version 06 (final)
# speedup vs baseline: 1.3332x; 1.3332x over previous
import math
import numpy as np
import concourse.bass as bass
import concourse.mybir as mybir
from concourse.bass_utils import run_bass_kernel_spmd

F32 = mybir.dt.float32
BF16 = mybir.dt.bfloat16
I32 = mybir.dt.int32
AF = mybir.ActivationFunctionType
ALU = mybir.AluOpType
AX = mybir.AxisListType

B, S, D = 4, 2048, 2048
TOK = 1024
NT = TOK // 128
ND = D // 128
NCORES = 8


_CM_METHODS = {"register", "If_eq", "If_ne", "If_lt", "Else", "While", "Fori"}


class _DummyVal:
    def __call__(self, *a, **k):
        return self


class _DummyCM:
    def __enter__(self):
        return _DummyVal()

    def __exit__(self, *a):
        return False


class _Ins:
    __slots__ = ("E", "real")

    def __init__(self, E, real):
        self.E = E
        self.real = real

    def then_inc(self, sem, n=1):
        if self.E.counting:
            self.E.counts[id(sem)] = self.E.counts.get(id(sem), 0) + n
        if self.real is not None:
            self.real.then_inc(sem, n)
        return self


_AUTO_OPS = {
    "tensor_scalar", "tensor_tensor", "tensor_copy", "tensor_reduce",
    "reduce_max", "reduce_sum", "reciprocal", "scalar_tensor_tensor",
    "memset", "memzero", "select", "tensor_add", "tensor_mul", "tensor_sub",
    "tensor_max", "tensor_relu", "bn_stats", "bn_aggr", "activation", "copy",
    "mul", "add", "square", "sqrt", "sign", "iota", "affine_select",
    "tensor_scalar_mul", "tensor_scalar_add", "tensor_single_scalar",
    "tensor_tensor_scan", "stream_shuffle", "transpose",
}
_AUTO_ENGS = {"vector", "scalar", "gpsimd"}


class _Eng:
    def __init__(self, E, name):
        self._E = E
        self._name = name

    @property
    def _active(self):
        return self._E.cur == self._name

    def __getattr__(self, attr):
        E = self._E
        name = self._name
        active = self._active
        if attr in _CM_METHODS:
            if active:
                return getattr(E.eng, attr)
            return lambda *a, **k: _DummyCM()
        if attr in ("alloc_register", "value_load", "scalar_reg_alu"):
            if active:
                return getattr(E.eng, attr)
            return lambda *a, **k: _DummyVal()
        auto = name in _AUTO_ENGS and attr in _AUTO_OPS
        if active:
            real = getattr(E.eng, attr)

            def call(*a, **k):
                r = real(*a, **k)
                if auto:
                    sem = E.selfsem[name]
                    c = E.c(sem)
                    if c > 0:
                        r._wait_ge(sem, c)
                    ins = _Ins(E, r)
                    ins.then_inc(sem, 1)
                    return ins
                return _Ins(E, r)

            return call

        def dummy(*a, **k):
            ins = _Ins(E, None)
            if auto:
                ins.then_inc(E.selfsem[name], 1)
            return ins

        return dummy


class Emit:
    def __init__(self, nc, block):
        self.nc = nc
        self.block = block
        self.cur = None
        self.eng = None
        self.counting = True
        self.counts = {}
        self.selfsem = {}
        self.sync = _Eng(self, "sync")
        self.tensor = _Eng(self, "tensor")
        self.vector = _Eng(self, "vector")
        self.scalar = _Eng(self, "scalar")
        self.gpsimd = _Eng(self, "gpsimd")

    def c(self, sem):
        return self.counts.get(id(sem), 0)

    def bump(self, sem, n):
        if self.counting:
            self.counts[id(sem)] = self.counts.get(id(sem), 0) + n

    class _NoCount:
        def __init__(self, E):
            self.E = E

        def __enter__(self):
            self.E.counting = False

        def __exit__(self, *a):
            self.E.counting = True
            return False

    def nocount(self):
        return Emit._NoCount(self)

    def build(self, fn):
        order = ["sync", "tensor", "vector", "scalar", "gpsimd"]
        for name in order:
            deco = getattr(self.block, name)

            def body(eng, name=name):
                self.cur = name
                self.eng = eng
                self.counts = {}
                fn(self)

            deco(body)




def emit_load_x(E, r, x_in):
    E.sync.dma_start(out=r.x_sb[:, :, :],
                     in_=x_in.rearrange("(t p) d -> p t d", p=128)).then_inc(r.s_x, 16)


def emit_store_x(E, r, x_out):
    E.sync.wait_ge(r.s_v, E.c(r.s_v))
    E.sync.wait_ge(r.s_s, E.c(r.s_s))
    E.sync.dma_start(out=x_out.rearrange("(t p) d -> p t d", p=128),
                     in_=r.x_sb[:, :, :]).then_inc(r.s_x, 16)


def emit_ln_T(E, r, eps=1e-5, out_tok=None):
    v, sc, t_ = E.vector, E.scalar, E.tensor
    xw = E.c(r.s_x)
    ww0 = E.c(r.s_w)
    s0 = E.c(r.s_s)
    for t in range(NT):
        if t == 0:
            v.wait_ge(r.s_x, xw)
            sc.wait_ge(r.s_x, xw)
        v.tensor_scalar(r.scr_bf[:, :], r.x_sb[:, t, :], 1.0, 0.0, ALU.mult,
                        ALU.add,
                        accum_out=r.stat[:, 4 * t + 0: 4 * t + 1])
        sc.activation(r.scr2_bf[:, :], r.x_sb[:, t, :], AF.Square,
                      accum_out=r.stat[:, 4 * t + 1: 4 * t + 2])
        emit_prep_pump(E, r, 1)
    for t in range(NT):
        v.tensor_scalar(r.stat[:, 4 * t + 0: 4 * t + 1],
                        r.stat[:, 4 * t + 0: 4 * t + 1], -1.0 / D, None,
                        ALU.mult)
        v.wait_ge(r.s_s, s0 + t + 1)
        v.tensor_scalar(r.stat[:, 4 * t + 1: 4 * t + 2],
                        r.stat[:, 4 * t + 1: 4 * t + 2], 1.0 / D, None,
                        ALU.mult)
        v.tensor_tensor(r.stat[:, 4 * t + 2: 4 * t + 3],
                        r.stat[:, 4 * t + 0: 4 * t + 1],
                        r.stat[:, 4 * t + 0: 4 * t + 1], ALU.mult)
        v.tensor_tensor(r.stat[:, 4 * t + 1: 4 * t + 2],
                        r.stat[:, 4 * t + 1: 4 * t + 2],
                        r.stat[:, 4 * t + 2: 4 * t + 3], ALU.subtract)
        v.tensor_scalar(r.stat[:, 4 * t + 1: 4 * t + 2],
                        r.stat[:, 4 * t + 1: 4 * t + 2], eps, None,
                        ALU.add)
        sc.wait_ge(r.s_v, E.c(r.s_v))
        sc.activation(r.stat[:, 4 * t + 3: 4 * t + 4],
                      r.stat[:, 4 * t + 1: 4 * t + 2], AF.Ln)
        v.wait_ge(r.s_s, E.c(r.s_s))
        v.tensor_scalar(r.stat[:, 4 * t + 3: 4 * t + 4],
                        r.stat[:, 4 * t + 3: 4 * t + 4], -0.5, None,
                        ALU.mult)
        sc.wait_ge(r.s_v, E.c(r.s_v))
        sc.activation(r.stat[:, 4 * t + 3: 4 * t + 4],
                      r.stat[:, 4 * t + 3: 4 * t + 4], AF.Exp)
        v.wait_ge(r.s_s, E.c(r.s_s))
        if t > 0:
            v.wait_ge(r.s_t, E.c(r.s_t))
        v.tensor_scalar(r.xstage[:, :], r.x_sb[:, t, :],
                        r.stat[:, 4 * t + 0: 4 * t + 1],
                        r.stat[:, 4 * t + 3: 4 * t + 4],
                        ALU.add, ALU.mult)
        if out_tok is not None:
            v.tensor_scalar(out_tok[:, t, :], r.x_sb[:, t, :],
                            r.stat[:, 4 * t + 0: 4 * t + 1],
                            r.stat[:, 4 * t + 3: 4 * t + 4],
                            ALU.add, ALU.mult)
        t_.wait_ge(r.s_v, E.c(r.s_v))
        if t == 0:
            t_.wait_ge(r.s_w, ww0)
        for dc in range(ND):
            i = dc % 2
            if dc >= 2:
                t_.wait_ge(r.s_v, E.c(r.s_v) - 1)
            t_.transpose(r.ps_tr[i][:, :], r.xstage[:, dc * 128:(dc + 1) * 128],
                         r.ident[:, :]).then_inc(r.s_t, 1)
            v.wait_ge(r.s_t, E.c(r.s_t))
            v.tensor_copy(r.xnT[:, dc, t * 128:(t + 1) * 128],
                          r.ps_tr[i][:, :])



FS = 2816
FE = 1408
NFE = FE // 128
NEXP = 8
NWK = 2
NWD = 2
ARN_BYTES = 71296


def emit_projT(E, r, wap, nf, dst, dst_off=0, act=None, bias_col=None,
               scale_imm=None, rhsT=None, bias_off=0):
    t_, v, sc, g = E.tensor, E.vector, E.scalar, E.gpsimd
    rhsT = r.xnT if rhsT is None else rhsT
    t0 = E.c(r.s_t)
    evsem = r.s_s if act == "silu" else r.s_v
    e0 = E.c(evsem)
    wv = []
    for m in range(nf):
        need = r.wk_lastuse.get(m % NWK, 0)
        if m >= NWK:
            need = max(need, t0 + 2 * (m - NWK + 1))
        if need > 0:
            g.wait_ge(r.s_t, need)
        sem = r.s_wp[m % NWK]
        g.dma_start(out=r.wk[m % NWK][:, :, :],
                    in_=wap[:, :, m * 128:(m + 1) * 128]).then_inc(sem, 16)
        wv.append(E.c(sem))
    t_.wait_ge(r.s_s, E.c(r.s_s))
    t_.wait_ge(r.s_v, E.c(r.s_v))
    for m in range(nf):
        t_.wait_ge(r.s_wp[m % NWK], wv[m])
        for half in range(2):
            i = 2 * m + half
            pb = i % 2
            if i >= 2:
                t_.wait_ge(evsem, e0 + i - 1)
            for kc in range(ND):
                ins = t_.matmul(r.ps_a[pb][:, :], r.wk[m % NWK][:, kc, :],
                                rhsT[:, kc, half * 512:(half + 1) * 512],
                                start=(kc == 0), stop=(kc == ND - 1))
                if kc == ND - 1:
                    ins.then_inc(r.s_t, 1)
            r.wk_lastuse[m % NWK] = E.c(r.s_t)
            dslice = dst[:, dst_off + m, half * 512:(half + 1) * 512]
            bc = None if bias_col is None else \
                bias_col[:, bias_off + m:bias_off + m + 1]
            if act == "silu":
                sc.wait_ge(r.s_t, E.c(r.s_t))
                if bc is None:
                    sc.activation(dslice, r.ps_a[pb][:, :], AF.Silu)
                else:
                    sc.activation(dslice, r.ps_a[pb][:, :], AF.Silu, bias=bc)
            else:
                v.wait_ge(r.s_t, E.c(r.s_t))
                if bc is not None and scale_imm is not None:
                    v.tensor_scalar(dslice, r.ps_a[pb][:, :], bc,
                                    scale_imm, ALU.add, ALU.mult)
                elif bc is not None:
                    v.tensor_scalar(dslice, r.ps_a[pb][:, :], bc, None,
                                    ALU.add)
                elif scale_imm is not None:
                    v.tensor_scalar(dslice, r.ps_a[pb][:, :], scale_imm, None,
                                    ALU.mult)
                else:
                    v.tensor_copy(dslice, r.ps_a[pb][:, :])


def emit_moe_up(E, r, wap, nf, silu, dst):
    emit_projT(E, r, wap, nf, dst, act="silu" if silu else None)


def emit_proj_tok(E, r, wap, F, dst, bias_row=None, rhsT=None, scale_imm=None,
                  pre_evac_wait=(), lora=None, act=None, epilogue=None,
                  epilogue_sem=None):
    t_, v, g = E.tensor, E.vector, E.gpsimd
    rhsT = r.xnT if rhsT is None else rhsT
    nchunks = F // 512
    t0 = E.c(r.s_t)
    wv = []
    if lora is not None:
        t_.wait_ge(r.s_w, E.c(r.s_w))
    evsem = epilogue_sem if epilogue_sem is not None else (
        r.s_s if (act == "sigmoid" or epilogue is not None) else r.s_v)
    nwt = len(r.wtok)
    e0 = E.c(evsem)
    for nchunk in range(nchunks):
        need = r.wtok_lastuse.get(nchunk % nwt, 0)
        if nchunk >= nwt:
            need = max(need, t0 + NT * (nchunk - nwt + 1))
        if need > 0:
            g.wait_ge(r.s_t, need)
        sem = r.s_wtp[nchunk % len(r.wtok)]
        g.dma_start(out=r.wtok[nchunk % len(r.wtok)][:, :, :],
                    in_=wap.rearrange("p c (n f) -> p c n f", f=512)[:, :, nchunk, :]
                    ).then_inc(sem, 16)
        wv.append(E.c(sem))
    t_.wait_ge(r.s_s, E.c(r.s_s))
    t_.wait_ge(r.s_v, E.c(r.s_v))
    for nchunk in range(nchunks):
        t_.wait_ge(r.s_wtp[nchunk % len(r.wtok)], wv[nchunk])
        for tt in range(NT):
            i = nchunk * NT + tt
            pb = i % 2
            if i >= 2:
                t_.wait_ge(evsem, e0 + i - 1)
            for kc in range(ND):
                last = (kc == ND - 1) and bias_row is None and lora is None
                ins = t_.matmul(r.ps_a[pb][:, :],
                                rhsT[:, kc, tt * 128:(tt + 1) * 128],
                                r.wtok[nchunk % len(r.wtok)][:, kc, :],
                                start=(kc == 0), stop=last)
                if last:
                    ins.then_inc(r.s_t, 1)
            if lora is not None:
                l1T, wBt = lora
                t_.matmul(r.ps_a[pb][:, :], l1T[:, tt * 128:(tt + 1) * 128],
                          wBt[:, nchunk * 512:(nchunk + 1) * 512],
                          start=False, stop=(bias_row is None)
                          ).then_inc(r.s_t, 1 if bias_row is None else 0)
                if bias_row is None:
                    E.bump(r.s_t, 0)
            if bias_row is not None:
                t_.matmul(r.ps_a[pb][:, :], r.ones_row[0:1, 0:128],
                          bias_row[0:1, nchunk * 512:(nchunk + 1) * 512],
                          start=False, stop=True).then_inc(r.s_t, 1)
            r.wtok_lastuse[nchunk % len(r.wtok)] = E.c(r.s_t)
            if nchunk == 0 and tt == 0:
                for sem_, val_ in pre_evac_wait:
                    v.wait_ge(sem_, val_)
            if epilogue is not None:
                epilogue(tt, nchunk, pb)
            elif act == "sigmoid":
                sc = E.scalar
                sc.wait_ge(r.s_t, E.c(r.s_t))
                sc.activation(dst[:, tt, nchunk * 512:(nchunk + 1) * 512],
                              r.ps_a[pb][:, :], AF.Sigmoid)
            elif scale_imm is not None:
                v.wait_ge(r.s_t, E.c(r.s_t))
                v.tensor_scalar(dst[:, tt, nchunk * 512:(nchunk + 1) * 512],
                                r.ps_a[pb][:, :], scale_imm, None, ALU.mult)
            else:
                v.wait_ge(r.s_t, E.c(r.s_t))
                v.tensor_copy(dst[:, tt, nchunk * 512:(nchunk + 1) * 512],
                              r.ps_a[pb][:, :])


def emit_moe_down(E, r, wap, nf, hbuf):
    t_, v, g = E.tensor, E.vector, E.gpsimd
    t0 = E.c(r.s_t)
    e0 = E.c(r.s_v)
    wv = []
    for nchunk in range(4):
        nwd = len(r.wd)
        need = r.wd_lastuse.get(nchunk % nwd, 0)
        if nchunk >= nwd:
            need = max(need, t0 + NT * (nchunk - nwd + 1))
        if need > 0:
            g.wait_ge(r.s_t, need)
        sem = r.s_wdp[nchunk % len(r.wd)]
        g.dma_start(out=r.wd[nchunk % nwd][:, :nf, :],
                    in_=wap[:, :, nchunk * 512:(nchunk + 1) * 512]).then_inc(sem, 16)
        wv.append(E.c(sem))
    t_.wait_ge(r.s_s, E.c(r.s_s))
    t_.wait_ge(r.s_v, e0)
    for nchunk in range(4):
        t_.wait_ge(r.s_wdp[nchunk % len(r.wd)], wv[nchunk])
        for tt in range(NT):
            i = nchunk * NT + tt
            pb = i % 2
            if i >= 2:
                t_.wait_ge(r.s_v, e0 + i - 1)
            for kc in range(nf):
                ins = t_.matmul(r.ps_a[pb][:, :],
                                hbuf[:, kc, tt * 128:(tt + 1) * 128],
                                r.wd[nchunk % len(r.wd)][:, kc, :],
                                start=(kc == 0), stop=(kc == nf - 1))
                if kc == nf - 1:
                    ins.then_inc(r.s_t, 1)
            r.wd_lastuse[nchunk % len(r.wd)] = E.c(r.s_t)
            v.wait_ge(r.s_t, E.c(r.s_t))
            v.tensor_tensor(r.x_sb[:, tt, nchunk * 512:(nchunk + 1) * 512],
                            r.x_sb[:, tt, nchunk * 512:(nchunk + 1) * 512],
                            r.ps_a[pb][:, :], ALU.add)


def emit_moe_routing(E, r, w):
    t_, v, sc, g = E.tensor, E.vector, E.scalar, E.gpsimd
    g.dma_start(out=r.wrT_sb[:, :, :],
                in_=w["wrT"].rearrange("(c p) e -> p c e", p=128)).then_inc(r.s_w, 16)
    g.wait_ge(r.s_w, E.c(r.s_w))
    ww = E.c(r.s_w)
    vxn = E.c(r.s_v)
    s0 = E.c(r.s_s)
    for tt in range(NT):
        t_.wait_ge(r.s_w, ww)
        if tt == 0:
            t_.wait_ge(r.s_v, vxn)
        if tt >= 2:
            t_.wait_ge(r.s_s, s0 + tt - 1)
        for kc in range(ND):
            ins = t_.matmul(r.ps_a[tt % 2][:, 0:8],
                            r.xnT[:, kc, tt * 128:(tt + 1) * 128],
                            r.wrT_sb[:, kc, :],
                            start=(kc == 0), stop=(kc == ND - 1))
            if kc == ND - 1:
                ins.then_inc(r.s_t, 1)
        st = r.mstat
        v.wait_ge(r.s_t, E.c(r.s_t))
        v.tensor_reduce(st[:, 0:1], r.ps_a[tt % 2][:, 0:8], AX.X, ALU.max,
                        negate=True)
        sc.wait_ge(r.s_v, E.c(r.s_v))
        sc.activation(r.wsm[:, 0:8], r.ps_a[tt % 2][:, 0:8], AF.Exp,
                      bias=st[:, 0:1],
                      accum_out=st[:, 1:2])
        v.wait_ge(r.s_s, E.c(r.s_s))
        v.reciprocal(st[:, 2:3], st[:, 1:2])
        v.tensor_scalar(r.wsm[:, 0:8], r.wsm[:, 0:8], st[:, 2:3], None,
                        ALU.mult)
        v.tensor_reduce(st[:, 3:4], r.wsm[:, 0:8], AX.X, ALU.max)
        v.tensor_scalar(r.mtmp[:, 0:8], r.wsm[:, 0:8], st[:, 3:4], None,
                        ALU.is_ge)
        v.scalar_tensor_tensor(r.mtmp[:, 8:16], r.mtmp[:, 0:8], -2.0,
                               r.wsm[:, 0:8], ALU.mult, ALU.add)
        v.tensor_reduce(st[:, 4:5], r.mtmp[:, 8:16], AX.X, ALU.max)
        v.tensor_scalar(r.mtmp[:, 8:16], r.mtmp[:, 8:16], st[:, 4:5], None,
                        ALU.is_ge)
        v.tensor_tensor(r.mtmp[:, 0:8], r.mtmp[:, 0:8], r.mtmp[:, 8:16],
                        ALU.add)
        v.tensor_tensor(r.wfb[:, tt, 0:8], r.wsm[:, 0:8],
                        r.mtmp[:, 0:8], ALU.mult)
        t_.wait_ge(r.s_v, E.c(r.s_v))
        t_.transpose(r.ps_w[0:8, :], r.wfb[:, tt, 0:8],
                     r.ident[:, :]).then_inc(r.s_t, 1)
        v.wait_ge(r.s_t, E.c(r.s_t))
        v.tensor_copy(r.wT8[0:8, tt * 128:(tt + 1) * 128],
                      r.ps_w[0:8, :])


def emit_moe(E, r, w):
    t_, v, sc, g = E.tensor, E.vector, E.scalar, E.gpsimd
    emit_moe_routing(E, r, w)
    wsg = w["wsgT"].rearrange("(c p) f -> p c f", p=128)
    wsu = w["wsuT"].rearrange("(c p) f -> p c f", p=128)
    wsd = w["wsdT"].rearrange("(c p) d -> p c d", p=128)
    wg = w["wgT"].rearrange("e (c p) f -> p e c f", p=128)
    wu = w["wuT"].rearrange("e (c p) f -> p e c f", p=128)
    wd = w["wdT"].rearrange("e (c p) d -> p e c d", p=128)
    for o in range(2):
        emit_moe_up(E, r, wsg[:, :, o * FE:(o + 1) * FE], NFE, True, r.gbuf)
        emit_moe_up(E, r, wsu[:, :, o * FE:(o + 1) * FE], NFE, False, r.ubuf)
        v0 = E.c(r.s_s)
        for m in range(NFE):
            if m == 0:
                v.wait_ge(r.s_s, v0)
            v.tensor_tensor(r.gbuf[:, m, :], r.gbuf[:, m, :], r.ubuf[:, m, :],
                            ALU.mult)
        t_.wait_ge(r.s_v, E.c(r.s_v))
        emit_moe_down(E, r, wsd[:, o * NFE:(o + 1) * NFE, :], NFE, r.gbuf)
    for e in range(NEXP):
        emit_moe_up(E, r, wg[:, e, :, :], NFE, True, r.gbuf)
        emit_moe_up(E, r, wu[:, e, :, :], NFE, False, r.ubuf)
        t_.wait_ge(r.s_v, E.c(r.s_v))
        for half in range(2):
            ins = t_.matmul(r.ps_g[half][:, :], r.sel8_sb[:, e * 128:(e + 1) * 128],
                            r.wT8[:, half * 512:(half + 1) * 512],
                            start=True, stop=True)
            if half == 1:
                ins.then_inc(r.s_t, 1)
        v.wait_ge(r.s_t, E.c(r.s_t))
        for half in range(2):
            v.tensor_copy(r.gatebc[:, half * 512:(half + 1) * 512],
                          r.ps_g[half][:, :])
        v0 = E.c(r.s_s)
        for m in range(NFE):
            if m == 0:
                v.wait_ge(r.s_s, v0)
            v.tensor_tensor(r.gbuf[:, m, :], r.gbuf[:, m, :], r.ubuf[:, m, :],
                            ALU.mult)
            v.tensor_tensor(r.gbuf[:, m, :], r.gbuf[:, m, :], r.gatebc[:, :],
                            ALU.mult)
        t_.wait_ge(r.s_v, E.c(r.s_v))
        emit_moe_down(E, r, wd[:, e, :, :], NFE, r.gbuf)



NH1B = 16
HG = 4
NHG = NH1B // HG


def emit_oproj_add(E, r, wap, bias_col, rhsT, nf=ND, bias_off=0, lora=None):
    t_, v, g = E.tensor, E.vector, E.gpsimd
    if lora is not None:
        t_.wait_ge(r.s_w, E.c(r.s_w))
    t0 = E.c(r.s_t)
    wv = []
    for m in range(nf):
        need = r.wk_lastuse.get(m % NWK, 0)
        if m >= NWK:
            need = max(need, t0 + (2 + NT) * (m - NWK) + 2)
        if need > 0:
            g.wait_ge(r.s_t, need)
        sem = r.s_wp[m % NWK]
        g.dma_start(out=r.wk[m % NWK][:, :, :],
                    in_=wap[:, :, m * 128:(m + 1) * 128]).then_inc(sem, 16)
        wv.append(E.c(sem))
    t_.wait_ge(r.s_s, E.c(r.s_s))
    t_.wait_ge(r.s_v, E.c(r.s_v))
    for m in range(nf):
        t_.wait_ge(r.s_wp[m % NWK], wv[m])
        for half in range(2):
            i = 2 * m + half
            pb = i % 2
            if i >= 2:
                t_.wait_ge(r.s_v, E.c(r.s_v) - NT - 1)
            for kc in range(ND):
                last = (kc == ND - 1) and lora is None
                ins = t_.matmul(r.ps_a[pb][:, :], r.wk[m % NWK][:, kc, :],
                                rhsT[:, kc, half * 512:(half + 1) * 512],
                                start=(kc == 0), stop=last)
                if last:
                    ins.then_inc(r.s_t, 1)
            if lora is not None:
                l1T, wBt = lora
                t_.matmul(r.ps_a[pb][:, :],
                          wBt[:, m * 128:(m + 1) * 128],
                          l1T[:, half * 512:(half + 1) * 512],
                          start=False, stop=True).then_inc(r.s_t, 1)
            v.wait_ge(r.s_t, E.c(r.s_t))
            if bias_col is None:
                v.tensor_copy(r.ystage[:, half * 512:(half + 1) * 512],
                              r.ps_a[pb][:, :])
            else:
                v.tensor_scalar(r.ystage[:, half * 512:(half + 1) * 512],
                                r.ps_a[pb][:, :],
                                bias_col[:, bias_off + m:bias_off + m + 1],
                                None, ALU.add)
        r.wk_lastuse[m % NWK] = E.c(r.s_t)
        t_.wait_ge(r.s_v, E.c(r.s_v))
        for tt in range(NT):
            pb = tt % 2
            if tt >= 2:
                t_.wait_ge(r.s_v, E.c(r.s_v) - 1)
            t_.transpose(r.ps_tr[pb][:, :], r.ystage[:, tt * 128:(tt + 1) * 128],
                         r.ident[:, :]).then_inc(r.s_t, 1)
            v.wait_ge(r.s_t, E.c(r.s_t))
            v.tensor_tensor(r.x_sb[:, tt, m * 128:(m + 1) * 128],
                            r.x_sb[:, tt, m * 128:(m + 1) * 128],
                            r.ps_tr[pb][:, :], ALU.add)


def emit_proj_tok64(E, r, wap, F, dst64, bias_row=None, rhsT=None):
    t_, v, g = E.tensor, E.vector, E.gpsimd
    rhsT = r.xnT if rhsT is None else rhsT
    assert F == 512
    sem = r.s_wtp[0]
    need = r.wtok_lastuse.get(0, 0)
    if need > 0:
        g.wait_ge(r.s_t, need)
    g.dma_start(out=r.wtok[0][:, :, :], in_=wap).then_inc(sem, 16)
    wv = E.c(sem)
    e0 = E.c(r.s_v)
    t_.wait_ge(sem, wv)
    t_.wait_ge(r.s_s, E.c(r.s_s))
    t_.wait_ge(r.s_v, e0)
    for cc in range(2 * NT):
        tt, sub = cc // 2, cc % 2
        pb = cc % 2
        if cc >= 2:
            t_.wait_ge(r.s_v, e0 + cc - 1)
        for kc in range(ND):
            ins = t_.matmul(r.ps_a[pb][0:64, :],
                            rhsT[:, kc, tt * 128 + sub * 64:
                                 tt * 128 + sub * 64 + 64],
                            r.wtok[0][:, kc, :],
                            start=(kc == 0), stop=(kc == ND - 1 and bias_row is None))
            if kc == ND - 1 and bias_row is None:
                ins.then_inc(r.s_t, 1)
        if bias_row is not None:
            t_.matmul(r.ps_a[pb][0:64, :], r.ones_row[0:1, 0:64],
                      bias_row[0:1, 0:512],
                      start=False, stop=True).then_inc(r.s_t, 1)
        r.wtok_lastuse[0] = E.c(r.s_t)
        v.wait_ge(r.s_t, E.c(r.s_t))
        v.tensor_copy(dst64[0:64, cc, :], r.ps_a[pb][0:64, :])


def emit_l1ab(E, r, w):
    t_, v, sc, g = E.tensor, E.vector, E.scalar, E.gpsimd
    qw = w["l1ab_qwT"].rearrange("(c p) f -> p c f", p=128)
    kw = w["l1ab_kwT"].rearrange("(c p) f -> p c f", p=128)
    vw = w["l1ab_vwT"].rearrange("(c p) f -> p c f", p=128)
    ogw = w["l1ab_ogwT"].rearrange("(c p) f -> p c f", p=128)
    ow = w["l1ab_owT"].rearrange("(c p) f -> p c f", p=128)
    qb = r.l1ab_cols
    kb = r.l1ab_cols
    ob = r.l1ab_cols
    vb = r.l1ab_vbrow[0:1, :]
    bw = E.c(r.s_w)
    v0beta = E.c(r.s_v)
    for cc in range(2 * NT):
        tt, sub = cc // 2, cc % 2
        t_.wait_ge(r.s_w, bw)
        if cc == 0:
            t_.wait_ge(r.s_v, v0beta)
        if cc >= 2:
            t_.wait_ge(r.s_v, v0beta + 5 * (cc - 1))
        for kc in range(ND):
            ins = t_.matmul(r.ps_a[cc % 2][0:64, 0:16],
                            r.xnT[:, kc, tt * 128 + sub * 64:
                                  tt * 128 + sub * 64 + 64],
                            r.bwT_sb[:, kc, :],
                            start=(kc == 0), stop=False)
        t_.matmul(r.ps_a[cc % 2][0:64, 0:16], r.ones_row[0:1, 0:64],
                  r.bb16[0:1, :], start=False, stop=True).then_inc(r.s_t, 1)
        v.wait_ge(r.s_t, E.c(r.s_t))
        v.tensor_scalar(r.mtmp[0:64, 16:32], r.ps_a[cc % 2][0:64, 0:16],
                        -1.0, None, ALU.mult)
        v.tensor_tensor(r.mtmp[0:64, 0:16], r.ps_a[cc % 2][0:64, 0:16],
                        r.mtmp[0:64, 16:32], ALU.min)
        sc.wait_ge(r.s_v, E.c(r.s_v))
        sc.activation(r.mtmp[0:64, 0:16], r.mtmp[0:64, 0:16], AF.Exp)
        v.wait_ge(r.s_s, E.c(r.s_s))
        v.tensor_scalar(r.mtmp[0:64, 0:16], r.mtmp[0:64, 0:16], 1.0, None,
                        ALU.add)
        sc.wait_ge(r.s_v, E.c(r.s_v))
        sc.activation(r.mtmp[0:64, 0:16], r.mtmp[0:64, 0:16], AF.Ln)
        v.wait_ge(r.s_s, E.c(r.s_s))
        v.tensor_scalar(r.mtmp[0:64, 16:32],
                        r.ps_a[cc % 2][0:64, 0:16], 0.0, None, ALU.max)
        v.tensor_tensor(r.betab[0:64, cc, :], r.mtmp[0:64, 16:32],
                        r.mtmp[0:64, 0:16], ALU.add)
    SCALE = float(128) ** 0.5
    for hg in range(NHG):
        emit_projT(E, r, qw[:, :, hg * 512:(hg + 1) * 512], HG, r.gbuf,
                   dst_off=0, bias_col=qb, bias_off=0 + hg * 4)
        emit_projT(E, r, kw[:, :, hg * 512:(hg + 1) * 512], HG, r.gbuf,
                   dst_off=4, bias_col=kb, bias_off=16 + hg * 4)
        emit_proj_tok64(E, r, vw[:, :, hg * 512:(hg + 1) * 512], 512, r.vtok,
                        bias_row=vb[0:1, hg * 512:(hg + 1) * 512])
        vv0 = E.c(r.s_v)
        for cc in range(2 * NT):
            if cc == 0:
                v.wait_ge(r.s_v, vv0)
            for hl in range(HG):
                h = hg * HG + hl
                v.tensor_scalar(r.vtok[0:64, cc, hl * 128:(hl + 1) * 128],
                                r.vtok[0:64, cc, hl * 128:(hl + 1) * 128],
                                r.betab[0:64, cc, h:h + 1], None, ALU.mult)
        for qk in range(2):
            for hl in range(HG):
                ch = qk * 4 + hl
                sc.wait_ge(r.s_v, E.c(r.s_v))
                sc.activation(r.scr2_bf[:, 0:1024], r.gbuf[:, ch, :], AF.Square)
                t_.wait_ge(r.s_s, E.c(r.s_s))
                t_.wait_ge(r.s_v, E.c(r.s_v))
                for half in range(2):
                    t_.matmul(r.ps_g[half][0:1, :], r.ones_col[:, 0:1],
                              r.scr2_bf[:, half * 512:(half + 1) * 512],
                              start=True, stop=True).then_inc(r.s_t, 1)
                for half in range(2):
                    v.wait_ge(r.s_t, E.c(r.s_t))
                    v.tensor_copy(r.nrm[0:1, :], r.ps_g[half][0:1, :])
                    sc.wait_ge(r.s_v, E.c(r.s_v))
                    sc.activation(r.nrm[0:1, :], r.nrm[0:1, :], AF.Sqrt)
                    v.wait_ge(r.s_s, E.c(r.s_s))
                    v.tensor_scalar(r.nrm[0:1, :], r.nrm[0:1, :],
                                    SCALE * 1e-12, None, ALU.max)
                    sc.wait_ge(r.s_v, E.c(r.s_v))
                    sc.activation(r.nrm[0:1, :], r.nrm[0:1, :], AF.Ln)
                    v.wait_ge(r.s_s, E.c(r.s_s))
                    v.tensor_scalar(r.nrm[0:1, :], r.nrm[0:1, :], -1.0, None,
                                    ALU.mult)
                    sc.wait_ge(r.s_v, E.c(r.s_v))
                    sc.activation(r.nrm[0:1, :], r.nrm[0:1, :], AF.Exp)
                    v.wait_ge(r.s_s, E.c(r.s_s))
                    v.tensor_copy(r.nrm_bf[0:1, :], r.nrm[0:1, :])
                    t_.wait_ge(r.s_v, E.c(r.s_v))
                    t_.matmul(r.ps_g[half][:, :], r.ones_row[0:1, 0:128],
                              r.nrm_bf[0:1, :],
                              start=True, stop=True).then_inc(r.s_t, 1)
                    v.wait_ge(r.s_t, E.c(r.s_t))
                    v.tensor_tensor(r.gbuf[:, ch, half * 512:(half + 1) * 512],
                                    r.gbuf[:, ch, half * 512:(half + 1) * 512],
                                    r.ps_g[half][:, :], ALU.mult)
        av0 = E.c(r.s_v)
        for hl in range(HG):
            h = hg * HG + hl
            for cc in range(16):
                i = hl * 16 + cc
                pb = i % 2
                if i == 0:
                    t_.wait_ge(r.s_v, av0)
                if i >= 2:
                    t_.wait_ge(r.s_v, E.c(r.s_v) - 1)
                t_.matmul(r.ps_s[pb][0:64, 0:64],
                          r.gbuf[:, 4 + hl, cc * 64:(cc + 1) * 64],
                          r.gbuf[:, hl, cc * 64:(cc + 1) * 64],
                          start=True, stop=True).then_inc(r.s_t, 1)
                v.wait_ge(r.s_t, E.c(r.s_t))
                v.tensor_tensor(r.attn64[0:64, :], r.ps_s[pb][0:64, 0:64],
                                r.decayT[0:64, :], ALU.mult)
                t_.wait_ge(r.s_v, E.c(r.s_v))
                t_.matmul(r.ps_o[pb][:, 0:64],
                          r.vtok[0:64, cc, hl * 128:(hl + 1) * 128],
                          r.attn64[0:64, :],
                          start=True, stop=True).then_inc(r.s_t, 1)
                v.wait_ge(r.s_t, E.c(r.s_t))
                v.tensor_copy(r.attT[:, h, cc * 64:(cc + 1) * 64],
                              r.ps_o[pb][:, 0:64])
        if getattr(r, "DBG_NOGATE", False):
            continue
        emit_projT(E, r, ogw[:, :, hg * 512:(hg + 1) * 512], HG, r.gbuf,
                   dst_off=4, act="silu")
        gv0 = E.c(r.s_s)
        for hl in range(HG):
            h = hg * HG + hl
            if hl == 0:
                v.wait_ge(r.s_s, gv0)
            v.tensor_tensor(r.attT[:, h, :], r.attT[:, h, :],
                            r.gbuf[:, 4 + hl, :], ALU.mult)
    if not getattr(r, "DBG_NOGATE", False):
        emit_oproj_add(E, r, ow, ob, r.attT, bias_off=32)



def emit_loraT(E, r, wAt_ap, dst, scale, rhsT=None, nf=ND):
    t_, v, g = E.tensor, E.vector, E.gpsimd
    rhsT = r.xnT if rhsT is None else rhsT
    g.wait_ge(r.s_t, E.c(r.s_t))
    g.dma_start(out=r.lA_sb[:, :, :], in_=wAt_ap).then_inc(r.s_w, 16)
    g.wait_ge(r.s_w, E.c(r.s_w))
    ww = E.c(r.s_w)
    t_.wait_ge(r.s_w, ww)
    t_.wait_ge(r.s_v, E.c(r.s_v))
    for half in range(2):
        for kc in range(nf):
            ins = t_.matmul(r.ps_d[half][0:8, :], r.lA_sb[:, kc, :],
                            rhsT[:, kc, half * 512:(half + 1) * 512],
                            start=(kc == 0), stop=(kc == nf - 1))
            if kc == nf - 1:
                ins.then_inc(r.s_t, 1)
    v.wait_ge(r.s_t, E.c(r.s_t))
    for half in range(2):
        v.tensor_scalar(dst[0:8, half * 512:(half + 1) * 512],
                        r.ps_d[half][0:8, :], scale, None, ALU.mult)


def emit_rope_interleaved(E, r, stage, ctab, stab, width):
    v, g = E.vector, E.gpsimd
    nh = width // 128
    for t in range(NT):
        pb = t % len(r.ropeCw)
        need = r.rope_lastuse.get(pb, 0)
        if need > 0:
            g.wait_ge(r.s_v, need)
        sem = r.s_wp[pb]
        sem = r.s_wp[pb]
        g.dma_start(out=r.ropeCw[pb][:, 0:width],
                    in_=ctab.rearrange("(t p) w -> p t w", p=128)[:, t, 0:width]
                    ).then_inc(sem, 16)
        g.dma_start(out=r.ropeSw[pb][:, 0:width],
                    in_=stab.rearrange("(t p) w -> p t w", p=128)[:, t, 0:width]
                    ).then_inc(sem, 16)
        wv = E.c(sem)
        qv = stage[:, t, 0:width].rearrange("p (d two) -> p d two", two=2)
        tmp = r.ropetmpw[:, 0:width].rearrange("p (d two) -> p d two", two=2)
        v.wait_ge(sem, wv)
        v.tensor_scalar(tmp[:, :, 0:1], qv[:, :, 1:2], -1.0, None, ALU.mult)
        v.tensor_scalar(tmp[:, :, 1:2], qv[:, :, 0:1], 1.0, None, ALU.mult)
        v.tensor_tensor(stage[:, t, 0:width], stage[:, t, 0:width],
                        r.ropeCw[pb][:, 0:width], ALU.mult)
        v.tensor_tensor(r.ropetmpw[:, 0:width], r.ropetmpw[:, 0:width],
                        r.ropeSw[pb][:, 0:width], ALU.mult)
        v.tensor_tensor(stage[:, t, 0:width], stage[:, t, 0:width],
                        r.ropetmpw[:, 0:width], ALU.add)
        r.rope_lastuse[pb] = E.c(r.s_v)


def emit_l0aa(E, r, w, ex):
    t_, v, sc, g, sy = E.tensor, E.vector, E.scalar, E.gpsimd, E.sync
    qw = w["l0aa_qwT"].rearrange("(c p) f -> p c f", p=128)
    kw = w["l0aa_kwT"].rearrange("(c p) f -> p c f", p=128)
    vw = w["l0aa_vwT"].rearrange("(c p) f -> p c f", p=128)
    ow = w["l0aa_owT"].rearrange("(c p) f -> p c f", p=128)
    SC = 1.0 / math.sqrt(128.0)
    emit_loraT(E, r, w["l0aa_qAt"].rearrange("(c p) e -> p c e", p=128),
               r.l1q, 2.0)
    emit_loraT(E, r, w["l0aa_kAt"].rearrange("(c p) e -> p c e", p=128),
               r.l1k, 2.0)
    emit_loraT(E, r, w["l0aa_vAt"].rearrange("(c p) e -> p c e", p=128),
               r.l1v, 2.0)
    emit_proj_tok(E, r, qw, D, r.qstage, scale_imm=SC,
                  lora=(r.l1q, w2sb(E, r, "l0aa_qBt", w)))
    emit_rope_interleaved(E, r, r.qstage, w["l0aa_ropeCw"], w["l0aa_ropeSw"], D)
    sy.wait_ge(r.s_v, E.c(r.s_v))
    sy.dma_start(out=r.qscr.ap().rearrange("(t p) d -> p t d", p=128),
                 in_=r.qstage[:, :, :]).then_inc(r.s_x, 16)
    qscr_done = E.c(r.s_x)
    emit_proj_tok(E, r, kw, 1024, r.kstage, pre_evac_wait=[(r.s_x, qscr_done)],
                  lora=(r.l1k, w2sb(E, r, "l0aa_kBt", w)))
    emit_rope_interleaved(E, r, r.kstage, w["l0aa_ropeCw"], w["l0aa_ropeSw"], 1024)
    sy.wait_ge(r.s_v, E.c(r.s_v))
    emit_halved_dma(E, r,
                    r.ksh_aa.ap().rearrange("(t p) d -> p t d", p=128)[:, 0:NT, :],
                    r.ksh_aa.ap().rearrange("(t p) d -> p t d", p=128)[:, NT:2 * NT, :],
                    r.kstage[:, :, 0:1024], r.kstage[:, :, 0:1024], r.s_kv)
    emit_proj_tok(E, r, vw, 1024, r.kstage, pre_evac_wait=[(r.s_kv, E.c(r.s_kv))],
                  lora=(r.l1v, w2sb(E, r, "l0aa_vBt", w)))
    sy.wait_ge(r.s_v, E.c(r.s_v))
    for h in range(8):
        vsh_h0 = r.vsh_aa.ap()[h, 0, :, :].rearrange("(t p) d -> p t d", p=128)
        vsh_h1 = r.vsh_aa.ap()[h, 1, :, :].rearrange("(t p) d -> p t d", p=128)
        emit_halved_dma(E, r, vsh_h0, vsh_h1,
                        r.kstage[:, :, h * 128:(h + 1) * 128],
                        r.kstage[:, :, h * 128:(h + 1) * 128], r.s_kv)
    emit_flag_write(E, r, ex, [(r.s_kv, E.c(r.s_kv))])
    emit_flag_poll(E, r, ex)
    sy.wait_ge(r.s_x, qscr_done)
    sy.wait_ge(r.s_t, E.c(r.s_t))
    for c in range(ND):
        sy.dma_start(out=r.qT[:, c, :], in_=r.qscr.ap()[:, c * 128:(c + 1) * 128],
                     transpose=True).then_inc(r.s_x, 16)
    qT_done = E.c(r.s_x)
    for kvh in range(8):
        emit_prep_pump(E, r, 6)
        hb = kvh % 2
        if kvh >= 2:
            sy.wait_ge(r.s_t, r.kv_lastuse.get(hb, 0))
        ksh_m = r.ksh_aa.ap()[:, kvh * 128:(kvh + 1) * 128]
        with sy.If_eq(r.rg_parity, 0):
            with E.nocount():
                sy.dma_start(out=r.kt[hb][:, 0:1024], in_=ksh_m[0:1024, :],
                             transpose=True).then_inc(r.s_kv, 16)
                sy.dma_start(out=r.kt[hb][:, 1024:2048], in_=ksh_m[1024:2048, :],
                             transpose=True).then_inc(r.s_kv, 16)
                sy.dma_start(out=r.vt[hb][:, 0:8, :],
                             in_=r.vsh_aa.ap()[kvh, 0, :, :].rearrange(
                                 "(c p) d -> p c d", p=128)).then_inc(r.s_kv, 16)
                sy.dma_start(out=r.vt[hb][:, 8:16, :],
                             in_=r.vsh_aa.ap()[kvh, 1, :, :].rearrange(
                                 "(c p) d -> p c d", p=128)).then_inc(r.s_kv, 16)
        with sy.Else():
            sy.dma_start(out=r.kt[hb][:, 0:1024], in_=ksh_m[1024:2048, :],
                         transpose=True).then_inc(r.s_kv, 16)
            sy.dma_start(out=r.kt[hb][:, 1024:2048], in_=ksh_m[0:1024, :],
                         transpose=True).then_inc(r.s_kv, 16)
            sy.dma_start(out=r.vt[hb][:, 0:8, :],
                         in_=r.vsh_aa.ap()[kvh, 1, :, :].rearrange(
                             "(c p) d -> p c d", p=128)).then_inc(r.s_kv, 16)
            sy.dma_start(out=r.vt[hb][:, 8:16, :],
                         in_=r.vsh_aa.ap()[kvh, 0, :, :].rearrange(
                             "(c p) d -> p c d", p=128)).then_inc(r.s_kv, 16)
        kvhc = E.c(r.s_kv)
        t_.wait_ge(r.s_kv, kvhc)
        for h in (2 * kvh, 2 * kvh + 1):
            t_.wait_ge(r.s_v, E.c(r.s_v))
            if kvh == 0 and h == 0:
                t_.wait_ge(r.s_x, qT_done)
            e0 = E.c(r.s_s)
            pv_done = {}
            fix_done = {}
            for kc in range(ND):
                for half in range(2):
                    i = kc * 2 + half
                    pb = i % 2
                    if i >= 2:
                        t_.wait_ge(r.s_s, e0 + i - 1)
                    ins = t_.matmul(r.ps_a[pb][:, :],
                                    r.kt[hb][:, kc * 128:(kc + 1) * 128],
                                    r.qT[:, h, half * 512:(half + 1) * 512],
                                    start=True, stop=True)
                    ins.then_inc(r.s_t, 1)
                    if kc >= 2:
                        sc.wait_ge(r.s_t, pv_done[kc - 2])
                        if kc - 2 in fix_done:
                            sc.wait_ge(r.s_v, fix_done[kc - 2])
                    sc.wait_ge(r.s_t, E.c(r.s_t))
                    sc.activation(r.at_sb[:, kc % 2, half * 512:(half + 1) * 512],
                                  r.ps_a[pb][:, :], AF.Exp,
                                  bias=r.l0ab_mb[:, kc:kc + 1])
                if kc < 8:
                    v.wait_ge(r.s_s, E.c(r.s_s))
                    if kc > 0:
                        v.memset(r.at_sb[:, kc % 2, 0:kc * 128], 0.0)
                    v.tensor_tensor(r.at_sb[:, kc % 2, kc * 128:(kc + 1) * 128],
                                    r.at_sb[:, kc % 2, kc * 128:(kc + 1) * 128],
                                    r.triu[:, :], ALU.mult)
                    fix_done[kc] = E.c(r.s_v)
                t_.wait_ge(r.s_v, E.c(r.s_v))
                t_.wait_ge(r.s_s, E.c(r.s_s))
                for half in range(2):
                    t_.matmul(r.ps_d[half][0:1, :], r.ones_col[:, 0:1],
                              r.at_sb[:, kc % 2, half * 512:(half + 1) * 512],
                              start=(kc == 0), stop=(kc == ND - 1))
                    ins = t_.matmul(r.ps_g[half][:, :],
                                    r.vt[hb][:, kc, :],
                                    r.at_sb[:, kc % 2, half * 512:(half + 1) * 512],
                                    start=(kc == 0), stop=(kc == ND - 1))
                    ins.then_inc(r.s_t, 1)
                pv_done[kc] = E.c(r.s_t)
            r.kv_lastuse[hb] = E.c(r.s_t)
            emit_attn_norm(E, r, h)
    emit_loraT(E, r, w["l0aa_oAt"].rearrange("(c p) e -> p c e", p=128),
               r.l1q, 2.0, rhsT=r.attT)
    emit_oproj_add(E, r, ow, None, r.attT,
                   lora=(r.l1q, w2sb(E, r, "l0aa_oBt", w)))


def emit_attn_norm(E, r, h):
    t_, v, sc = E.tensor, E.vector, E.scalar
    v.wait_ge(r.s_t, E.c(r.s_t))
    for half, dstb in ((0, r.nrm_bf), (1, r.nrm_bf2)):
        v.tensor_copy(r.nrm[0:1, 0:512], r.ps_d[half][0:1, :])
        sc.wait_ge(r.s_v, E.c(r.s_v))
        sc.activation(r.nrm[0:1, 0:512], r.nrm[0:1, 0:512], AF.Ln)
        v.wait_ge(r.s_s, E.c(r.s_s))
        v.tensor_scalar(r.nrm[0:1, 0:512], r.nrm[0:1, 0:512], -1.0, None, ALU.mult)
        sc.wait_ge(r.s_v, E.c(r.s_v))
        sc.activation(r.nrm[0:1, 0:512], r.nrm[0:1, 0:512], AF.Exp)
        v.wait_ge(r.s_s, E.c(r.s_s))
        v.tensor_copy(dstb[0:1, 0:512], r.nrm[0:1, 0:512])
    t_.wait_ge(r.s_v, E.c(r.s_v))
    t_.matmul(r.ps_a[0][:, :], r.ones_row[0:1, 0:128],
              r.nrm_bf[0:1, 0:512], start=True, stop=True)
    t_.matmul(r.ps_a[1][:, :], r.ones_row[0:1, 0:128],
              r.nrm_bf2[0:1, 0:512], start=True, stop=True).then_inc(r.s_t, 1)
    v.wait_ge(r.s_t, E.c(r.s_t))
    for half in range(2):
        v.tensor_copy(r.bcbuf[:, half * 512:(half + 1) * 512],
                      r.ps_a[half][:, :])
    for half in range(2):
        v.tensor_tensor(r.attT[:, h, half * 512:(half + 1) * 512],
                        r.ps_g[half][:, :],
                        r.bcbuf[:, half * 512:(half + 1) * 512], ALU.mult)


def w2sb(E, r, name, w):
    F = w[name].shape[1]
    E.gpsimd.wait_ge(r.s_t, E.c(r.s_t))
    E.gpsimd.dma_start(out=r.lB_sb[:, 0:F], in_=w[name]).then_inc(r.s_w, 16)
    E.gpsimd.wait_ge(r.s_w, E.c(r.s_w))
    return r.lB_sb[:, 0:F]



H0F = 8192


def _row_to_colT(E, r, row_bf, ncol, dst):
    t_, v = E.tensor, E.vector
    t_.wait_ge(r.s_v, E.c(r.s_v))
    for c in range(ncol // 128):
        pb = c % 2
        if c >= 2:
            t_.wait_ge(r.s_v, E.c(r.s_v) - 1)
        t_.transpose(r.ps_tr[pb][:, 0:1], row_bf[0:1, c * 128:(c + 1) * 128],
                     r.ident[0:1, 0:1]).then_inc(r.s_t, 1)
        v.wait_ge(r.s_t, E.c(r.s_t))
        v.tensor_copy(dst[:, c:c + 1], r.ps_tr[pb][:, 0:1])


def _row_ln(E, r, src_row, dst_row, width, eps, gam=None, bet=None):
    v, sc = E.vector, E.scalar
    st = r.mtmp
    v.tensor_scalar(r.scr2_bf[0:1, 0:width], src_row, 1.0, 0.0, ALU.mult,
                    ALU.add, accum_out=st[0:1, 0:1])
    v.tensor_scalar(st[0:1, 0:1], st[0:1, 0:1], -1.0 / width, None, ALU.mult)
    v.tensor_scalar(dst_row, src_row, st[0:1, 0:1], None, ALU.add)
    sc.wait_ge(r.s_v, E.c(r.s_v))
    sc.activation(r.scr2_bf[0:1, 0:width], dst_row, AF.Square,
                  accum_out=st[0:1, 1:2])
    v.wait_ge(r.s_s, E.c(r.s_s))
    v.tensor_scalar(st[0:1, 1:2], st[0:1, 1:2], 1.0 / width, None, ALU.mult)
    v.tensor_scalar(st[0:1, 1:2], st[0:1, 1:2], eps, None, ALU.add)
    sc.wait_ge(r.s_v, E.c(r.s_v))
    sc.activation(st[0:1, 2:3], st[0:1, 1:2], AF.Ln)
    v.wait_ge(r.s_s, E.c(r.s_s))
    v.tensor_scalar(st[0:1, 2:3], st[0:1, 2:3], -0.5, None, ALU.mult)
    sc.wait_ge(r.s_v, E.c(r.s_v))
    sc.activation(st[0:1, 2:3], st[0:1, 2:3], AF.Exp)
    v.wait_ge(r.s_s, E.c(r.s_s))
    v.tensor_scalar(dst_row, dst_row, st[0:1, 2:3], None, ALU.mult)
    if gam is not None:
        gr = gam()
        v.tensor_tensor(dst_row, dst_row, gr, ALU.mult)
        br = bet()
        v.tensor_tensor(dst_row, dst_row, br, ALU.add)


def _load_row(E, r, w, seg, which):
    g = E.gpsimd
    g.wait_ge(r.s_v, E.c(r.s_v))
    g.dma_start(out=r.rowc[which][0:1, :],
                in_=w["l0ffn_rows_in"][0:1, seg * D:(seg + 1) * D]).then_inc(r.s_w, 16)
    g.wait_ge(r.s_w, E.c(r.s_w))
    ww = E.c(r.s_w)
    E.vector.wait_ge(r.s_w, ww)
    return r.rowc[which][0:1, :]


def _row_stream_mm(E, r, wap3, colT, col0, ngrp, psum, first, last_extra=None):
    t_, g = E.tensor, E.gpsimd
    wv = {}

    def _issue(grp):
        wb = (r.wrow_rot + grp) % 2
        g.wait_ge(r.s_t, r.wrow_lastuse.get(wb, 0))
        sem = r.s_wp[wb]
        g.dma_start(out=r.wrow[wb][:, :, :],
                    in_=wap3[:, grp * 8:(grp + 1) * 8, :]).then_inc(sem, 16)
        wv[grp] = E.c(sem)

    _issue(0)
    if ngrp > 1:
        _issue(1)
    for grp in range(ngrp):
        wb = (r.wrow_rot + grp) % 2
        t_.wait_ge(r.s_wp[wb], wv[grp])
        if grp == 0:
            t_.wait_ge(r.s_v, E.c(r.s_v))
        for kc in range(8):
            lastmm = (grp == ngrp - 1 and kc == 7 and last_extra is None)
            ins = t_.matmul(psum, colT[:, col0 + grp * 8 + kc:col0 + grp * 8 + kc + 1],
                            r.wrow[wb][:, kc, :],
                            start=(first and grp == 0 and kc == 0), stop=lastmm)
            if lastmm:
                ins.then_inc(r.s_t, 1)
            elif kc == 7:
                ins.then_inc(r.s_t, 1)
        r.wrow_lastuse[wb] = E.c(r.s_t)
        if grp + 2 < ngrp:
            _issue(grp + 2)
    r.wrow_rot = (r.wrow_rot + ngrp) % 2
    if last_extra is not None:
        t_.matmul(psum, r.ones_row[0:1, 0:1], last_extra,
                  start=False, stop=True).then_inc(r.s_t, 1)


def _row_proj_full(E, r, wap, out_row, bias_row):
    t_, v, g = E.tensor, E.vector, E.gpsimd
    for nchunk in range(4):
        _row_stream_mm(E, r,
                       wap.rearrange("p c (n f) -> p c n f", f=512)[:, :, nchunk, :],
                       r.tcolT, 0, 2, r.ps_d[0][0:1, :], True)
        v.wait_ge(r.s_t, E.c(r.s_t))
        v.tensor_copy(out_row[0:1, nchunk * 512:(nchunk + 1) * 512],
                      r.ps_d[0][0:1, :])
    v.tensor_tensor(out_row, out_row, bias_row, ALU.add)


def emit_l0ffn(E, r, w, ex, xntok):
    t_, v, sc, g, sy = E.tensor, E.vector, E.scalar, E.gpsimd, E.sync
    sy.wait_ge(r.s_v, E.c(r.s_v))
    sy.dma_start(out=r.rowscr.ap()[0:1, :],
                 in_=xntok[127:128, NT - 1, :]).then_inc(r.s_x, 16)
    sy.wait_ge(r.s_x, E.c(r.s_x))
    emit_halved_dma(E, r, r.lastrow_sh.ap()[0:1, :], r.lastrow_sh.ap()[1:2, :],
                    r.rowscr.ap()[0:1, :], r.rowscr.ap()[0:1, :], r.s_kv)
    emit_flag_write(E, r, ex, [(r.s_kv, E.c(r.s_kv))])
    emit_flag_poll(E, r, ex)
    sy.dma_start(out=r.row_bfin[0:1, :], in_=r.lastrow_sh.ap()[1:2, :]).then_inc(r.s_x, 16)
    v.wait_ge(r.s_x, E.c(r.s_x))
    _row_ln(E, r, r.row_bfin[0:1, :], r.row_b[0:1, :], D, 1e-6,
            lambda: _load_row(E, r, w, 0, 0), lambda: _load_row(E, r, w, 1, 0))
    v.tensor_copy(r.row_bf[0:1, :], r.row_b[0:1, :])
    _row_to_colT(E, r, r.row_bf[0:1, :], D, r.tcolT)
    aggb = _load_row(E, r, w, 2, 0)
    _row_proj_full(E, r, w["l0ffn_aggw"].rearrange("(c p) f -> p c f", p=128),
                   r.thought[0:1, :], aggb)
    d1 = r.d1c.rearrange("(c p) f -> p c f", p=128)
    d2 = r.d2c.rearrange("(c p) f -> p c f", p=128)
    for step in range(4):
        _row_ln(E, r, r.thought[0:1, :], r.row_b[0:1, :], D, 1e-6,
                lambda: _load_row(E, r, w, 3, 0), lambda: _load_row(E, r, w, 4, 0))
        v.tensor_copy(r.row_bf[0:1, :], r.row_b[0:1, :])
        _row_to_colT(E, r, r.row_bf[0:1, :], D, r.tcolT)
        for m in range(8):
            t_.wait_ge(r.s_s, E.c(r.s_s))
            _row_stream_mm(E, r, d1[:, :, m * 512:(m + 1) * 512], r.tcolT, 0, 2,
                           r.ps_d[0][0:1, :], True,
                           last_extra=r.b1h[0:1, m * 512:(m + 1) * 512])
            sc.wait_ge(r.s_t, E.c(r.s_t))
            sc.activation(r.t1h_bf[0:1, m * 512:(m + 1) * 512], r.ps_d[0][0:1, :],
                          AF.Gelu)
        sc_done = E.c(r.s_s)
        v.wait_ge(r.s_s, sc_done)
        _row_to_colT(E, r, r.t1h_bf[0:1, :], 4096, r.t1colT)
        for nchunk in range(4):
            _row_stream_mm(E, r, d2[:, :, nchunk * 512:(nchunk + 1) * 512],
                           r.t1colT, 0, 4, r.ps_d[0][0:1, :], True)
            v.wait_ge(r.s_t, E.c(r.s_t))
            v.tensor_copy(r.row_b[0:1, nchunk * 512:(nchunk + 1) * 512],
                          r.ps_d[0][0:1, :])
        sy.wait_ge(r.s_v, E.c(r.s_v))
        dsh = r.delta_sh.ap()
        emit_halved_dma(E, r, dsh[2 * step:2 * step + 1, :],
                        dsh[2 * step + 1:2 * step + 2, :],
                        r.row_b[0:1, :], r.row_b[0:1, :], r.s_kv)
        emit_flag_write(E, r, ex + 1 + step, [(r.s_kv, E.c(r.s_kv))])
        emit_flag_poll(E, r, ex + 1 + step)
        with sy.If_eq(r.rg_parity, 0):
            with E.nocount():
                sy.dma_start(out=r.row_a[0:1, :],
                             in_=dsh[2 * step + 1:2 * step + 2, :]).then_inc(r.s_x, 16)
        with sy.Else():
            sy.dma_start(out=r.row_a[0:1, :],
                         in_=dsh[2 * step:2 * step + 1, :]).then_inc(r.s_x, 16)
        v.wait_ge(r.s_x, E.c(r.s_x))
        d2b = _load_row(E, r, w, 5, 0)
        v.tensor_tensor(r.thought[0:1, :], r.thought[0:1, :], r.row_b[0:1, :],
                        ALU.add)
        v.tensor_tensor(r.thought[0:1, :], r.thought[0:1, :], r.row_a[0:1, :],
                        ALU.add)
        v.tensor_tensor(r.thought[0:1, :], r.thought[0:1, :], d2b, ALU.add)
    v.tensor_copy(r.row_bf[0:1, :], r.thought[0:1, :])
    _row_to_colT(E, r, r.row_bf[0:1, :], D, r.tcolT)
    bcb_row = _load_row(E, r, w, 6, 0)
    _row_proj_full(E, r, w["l0ffn_bcw"].rearrange("(c p) f -> p c f", p=128),
                   r.row_b[0:1, :], bcb_row)
    v.tensor_copy(r.row_bf[0:1, :], r.row_b[0:1, :])
    for nchunk in range(4):
        t_.wait_ge(r.s_v, E.c(r.s_v))
        t_.matmul(r.ps_g[nchunk % 2][:, :], r.ones_row[0:1, 0:128],
                  r.row_bf[0:1, nchunk * 512:(nchunk + 1) * 512],
                  start=True, stop=True).then_inc(r.s_t, 1)
        v.wait_ge(r.s_t, E.c(r.s_t))
        v.tensor_copy(r.bcb[:, nchunk * 512:(nchunk + 1) * 512],
                      r.ps_g[nchunk % 2][:, :])
    v.wait_ge(r.s_t, E.c(r.s_t))
    r.wk_lastuse[0] = max(r.wk_lastuse.get(0, 0), E.c(r.s_t))
    r.wk_lastuse[1] = max(r.wk_lastuse.get(1, 0), E.c(r.s_t))
    emit_gate_out(E, r, w["l0ffn_gatew"].rearrange("(c p) f -> p c f", p=128),
                  xntok)
    emit_l0ffn_outln(E, r, w, xntok)


def emit_gate_out(E, r, wap, xntok):
    t_, v, sc, g = E.tensor, E.vector, E.gpsimd if False else E.scalar, E.gpsimd
    sc = E.scalar
    g = E.gpsimd
    t0 = E.c(r.s_t)
    wv = []
    for m in range(ND):
        need = r.wk_lastuse.get(m % NWK, 0)
        if m >= NWK:
            need = max(need, t0 + (2 + NT) * (m - NWK) + 2)
        if need > 0:
            g.wait_ge(r.s_t, need)
        sem = r.s_wp[m % NWK]
        g.dma_start(out=r.wk[m % NWK][:, :, :],
                    in_=wap[:, :, m * 128:(m + 1) * 128]).then_inc(sem, 16)
        wv.append(E.c(sem))
    t_.wait_ge(r.s_s, E.c(r.s_s))
    t_.wait_ge(r.s_v, E.c(r.s_v))
    for m in range(ND):
        t_.wait_ge(r.s_wp[m % NWK], wv[m])
        for half in range(2):
            i = 2 * m + half
            pb = i % 2
            if i >= 2:
                t_.wait_ge(r.s_s, E.c(r.s_s) - NT * 0 - 1)
            for kc in range(ND):
                ins = t_.matmul(r.ps_a[pb][:, :], r.wk[m % NWK][:, kc, :],
                                r.xnT[:, kc, half * 512:(half + 1) * 512],
                                start=(kc == 0), stop=(kc == ND - 1))
                if kc == ND - 1:
                    ins.then_inc(r.s_t, 1)
            sc.wait_ge(r.s_t, E.c(r.s_t))
            sc.activation(r.ystage[:, half * 512:(half + 1) * 512],
                          r.ps_a[pb][:, :], AF.Sigmoid,
                          bias=r.l0ffn_gcol[:, m:m + 1])
        r.wk_lastuse[m % NWK] = E.c(r.s_t)
        t_.wait_ge(r.s_s, E.c(r.s_s))
        for tt in range(NT):
            pb = tt % 2
            if tt >= 2:
                t_.wait_ge(r.s_v, E.c(r.s_v) - 3)
            t_.transpose(r.ps_tr[pb][:, :], r.ystage[:, tt * 128:(tt + 1) * 128],
                         r.ident[:, :]).then_inc(r.s_t, 1)
            v.wait_ge(r.s_t, E.c(r.s_t))
            v.tensor_tensor(r.scr2_bf[:, 0:128], r.ps_tr[pb][:, :],
                            r.bcb[:, m * 128:(m + 1) * 128], ALU.mult)
            v.tensor_tensor(xntok[:, tt, m * 128:(m + 1) * 128],
                            xntok[:, tt, m * 128:(m + 1) * 128],
                            r.scr2_bf[:, 0:128], ALU.add)


def emit_l0ffn_outln(E, r, w, xntok, eps=1e-6):
    t_, v, sc = E.tensor, E.vector, E.scalar
    row = _load_row(E, r, w, 7, 0)
    v.tensor_copy(r.row_bf[0:1, :], row)
    for nchunk in range(4):
        t_.wait_ge(r.s_v, E.c(r.s_v))
        t_.matmul(r.ps_g[nchunk % 2][:, :],
                  r.ones_row[0:1, 0:128],
                  r.row_bf[0:1, nchunk * 512:(nchunk + 1) * 512],
                  start=True, stop=True).then_inc(r.s_t, 1)
        v.wait_ge(r.s_t, E.c(r.s_t))
        v.tensor_copy(r.bcb[:, nchunk * 512:(nchunk + 1) * 512],
                      r.ps_g[nchunk % 2][:, :])
    row = _load_row(E, r, w, 8, 0)
    v.tensor_copy(r.row_bf[0:1, :], row)
    t_.wait_ge(r.s_v, E.c(r.s_v))
    bbanks = [r.ps_a[0], r.ps_a[1], r.ps_g[0], r.ps_g[1]]
    for nchunk in range(4):
        t_.matmul(bbanks[nchunk][:, :], r.ones_row[0:1, 0:128],
                  r.row_bf[0:1, nchunk * 512:(nchunk + 1) * 512],
                  start=True, stop=True).then_inc(r.s_t, 1)
    s0 = E.c(r.s_s)
    v.wait_ge(r.s_t, E.c(r.s_t))
    for t in range(NT):
        v.tensor_scalar(r.scr_bf[:, :], xntok[:, t, :], 1.0, 0.0, ALU.mult,
                        ALU.add, accum_out=r.stat[:, 4 * t + 0: 4 * t + 1])
        if t == 0:
            sc.wait_ge(r.s_v, E.c(r.s_v))
        sc.activation(r.scr2_bf[:, :], xntok[:, t, :], AF.Square,
                      accum_out=r.stat[:, 4 * t + 1: 4 * t + 2])
    for t in range(NT):
        v.tensor_scalar(r.stat[:, 4 * t + 0: 4 * t + 1],
                        r.stat[:, 4 * t + 0: 4 * t + 1], -1.0 / D, None, ALU.mult)
        v.wait_ge(r.s_s, s0 + t + 1)
        v.tensor_scalar(r.stat[:, 4 * t + 1: 4 * t + 2],
                        r.stat[:, 4 * t + 1: 4 * t + 2], 1.0 / D, None, ALU.mult)
        v.tensor_tensor(r.stat[:, 4 * t + 2: 4 * t + 3],
                        r.stat[:, 4 * t + 0: 4 * t + 1],
                        r.stat[:, 4 * t + 0: 4 * t + 1], ALU.mult)
        v.tensor_tensor(r.stat[:, 4 * t + 1: 4 * t + 2],
                        r.stat[:, 4 * t + 1: 4 * t + 2],
                        r.stat[:, 4 * t + 2: 4 * t + 3], ALU.subtract)
        v.tensor_scalar(r.stat[:, 4 * t + 1: 4 * t + 2],
                        r.stat[:, 4 * t + 1: 4 * t + 2], eps, None, ALU.add)
        sc.wait_ge(r.s_v, E.c(r.s_v))
        sc.activation(r.stat[:, 4 * t + 3: 4 * t + 4],
                      r.stat[:, 4 * t + 1: 4 * t + 2], AF.Ln)
        v.wait_ge(r.s_s, E.c(r.s_s))
        v.tensor_scalar(r.stat[:, 4 * t + 3: 4 * t + 4],
                        r.stat[:, 4 * t + 3: 4 * t + 4], -0.5, None, ALU.mult)
        sc.wait_ge(r.s_v, E.c(r.s_v))
        sc.activation(r.stat[:, 4 * t + 3: 4 * t + 4],
                      r.stat[:, 4 * t + 3: 4 * t + 4], AF.Exp)
        v.wait_ge(r.s_s, E.c(r.s_s))
        v.tensor_scalar(r.scr2_bf[:, :], xntok[:, t, :],
                        r.stat[:, 4 * t + 0: 4 * t + 1],
                        r.stat[:, 4 * t + 3: 4 * t + 4], ALU.add, ALU.mult)
        v.tensor_tensor(r.scr2_bf[:, :], r.scr2_bf[:, :], r.bcb[:, :], ALU.mult)
        for nchunk in range(4):
            v.tensor_tensor(r.scr2_bf[:, nchunk * 512:(nchunk + 1) * 512],
                            r.scr2_bf[:, nchunk * 512:(nchunk + 1) * 512],
                            bbanks[nchunk][:, :], ALU.add)
        v.tensor_tensor(r.x_sb[:, t, :], r.x_sb[:, t, :], r.scr2_bf[:, :], ALU.add)



SC128 = 1.0 / math.sqrt(128.0)


def emit_l1aa(E, r, w, ex):
    t_, v, sc, g, sy = E.tensor, E.vector, E.scalar, E.gpsimd, E.sync
    attT_ap = r.attT.ap() if callable(getattr(r.attT, "ap", None)) else r.attT
    av = attT_ap.rearrange("p j (g h m) -> p g h j m", g=2, h=16, m=32)
    qw = w["l1aa_qw"].rearrange("(c p) f -> p c f", p=128)
    kw = w["l1aa_kw"].rearrange("(c p) f -> p c f", p=128)
    vw = w["l1aa_vw"].rearrange("(c p) f -> p c f", p=128)
    ow = w["l1aa_ow"].rearrange("(c p) f -> p c f", p=128)
    v.wait_ge(r.s_w, E.c(r.s_w))
    sc.wait_ge(r.s_w, E.c(r.s_w))
    sc.wait_ge(r.s_v, E.c(r.s_v))
    sc.activation(r.betcol[:, :], r.l1aa_bet_sb[:, :], AF.Sigmoid)
    v.wait_ge(r.s_s, E.c(r.s_s))
    v.tensor_scalar(r.ombet[:, :], r.betcol[:, :], -1.0, 1.0, ALU.mult, ALU.add)
    for hg in range(NHG):
        exi = ex + hg
        emit_prep_pump(E, r, 8)
        emit_projT(E, r, qw[:, :, hg * 512:(hg + 1) * 512], HG, r.gbuf, dst_off=0)
        emit_projT(E, r, kw[:, :, hg * 512:(hg + 1) * 512], HG, r.gbuf, dst_off=4)
        r.wtok_lastuse[0] = max(r.wtok_lastuse.get(0, 0), E.c(r.s_t))
        emit_proj_tok(E, r, vw[:, :, hg * 512:(hg + 1) * 512], 512, r.vtk)
        pv0 = E.c(r.s_v)
        for ch in range(8):
            if ch == 0:
                v.wait_ge(r.s_v, pv0)
                v.wait_ge(r.s_s, E.c(r.s_s))
            v.tensor_scalar(r.xstage[:, 0:1024], r.gbuf[:, ch, :], 0.0, None,
                            ALU.min)
            sc.wait_ge(r.s_v, E.c(r.s_v))
            sc.activation(r.xstage[:, 1024:2048], r.xstage[:, 0:1024], AF.Exp)
            v.tensor_scalar(r.sgbuf[:, ch, :], r.gbuf[:, ch, :], 0.0, None,
                            ALU.max)
            v.wait_ge(r.s_s, E.c(r.s_s))
            v.tensor_tensor(r.sgbuf[:, ch, :], r.sgbuf[:, ch, :],
                            r.xstage[:, 1024:2048], ALU.add)
        t_.wait_ge(r.s_v, E.c(r.s_v))
        for hl in range(HG):
            for kc in range(8):
                i = hl * 8 + kc
                pb = i % 2
                if i >= 2:
                    t_.wait_ge(r.s_v, E.c(r.s_v) - 1)
                t_.transpose(r.ps_tr[pb][:, :],
                             r.sgbuf[:, 4 + hl, kc * 128:(kc + 1) * 128],
                             r.ident[:, :]).then_inc(r.s_t, 1)
                v.wait_ge(r.s_t, E.c(r.s_t))
                v.tensor_copy(r.sktok[:, kc, hl * 128:(hl + 1) * 128],
                              r.ps_tr[pb][:, :])
        for hl in range(HG):
            h = hg * HG + hl
            for seg in range(2):
                q0 = seg * 512
                e0 = E.c(r.s_s)
                pvd = {}
                fxd = {}
                t_.wait_ge(r.s_v, E.c(r.s_v))
                t_.wait_ge(r.s_s, E.c(r.s_s))
                for kc in range(4):
                    pb = kc % 2
                    if kc >= 2:
                        t_.wait_ge(r.s_s, e0 + kc - 1)
                    t_.matmul(r.ps_a[pb][:, :],
                              r.gbuf[:, 4 + hl, q0 + kc * 128:q0 + (kc + 1) * 128],
                              r.gbuf[:, hl, q0:q0 + 512],
                              start=True, stop=True).then_inc(r.s_t, 1)
                    if kc >= 2:
                        sc.wait_ge(r.s_t, pvd[kc - 2])
                        sc.wait_ge(r.s_v, fxd[kc - 2])
                    sc.wait_ge(r.s_t, E.c(r.s_t))
                    sc.activation(r.aat[:, kc % 2, :], r.ps_a[pb][:, :],
                                  AF.Exp, scale=SC128)
                    v.wait_ge(r.s_s, E.c(r.s_s))
                    if kc > 0:
                        v.memset(r.aat[:, kc % 2, 0:kc * 128], 0.0)
                    v.tensor_tensor(r.aat[:, kc % 2, kc * 128:(kc + 1) * 128],
                                    r.aat[:, kc % 2, kc * 128:(kc + 1) * 128],
                                    r.triu[:, :], ALU.mult)
                    fxd[kc] = E.c(r.s_v)
                    t_.wait_ge(r.s_v, E.c(r.s_v))
                    t_.wait_ge(r.s_s, E.c(r.s_s))
                    t_.matmul(r.ps_d[0][0:1, :], r.ones_col[:, 0:1],
                              r.aat[:, kc % 2, :],
                              start=(kc == 0), stop=(kc == 3))
                    ins = t_.matmul(r.ps_g[0][:, :],
                                    r.vtk[:, seg * 4 + kc, hl * 128:(hl + 1) * 128],
                                    r.aat[:, kc % 2, :],
                                    start=(kc == 0), stop=(kc == 3))
                    ins.then_inc(r.s_t, 1)
                    pvd[kc] = E.c(r.s_t)
                v.wait_ge(r.s_t, E.c(r.s_t))
                v.tensor_copy(r.nrm[0:1, 0:512], r.ps_d[0][0:1, :])
                sc.wait_ge(r.s_v, E.c(r.s_v))
                sc.activation(r.nrm[0:1, 0:512], r.nrm[0:1, 0:512], AF.Ln)
                v.wait_ge(r.s_s, E.c(r.s_s))
                v.tensor_scalar(r.nrm[0:1, 0:512], r.nrm[0:1, 0:512], -1.0,
                                None, ALU.mult)
                sc.wait_ge(r.s_v, E.c(r.s_v))
                sc.activation(r.nrm[0:1, 0:512], r.nrm[0:1, 0:512], AF.Exp)
                v.wait_ge(r.s_s, E.c(r.s_s))
                v.tensor_copy(r.nrm_bf[0:1, 0:512], r.nrm[0:1, 0:512])
                t_.wait_ge(r.s_v, E.c(r.s_v))
                t_.matmul(r.ps_g[1][:, :], r.ones_row[0:1, 0:128],
                          r.nrm_bf[0:1, 0:512], start=True,
                          stop=True).then_inc(r.s_t, 1)
                v.wait_ge(r.s_t, E.c(r.s_t))
                v.tensor_copy(r.scr2_bf[:, 1024:1536], r.ps_g[1][:, :])
                v.tensor_tensor(
                    av[:, seg, h, :, :],
                    r.ps_g[0][:, :].rearrange("p (m j) -> p j m", m=32),
                    r.scr2_bf[:, 1024:1536].rearrange("p (m j) -> p j m", m=32),
                    ALU.mult)
        for ph in range(2):
            if ph == 0:
                v.wait_ge(r.s_t, E.c(r.s_t))
                v.memset(r.mem_f[:, :], 0.0)
                v.memset(r.z_f[:, 0:4], 1.0 / 128.0)
            else:
                sy.wait_ge(r.s_v, E.c(r.s_v))
                sy.dma_start(out=r.mem_f[:, :],
                             in_=r.state_sh.ap()[hg, 0, :, 0:512]).then_inc(r.s_x, 16)
                sy.dma_start(out=r.z_f[:, 0:4],
                             in_=r.state_sh.ap()[hg, 0, :, 512:516]).then_inc(r.s_x, 16)
                v.wait_ge(r.s_x, E.c(r.s_x))
            v.tensor_copy(r.mem_b[:, :], r.mem_f[:, :])
            v.tensor_copy(r.z_b[:, 0:4], r.z_f[:, 0:4])
            for hl in range(HG):
                h = hg * HG + hl
                for seg in range(2):
                    q0 = seg * 512
                    t_.wait_ge(r.s_v, E.c(r.s_v))
                    t_.matmul(r.ps_a[0][:, :], r.mem_b[:, hl * 128:(hl + 1) * 128],
                              r.sgbuf[:, hl, q0:q0 + 512],
                              start=True, stop=True)
                    t_.matmul(r.ps_d[0][0:1, :], r.z_b[:, hl:hl + 1],
                              r.sgbuf[:, hl, q0:q0 + 512],
                              start=True, stop=True).then_inc(r.s_t, 1)
                    v.wait_ge(r.s_t, E.c(r.s_t))
                    v.tensor_copy(r.nrm[0:1, 0:512], r.ps_d[0][0:1, :])
                    sc.wait_ge(r.s_v, E.c(r.s_v))
                    sc.activation(r.nrm[0:1, 0:512], r.nrm[0:1, 0:512], AF.Ln)
                    v.wait_ge(r.s_s, E.c(r.s_s))
                    v.tensor_scalar(r.nrm[0:1, 0:512], r.nrm[0:1, 0:512], -1.0,
                                    None, ALU.mult)
                    sc.wait_ge(r.s_v, E.c(r.s_v))
                    sc.activation(r.nrm[0:1, 0:512], r.nrm[0:1, 0:512], AF.Exp)
                    v.wait_ge(r.s_s, E.c(r.s_s))
                    v.tensor_copy(r.nrm_bf[0:1, 0:512], r.nrm[0:1, 0:512])
                    t_.wait_ge(r.s_v, E.c(r.s_v))
                    t_.matmul(r.ps_g[1][:, :], r.ones_row[0:1, 0:128],
                              r.nrm_bf[0:1, 0:512], start=True,
                              stop=True).then_inc(r.s_t, 1)
                    v.wait_ge(r.s_t, E.c(r.s_t))
                    v.tensor_copy(r.scr2_bf[:, 1024:1536], r.ps_g[1][:, :])
                    slot = hl * 2 + seg
                    if ph == 0:
                        v.tensor_tensor(r.amem[:, slot, :], r.ps_a[0][:, :],
                                        r.scr2_bf[:, 1024:1536], ALU.mult)
                    else:
                        v.tensor_tensor(r.scr2_bf[:, 1536:2048], r.ps_a[0][:, :],
                                        r.scr2_bf[:, 1024:1536], ALU.mult)
                        v.tensor_scalar(r.amem[:, slot, :], r.amem[:, slot, :],
                                        r.selcol[:, 0:1], None, ALU.mult)
                        v.tensor_scalar(r.scr2_bf[:, 1536:2048],
                                        r.scr2_bf[:, 1536:2048],
                                        r.selcol[:, 1:2], None, ALU.mult)
                        v.tensor_tensor(r.amem[:, slot, :], r.amem[:, slot, :],
                                        r.scr2_bf[:, 1536:2048], ALU.add)
                    for c in range(4):
                        t_.wait_ge(r.s_v, E.c(r.s_v))
                        t_.matmul(r.ps_g[0][:, 0:128],
                                  r.sgbuf[:, 4 + hl, q0 + c * 128:q0 + (c + 1) * 128],
                                  r.mem_b[:, hl * 128:(hl + 1) * 128],
                                  start=True, stop=True)
                        t_.matmul(r.ps_a[1][:, 0:1],
                                  r.sgbuf[:, 4 + hl, q0 + c * 128:q0 + (c + 1) * 128],
                                  r.z_b[:, hl:hl + 1],
                                  start=True, stop=True).then_inc(r.s_t, 1)
                        v.wait_ge(r.s_t, E.c(r.s_t))
                        v.reciprocal(r.mtmp[:, 0:1], r.ps_a[1][:, 0:1])
                        v.tensor_scalar(r.scr2_bf[:, 1536:1664], r.ps_g[0][:, 0:128],
                                        r.mtmp[:, 0:1], None, ALU.mult)
                        v.tensor_tensor(r.delta[:, c, :],
                                        r.vtk[:, seg * 4 + c, hl * 128:(hl + 1) * 128],
                                        r.scr2_bf[:, 1536:1664], ALU.subtract)
                    t_.wait_ge(r.s_v, E.c(r.s_v))
                    for c in range(4):
                        t_.matmul(r.ps_g[0][:, 0:128],
                                  r.sktok[:, seg * 4 + c, hl * 128:(hl + 1) * 128],
                                  r.delta[:, c, :],
                                  start=(c == 0), stop=(c == 3))
                        ins = t_.matmul(r.ps_a[1][:, 0:1],
                                        r.sktok[:, seg * 4 + c, hl * 128:(hl + 1) * 128],
                                        r.ones_col[:, 0:1],
                                        start=(c == 0), stop=(c == 3))
                        if c == 3:
                            ins.then_inc(r.s_t, 1)
                    v.wait_ge(r.s_t, E.c(r.s_t))
                    v.tensor_tensor(r.mem_f[:, hl * 128:(hl + 1) * 128],
                                    r.mem_f[:, hl * 128:(hl + 1) * 128],
                                    r.ps_g[0][:, 0:128], ALU.add)
                    v.tensor_tensor(r.z_f[:, hl:hl + 1], r.z_f[:, hl:hl + 1],
                                    r.ps_a[1][:, 0:1], ALU.add)
                    v.tensor_copy(r.mem_b[:, hl * 128:(hl + 1) * 128],
                                  r.mem_f[:, hl * 128:(hl + 1) * 128])
                    v.tensor_copy(r.z_b[:, hl:hl + 1], r.z_f[:, hl:hl + 1])
            if ph == 0:
                sy.wait_ge(r.s_v, E.c(r.s_v))
                emit_halved_dma(E, r, r.state_sh.ap()[hg, 0, :, 0:512],
                                r.state_sh.ap()[hg, 1, :, 0:512],
                                r.mem_f[:, :], r.mem_f[:, :], r.s_kv)
                emit_halved_dma(E, r, r.state_sh.ap()[hg, 0, :, 512:516],
                                r.state_sh.ap()[hg, 1, :, 512:516],
                                r.z_f[:, 0:4], r.z_f[:, 0:4], r.s_kv)
                emit_flag_write(E, r, exi, [(r.s_kv, E.c(r.s_kv))])
                emit_flag_poll(E, r, exi)
        for hl in range(HG):
            h = hg * HG + hl
            for seg in range(2):
                slot = hl * 2 + seg
                v.tensor_scalar(av[:, seg, h, :, :], av[:, seg, h, :, :],
                                r.ombet[:, h:h + 1], None, ALU.mult)
                v.tensor_scalar(r.scr2_bf[:, 1024:1536], r.amem[:, slot, :],
                                r.betcol[:, h:h + 1], None, ALU.mult)
                v.tensor_tensor(
                    av[:, seg, h, :, :], av[:, seg, h, :, :],
                    r.scr2_bf[:, 1024:1536].rearrange("p (m j) -> p j m", m=32),
                    ALU.add)
    r.wtok_lastuse[0] = max(r.wtok_lastuse.get(0, 0), E.c(r.s_t))

    def oproj_ep(tt, nchunk, pb):
        v.wait_ge(r.s_t, E.c(r.s_t))
        v.tensor_tensor(r.x_sb[:, tt, nchunk * 512:(nchunk + 1) * 512],
                        r.x_sb[:, tt, nchunk * 512:(nchunk + 1) * 512],
                        r.ps_a[pb][:, :], ALU.add)
    emit_proj_tok(E, r, ow, D, None, rhsT=r.attT, epilogue=oproj_ep,
                  epilogue_sem=r.s_v)



NFLAGS = 64


def emit_exch_init(E, r, meta_in):
    sy = E.sync
    sy.dma_start(out=r.meta_sb[:, :], in_=meta_in).then_inc(r.s_x, 16)
    sy.wait_ge(r.s_x, E.c(r.s_x))
    r.rg_parity = sy.alloc_register("rg_parity")
    sy.reg_load(r.rg_parity, r.meta_sb[0:1, 0:1])


def emit_flag_write(E, r, idx, wait_sems):
    sy = E.sync
    for sem, val in wait_sems:
        sy.wait_ge(sem, val)
    with sy.If_eq(r.rg_parity, 0):
        with E.nocount():
            sy.dma_start(out=r.flags_sh.ap()[2 * idx:2 * idx + 1, :],
                         in_=r.meta_sb[0:1, 2 + idx:3 + idx]).then_inc(r.s_f, 16)
    with sy.Else():
        sy.dma_start(out=r.flags_sh.ap()[2 * idx + 1:2 * idx + 2, :],
                     in_=r.meta_sb[0:1, 2 + idx:3 + idx]).then_inc(r.s_f, 16)


def emit_flag_poll(E, r, idx):
    sy = E.sync
    sy.wait_ge(r.s_f, E.c(r.s_f))
    u = f"_{idx}"
    with sy.register("tok" + u) as tok, sy.register("f0" + u) as f0, \
         sy.register("f1" + u) as f1, sy.register("cond" + u) as cond:
        sy.reg_load(tok, r.meta_sb[0:1, 2 + idx:3 + idx])
        sy.reg_mov(cond, 1)
        with sy.While(cond):
            sy.reg_load(f0, r.flags_sh.ap()[2 * idx:2 * idx + 1, :])
            sy.reg_load(f1, r.flags_sh.ap()[2 * idx + 1:2 * idx + 2, :])
            sy.reg_alu(f0, f0, tok, ALU.subtract)
            sy.reg_alu(f1, f1, tok, ALU.subtract)
            sy.reg_alu(cond, f0, f1, ALU.bitwise_or)
    sy.sem_inc(r.s_p, 1)
    E.bump(r.s_p, 1)


def emit_halved_dma(E, r, dst_my, dst_sib, src_my, src_sib, sem):
    sy = E.sync
    with sy.If_eq(r.rg_parity, 0):
        with E.nocount():
            sy.dma_start(out=dst_my, in_=src_my).then_inc(sem, 16)
    with sy.Else():
        sy.dma_start(out=dst_sib, in_=src_sib).then_inc(sem, 16)


def emit_prep_pump(E, r, k=1):
    g = E.gpsimd
    while k > 0 and r.prep_pos < len(r.prep_plan):
        dst, src = r.prep_plan[r.prep_pos]
        g.dma_start(out=dst, in_=src).then_inc(r.s_prep, 16)
        r.prep_pos += 1
        k -= 1



def emit_rope_tok(E, r, stage, ctab, stab, nheads=16, drope=32):
    v, g = E.vector, E.gpsimd
    half = drope // 2
    for t in range(NT):
        pb = t % len(r.ropeC)
        need = r.rope_lastuse.get(pb, 0)
        if need > 0:
            g.wait_ge(r.s_v, need)
        sem = r.s_wp[pb]
        g.dma_start(out=r.ropeC[pb][:, :],
                    in_=ctab.rearrange("(t p) w -> p t w", p=128)[:, t, :]
                    ).then_inc(sem, 16)
        g.dma_start(out=r.ropeS[pb][:, :],
                    in_=stab.rearrange("(t p) w -> p t w", p=128)[:, t, :]
                    ).then_inc(sem, 16)
        wv = E.c(sem)
        qv = stage[:, t, :].rearrange("p (h d) -> p h d", d=128)
        qa = qv[:, :, 0:half]
        qb_ = qv[:, :, half:drope]
        qr = qv[:, :, 0:drope]
        tmp = r.ropetmp[:, :].rearrange("p (h d) -> p h d", d=drope)
        cp = r.ropeC[pb][:, :].rearrange("p (h d) -> p h d", d=drope)
        sp = r.ropeS[pb][:, :].rearrange("p (h d) -> p h d", d=drope)
        v.wait_ge(sem, wv)
        v.tensor_scalar(tmp[:, :, 0:half], qb_, -1.0, None, ALU.mult)
        v.tensor_scalar(tmp[:, :, half:drope], qa, 1.0, None, ALU.mult)
        v.tensor_tensor(qr, qr, cp, ALU.mult)
        v.tensor_tensor(tmp[:, :, :], tmp[:, :, :], sp, ALU.mult)
        v.tensor_tensor(qr, qr, tmp[:, :, :], ALU.add)
        r.rope_lastuse[pb] = E.c(r.s_v)


def emit_l0ab(E, r, w, ex):
    t_, v, sc, g, sy = E.tensor, E.vector, E.scalar, E.gpsimd, E.sync
    qw = w["l0ab_qwT"].rearrange("(c p) f -> p c f", p=128)
    kw = w["l0ab_kwT"].rearrange("(c p) f -> p c f", p=128)
    vw = w["l0ab_vwT"].rearrange("(c p) f -> p c f", p=128)
    ow = w["l0ab_owT"].rearrange("(c p) f -> p c f", p=128)
    SC = 1.0 / math.sqrt(128.0)
    emit_proj_tok(E, r, qw, D, r.qstage, bias_row=r.l0ab_qbrow[0:1, :],
                  scale_imm=SC)
    emit_rope_tok(E, r, r.qstage, w["l0ab_ropeC"], w["l0ab_ropeS"])
    sy.wait_ge(r.s_v, E.c(r.s_v))
    sy.dma_start(out=r.qscr.ap().rearrange("(t p) d -> p t d", p=128),
                 in_=r.qstage[:, :, :]).then_inc(r.s_x, 16)
    qscr_done = E.c(r.s_x)
    emit_proj_tok(E, r, kw, D, r.qstage, bias_row=r.l0ab_kbrow[0:1, :],
                  pre_evac_wait=[(r.s_x, qscr_done)])
    emit_rope_tok(E, r, r.qstage, w["l0ab_ropeC"], w["l0ab_ropeS"])
    sy.wait_ge(r.s_v, E.c(r.s_v))
    emit_halved_dma(E, r,
                    r.ksh.ap().rearrange("(t p) d -> p t d", p=128)[:, 0:NT, :],
                    r.ksh.ap().rearrange("(t p) d -> p t d", p=128)[:, NT:2 * NT, :],
                    r.qstage[:, :, :], r.qstage[:, :, :], r.s_kv)
    emit_proj_tok(E, r, vw, D, r.qstage, bias_row=r.l0ab_vbrow[0:1, :],
                  pre_evac_wait=[(r.s_kv, E.c(r.s_kv))])
    sy.wait_ge(r.s_v, E.c(r.s_v))
    for h in range(NH1B):
        vsh_h0 = r.vsh.ap()[h, 0, :, :].rearrange("(t p) d -> p t d", p=128)
        vsh_h1 = r.vsh.ap()[h, 1, :, :].rearrange("(t p) d -> p t d", p=128)
        emit_halved_dma(E, r, vsh_h0, vsh_h1,
                        r.qstage[:, :, h * 128:(h + 1) * 128],
                        r.qstage[:, :, h * 128:(h + 1) * 128], r.s_kv)
    emit_flag_write(E, r, ex, [(r.s_kv, E.c(r.s_kv))])
    emit_flag_poll(E, r, ex)
    sy.wait_ge(r.s_x, qscr_done)
    sy.wait_ge(r.s_t, E.c(r.s_t))
    for c in range(ND):
        sy.dma_start(out=r.qT[:, c, :], in_=r.qscr.ap()[:, c * 128:(c + 1) * 128],
                     transpose=True).then_inc(r.s_x, 16)
    qT_done = E.c(r.s_x)
    pol = E.c(r.s_p)
    for h in range(NH1B):
        emit_prep_pump(E, r, 3)
        hb = h % 2
        if h >= 2:
            sy.wait_ge(r.s_t, r.kv_lastuse.get(hb, 0))
        ksh_m = r.ksh.ap()[:, h * 128:(h + 1) * 128]
        with sy.If_eq(r.rg_parity, 0):
            with E.nocount():
                sy.dma_start(out=r.kt[hb][:, 0:1024], in_=ksh_m[0:1024, :],
                             transpose=True).then_inc(r.s_kv, 16)
                sy.dma_start(out=r.kt[hb][:, 1024:2048], in_=ksh_m[1024:2048, :],
                             transpose=True).then_inc(r.s_kv, 16)
                sy.dma_start(out=r.vt[hb][:, 0:8, :],
                             in_=r.vsh.ap()[h, 0, :, :].rearrange(
                                 "(c p) d -> p c d", p=128)).then_inc(r.s_kv, 16)
                sy.dma_start(out=r.vt[hb][:, 8:16, :],
                             in_=r.vsh.ap()[h, 1, :, :].rearrange(
                                 "(c p) d -> p c d", p=128)).then_inc(r.s_kv, 16)
        with sy.Else():
            sy.dma_start(out=r.kt[hb][:, 0:1024], in_=ksh_m[1024:2048, :],
                         transpose=True).then_inc(r.s_kv, 16)
            sy.dma_start(out=r.kt[hb][:, 1024:2048], in_=ksh_m[0:1024, :],
                         transpose=True).then_inc(r.s_kv, 16)
            sy.dma_start(out=r.vt[hb][:, 0:8, :],
                         in_=r.vsh.ap()[h, 1, :, :].rearrange(
                             "(c p) d -> p c d", p=128)).then_inc(r.s_kv, 16)
            sy.dma_start(out=r.vt[hb][:, 8:16, :],
                         in_=r.vsh.ap()[h, 0, :, :].rearrange(
                             "(c p) d -> p c d", p=128)).then_inc(r.s_kv, 16)
        kvh = E.c(r.s_kv)
        t_.wait_ge(r.s_kv, kvh)
        t_.wait_ge(r.s_v, E.c(r.s_v))
        if h == 0:
            t_.wait_ge(r.s_x, qT_done)
        e0 = E.c(r.s_s)
        pv_done = {}
        fix_done = {}
        for kc in range(ND):
            for half in range(2):
                i = kc * 2 + half
                pb = i % 2
                if i >= 2:
                    t_.wait_ge(r.s_s, e0 + i - 1)
                ins = t_.matmul(r.ps_a[pb][:, :],
                                r.kt[hb][:, kc * 128:(kc + 1) * 128],
                                r.qT[:, h, half * 512:(half + 1) * 512],
                                start=True, stop=True)
                ins.then_inc(r.s_t, 1)
                if kc >= 2:
                    sc.wait_ge(r.s_t, pv_done[kc - 2])
                    if kc - 2 in fix_done:
                        sc.wait_ge(r.s_v, fix_done[kc - 2])
                sc.wait_ge(r.s_t, E.c(r.s_t))
                sc.activation(r.at_sb[:, kc % 2, half * 512:(half + 1) * 512],
                              r.ps_a[pb][:, :], AF.Exp,
                              bias=r.l0ab_mb[:, kc:kc + 1])
            if kc < 8:
                v.wait_ge(r.s_s, E.c(r.s_s))
                if kc > 0:
                    v.memset(r.at_sb[:, kc % 2, 0:kc * 128], 0.0)
                v.tensor_tensor(r.at_sb[:, kc % 2, kc * 128:(kc + 1) * 128],
                                r.at_sb[:, kc % 2, kc * 128:(kc + 1) * 128],
                                r.triu[:, :], ALU.mult)
                fix_done[kc] = E.c(r.s_v)
            t_.wait_ge(r.s_v, E.c(r.s_v))
            t_.wait_ge(r.s_s, E.c(r.s_s))
            for half in range(2):
                t_.matmul(r.ps_d[half][0:1, :], r.ones_col[:, 0:1],
                          r.at_sb[:, kc % 2, half * 512:(half + 1) * 512],
                          start=(kc == 0), stop=(kc == ND - 1))
                ins = t_.matmul(r.ps_g[half][:, :],
                                r.vt[hb][:, kc, :],
                                r.at_sb[:, kc % 2, half * 512:(half + 1) * 512],
                                start=(kc == 0), stop=(kc == ND - 1))
                ins.then_inc(r.s_t, 1)
            pv_done[kc] = E.c(r.s_t)
        r.kv_lastuse[hb] = E.c(r.s_t)
        v.wait_ge(r.s_t, E.c(r.s_t))
        v.tensor_copy(r.nrm[0:1, 0:512], r.ps_d[0][0:1, :])
        sc.wait_ge(r.s_v, E.c(r.s_v))
        sc.activation(r.nrm[0:1, 0:512], r.nrm[0:1, 0:512], AF.Ln)
        v.wait_ge(r.s_s, E.c(r.s_s))
        v.tensor_scalar(r.nrm[0:1, 0:512], r.nrm[0:1, 0:512], -1.0, None, ALU.mult)
        sc.wait_ge(r.s_v, E.c(r.s_v))
        sc.activation(r.nrm[0:1, 0:512], r.nrm[0:1, 0:512], AF.Exp)
        v.wait_ge(r.s_s, E.c(r.s_s))
        v.tensor_copy(r.nrm_bf[0:1, 0:512], r.nrm[0:1, 0:512])
        v.tensor_copy(r.nrm[0:1, 0:512], r.ps_d[1][0:1, :])
        sc.wait_ge(r.s_v, E.c(r.s_v))
        sc.activation(r.nrm[0:1, 0:512], r.nrm[0:1, 0:512], AF.Ln)
        v.wait_ge(r.s_s, E.c(r.s_s))
        v.tensor_scalar(r.nrm[0:1, 0:512], r.nrm[0:1, 0:512], -1.0, None, ALU.mult)
        sc.wait_ge(r.s_v, E.c(r.s_v))
        sc.activation(r.nrm[0:1, 0:512], r.nrm[0:1, 0:512], AF.Exp)
        v.wait_ge(r.s_s, E.c(r.s_s))
        v.tensor_copy(r.nrm_bf2[0:1, 0:512], r.nrm[0:1, 0:512])
        t_.wait_ge(r.s_v, E.c(r.s_v))
        t_.matmul(r.ps_a[0][:, :], r.ones_row[0:1, 0:128],
                  r.nrm_bf[0:1, 0:512], start=True, stop=True)
        t_.matmul(r.ps_a[1][:, :], r.ones_row[0:1, 0:128],
                  r.nrm_bf2[0:1, 0:512], start=True, stop=True).then_inc(r.s_t, 1)
        v.wait_ge(r.s_t, E.c(r.s_t))
        for half in range(2):
            v.tensor_copy(r.bcbuf[:, half * 512:(half + 1) * 512],
                          r.ps_a[half][:, :])
        for half in range(2):
            v.tensor_tensor(r.attT[:, h, half * 512:(half + 1) * 512],
                            r.ps_g[half][:, :],
                            r.bcbuf[:, half * 512:(half + 1) * 512], ALU.mult)
    emit_oproj_add(E, r, ow, r.l0ab_ocol, r.attT)



def _shard_x(x):
    out = []
    for c in range(NCORES):
        b, h = c // 2, c % 2
        out.append(np.ascontiguousarray(x[b, h * TOK:(h + 1) * TOK, :], dtype=np.float32))
    return out


def _ident128():
    return np.eye(128, dtype=np.float32)


def _sel8():
    m = np.zeros((8, 8 * 128), dtype=np.float32)
    for e in range(8):
        m[e, e * 128:(e + 1) * 128] = 1.0
    return m


def l0ab_host_weights(inputs, core):
    half = core % 2
    Wqkv = inputs["L0_attn_b_W_qkv"]
    bqkv = inputs["L0_attn_b_b_qkv"]
    hh = np.arange(16)
    qi = (hh[:, None] * 384 + np.arange(128)[None, :]).ravel()
    ki = qi + 128
    vi = qi + 256
    t = (np.arange(TOK) + half * TOK).astype(np.float32)
    theta = 1.0 / (10000.0 ** (np.arange(0, 32, 2, dtype=np.float32) / 32.0))
    idx = t[:, None] * theta[None, :]
    idx2 = np.concatenate([idx, idx], axis=1)
    cosw = np.tile(np.cos(idx2), (1, 16)).astype(np.float32)
    sinw = np.tile(np.sin(idx2), (1, 16)).astype(np.float32)
    mb = np.zeros((128, 16), np.float32)
    if half == 0:
        mb[:, 8:] = -30000.0
    triu = (np.arange(128)[:, None] <= np.arange(128)[None, :]).astype(np.float32)
    ocol = inputs["L0_attn_b_b_out"].reshape(16, 128).T
    return {
        "l0ab_qwT": np.ascontiguousarray(Wqkv[qi, :].T),
        "l0ab_kwT": np.ascontiguousarray(Wqkv[ki, :].T),
        "l0ab_vwT": np.ascontiguousarray(Wqkv[vi, :].T),
        "l0ab_owT": np.ascontiguousarray(inputs["L0_attn_b_W_out"].T),
        "l0ab_qb": np.ascontiguousarray(bqkv[qi][None, :]),
        "l0ab_kb": np.ascontiguousarray(bqkv[ki][None, :]),
        "l0ab_vb": np.ascontiguousarray(bqkv[vi][None, :]),
        "l0ab_ocol_in": np.ascontiguousarray(ocol),
        "l0ab_ropeC": cosw, "l0ab_ropeS": sinw,
        "l0ab_mb_in": mb, "triu_in": triu,
    }


def l0aa_host_weights(inputs, core):
    half = core % 2
    t = (np.arange(TOK) + half * TOK).astype(np.float64)
    invf = 1.0 / (10000.0 ** (np.arange(0, 128, 2).astype(np.float64) / 128.0))
    ang = t[:, None] * invf[None, :]
    cosI = np.repeat(np.cos(ang), 2, axis=1)
    sinI = np.repeat(np.sin(ang), 2, axis=1)
    cosW = np.tile(cosI, (1, 16)).astype(np.float32)
    sinW = np.tile(sinI, (1, 16)).astype(np.float32)
    mb = np.zeros((128, 16), np.float32)
    if half == 0:
        mb[:, 8:] = -30000.0
    triu = (np.arange(128)[:, None] <= np.arange(128)[None, :]).astype(np.float32)
    return {
        "l0aa_qwT": np.ascontiguousarray(inputs["L0_attn_a_Wq"].T),
        "l0aa_kwT": np.ascontiguousarray(inputs["L0_attn_a_Wk"].T),
        "l0aa_vwT": np.ascontiguousarray(inputs["L0_attn_a_Wv"].T),
        "l0aa_owT": np.ascontiguousarray(inputs["L0_attn_a_Wo"].T),
        "l0aa_qAt": np.ascontiguousarray(inputs["L0_attn_a_qA"].T),
        "l0aa_kAt": np.ascontiguousarray(inputs["L0_attn_a_kA"].T),
        "l0aa_vAt": np.ascontiguousarray(inputs["L0_attn_a_vA"].T),
        "l0aa_oAt": np.ascontiguousarray(inputs["L0_attn_a_oA"].T),
        "l0aa_qBt": np.ascontiguousarray(inputs["L0_attn_a_qB"].T),
        "l0aa_kBt": np.ascontiguousarray(inputs["L0_attn_a_kB"].T),
        "l0aa_vBt": np.ascontiguousarray(inputs["L0_attn_a_vB"].T),
        "l0aa_oBt": np.ascontiguousarray(inputs["L0_attn_a_oB"].T),
        "l0aa_ropeCw": cosW, "l0aa_ropeSw": sinW,
        "l0ab_mb_in": mb, "triu_in": triu,
    }


def l0ffn_host_weights(inputs, core):
    half = core % 2
    d1 = inputs["L0_ffn_d1_w"]
    d2 = inputs["L0_ffn_d2_w"]
    rows = np.concatenate([
        inputs["L0_ffn_in_gamma"], inputs["L0_ffn_in_beta"],
        inputs["L0_ffn_agg_b"], inputs["L0_ffn_pn_gamma"],
        inputs["L0_ffn_pn_beta"], inputs["L0_ffn_d2_b"],
        inputs["L0_ffn_bc_b"], inputs["L0_ffn_out_gamma"],
        inputs["L0_ffn_out_beta"]])[None, :]
    return {
        "l0ffn_aggw": np.ascontiguousarray(inputs["L0_ffn_agg_w"]),
        "l0ffn_bcw": np.ascontiguousarray(inputs["L0_ffn_bc_w"]),
        "l0ffn_gatew": np.ascontiguousarray(inputs["L0_ffn_gate_w"]),
        "l0ffn_d1h": np.ascontiguousarray(d1[:, half * 4096:(half + 1) * 4096]),
        "l0ffn_d2h": np.ascontiguousarray(d2[half * 4096:(half + 1) * 4096, :]),
        "l0ffn_b1h_in": np.ascontiguousarray(
            inputs["L0_ffn_d1_b"][half * 4096:(half + 1) * 4096][None, :]),
        "l0ffn_rows_in": np.ascontiguousarray(rows),
        "l0ffn_gb": np.ascontiguousarray(inputs["L0_ffn_gate_b"].reshape(16, 128).T),
    }


def moe_host_weights(inputs):
    return {
        "wsgT": np.ascontiguousarray(inputs["L1_ffn_Wsg"].T),
        "wsuT": np.ascontiguousarray(inputs["L1_ffn_Wsu"].T),
        "wsdT": np.ascontiguousarray(inputs["L1_ffn_Wsd"].T),
        "wrT": np.ascontiguousarray(inputs["L1_ffn_Wr"].T),
        "wgT": np.ascontiguousarray(inputs["L1_ffn_Wg"].transpose(0, 2, 1)),
        "wuT": np.ascontiguousarray(inputs["L1_ffn_Wu"].transpose(0, 2, 1)),
        "wdT": np.ascontiguousarray(inputs["L1_ffn_Wd"].transpose(0, 2, 1)),
    }



class Res:
    pass


MOE_W_NAMES = ["wsgT", "wsuT", "wsdT", "wrT", "wgT", "wuT", "wdT"]


L1AB_W = ["l1ab_qwT", "l1ab_kwT", "l1ab_vwT", "l1ab_ogwT", "l1ab_owT",
          "l1ab_bwT", "l1ab_cols_in", "l1ab_vbrow_in", "l1ab_bb", "decayT_in"]


def l1ab_host_weights(inputs):
    qw = inputs["L1_attn_b_q_w"]
    kw = inputs["L1_attn_b_k_w"]
    vw = inputs["L1_attn_b_v_w"]
    ogw = inputs["L1_attn_b_og_w"]
    ow = inputs["L1_attn_b_o_w"]
    bw = inputs["L1_attn_b_b_w"]
    j = np.arange(64)[:, None].astype(np.float32)
    i = np.arange(64)[None, :].astype(np.float32)
    decay = np.where(j <= i, 1.0, 0.0).astype(np.float32)
    return {
        "l1ab_qwT": np.ascontiguousarray(qw.T),
        "l1ab_kwT": np.ascontiguousarray(kw.T),
        "l1ab_vwT": np.ascontiguousarray(vw.T),
        "l1ab_ogwT": np.ascontiguousarray(ogw.T),
        "l1ab_owT": np.ascontiguousarray(ow.T),
        "l1ab_bwT": np.ascontiguousarray(bw.T),
        "l1ab_cols_in": np.ascontiguousarray(np.concatenate([
            inputs["L1_attn_b_q_b"].reshape(16, 128).T,
            inputs["L1_attn_b_k_b"].reshape(16, 128).T,
            inputs["L1_attn_b_o_b"].reshape(16, 128).T], axis=1)),
        "l1ab_vbrow_in": np.ascontiguousarray(inputs["L1_attn_b_v_b"][None, :]),
        "l1ab_bb": np.ascontiguousarray(inputs["L1_attn_b_b_b"][None, :]),
        "decayT_in": decay,
    }


def l1aa_host_weights(inputs, core):
    par = core % 2
    bet = np.ascontiguousarray(
        inputs["L1_attn_a_betas"].reshape(16, 128).T.astype(np.float32))
    sel = np.zeros((128, 2), np.float32)
    sel[:, 0] = 1.0 if par == 0 else 0.0
    sel[:, 1] = 1.0 - sel[0, 0]
    return {
        "l1aa_qw": np.ascontiguousarray(inputs["L1_attn_a_Wq"]),
        "l1aa_kw": np.ascontiguousarray(inputs["L1_attn_a_Wk"]),
        "l1aa_vw": np.ascontiguousarray(inputs["L1_attn_a_Wv"]),
        "l1aa_ow": np.ascontiguousarray(inputs["L1_attn_a_Wo"]),
        "l1aa_bet_in": bet,
        "l1aa_sel_in": sel,
        "triu_in": np.triu(np.ones((128, 128), np.float32)),
    }


def build_core(stages):
    import contextlib
    nc = bass.Bass("TRN2", target_bir_lowering=False, debug=False,
                   detect_race_conditions=False)
    r = Res()
    x_in = nc.dram_tensor("x_in", [TOK, D], F32, kind="ExternalInput").ap()
    ident = nc.dram_tensor("ident_in", [128, 128], F32, kind="ExternalInput").ap()
    x_out = nc.dram_tensor("x_out", [TOK, D], F32, kind="ExternalOutput").ap()
    w = {}
    if "moe" in stages:
        w["sel8"] = nc.dram_tensor("sel8", [8, 8 * 128], F32, kind="ExternalInput").ap()
        w["wsgT"] = nc.dram_tensor("wsgT", [D, FS], F32, kind="ExternalInput").ap()
        w["wsuT"] = nc.dram_tensor("wsuT", [D, FS], F32, kind="ExternalInput").ap()
        w["wsdT"] = nc.dram_tensor("wsdT", [FS, D], F32, kind="ExternalInput").ap()
        w["wrT"] = nc.dram_tensor("wrT", [D, 8], F32, kind="ExternalInput").ap()
        w["wgT"] = nc.dram_tensor("wgT", [NEXP, D, FE], F32, kind="ExternalInput").ap()
        w["wuT"] = nc.dram_tensor("wuT", [NEXP, D, FE], F32, kind="ExternalInput").ap()
        w["wdT"] = nc.dram_tensor("wdT", [NEXP, FE, D], F32, kind="ExternalInput").ap()
    if "l0aa" in stages:
        w["l0aa_qwT"] = nc.dram_tensor("l0aa_qwT", [D, D], F32, kind="ExternalInput").ap()
        w["l0aa_kwT"] = nc.dram_tensor("l0aa_kwT", [D, 1024], F32, kind="ExternalInput").ap()
        w["l0aa_vwT"] = nc.dram_tensor("l0aa_vwT", [D, 1024], F32, kind="ExternalInput").ap()
        w["l0aa_owT"] = nc.dram_tensor("l0aa_owT", [D, D], F32, kind="ExternalInput").ap()
        for nm, sh in [("l0aa_qAt", [D, 8]), ("l0aa_kAt", [D, 8]), ("l0aa_vAt", [D, 8]),
                       ("l0aa_oAt", [D, 8]), ("l0aa_qBt", [8, D]), ("l0aa_kBt", [8, 1024]),
                       ("l0aa_vBt", [8, 1024]), ("l0aa_oBt", [8, D]),
                       ("l0aa_ropeCw", [TOK, D]), ("l0aa_ropeSw", [TOK, D])]:
            w[nm] = nc.dram_tensor(nm, sh, F32, kind="ExternalInput").ap()
    if ("l0ab" in stages) or ("l0aa" in stages):
        w["l0ab_mb_in"] = nc.dram_tensor("l0ab_mb_in", [128, 16], F32, kind="ExternalInput").ap()
    if ("l0ab" in stages) or ("l0aa" in stages) or ("l1aa" in stages):
        w["triu_in"] = nc.dram_tensor("triu_in", [128, 128], F32, kind="ExternalInput").ap()
    if "l1aa" in stages:
        for nm in ["l1aa_qw", "l1aa_kw", "l1aa_vw", "l1aa_ow"]:
            w[nm] = nc.dram_tensor(nm, [D, D], F32, kind="ExternalInput").ap()
        w["l1aa_bet_in"] = nc.dram_tensor("l1aa_bet_in", [128, 16], F32, kind="ExternalInput").ap()
        w["l1aa_sel_in"] = nc.dram_tensor("l1aa_sel_in", [128, 2], F32, kind="ExternalInput").ap()
        r.state_sh = nc.dram_tensor("state_sh", [NHG, 2, 128, 516], F32, addr_space="Shared")
    if "l0ab" in stages:
        for nm in ["l0ab_qwT", "l0ab_kwT", "l0ab_vwT", "l0ab_owT"]:
            w[nm] = nc.dram_tensor(nm, [D, D], F32, kind="ExternalInput").ap()
        for nm in ["l0ab_qb", "l0ab_kb", "l0ab_vb"]:
            w[nm] = nc.dram_tensor(nm, [1, D], F32, kind="ExternalInput").ap()
        w["l0ab_ocol_in"] = nc.dram_tensor("l0ab_ocol_in", [128, 16], F32, kind="ExternalInput").ap()
        w["l0ab_ropeC"] = nc.dram_tensor("l0ab_ropeC", [TOK, 512], F32, kind="ExternalInput").ap()
        w["l0ab_ropeS"] = nc.dram_tensor("l0ab_ropeS", [TOK, 512], F32, kind="ExternalInput").ap()
    if "l1ab" in stages:
        for nm in ["l1ab_qwT", "l1ab_kwT", "l1ab_vwT", "l1ab_ogwT", "l1ab_owT"]:
            w[nm] = nc.dram_tensor(nm, [D, D], F32, kind="ExternalInput").ap()
        w["l1ab_bwT"] = nc.dram_tensor("l1ab_bwT", [D, 16], F32, kind="ExternalInput").ap()
        w["l1ab_cols_in"] = nc.dram_tensor("l1ab_cols_in", [128, 48], F32, kind="ExternalInput").ap()
        w["l1ab_vbrow_in"] = nc.dram_tensor("l1ab_vbrow_in", [1, D], F32, kind="ExternalInput").ap()
        w["l1ab_bb"] = nc.dram_tensor("l1ab_bb", [1, 16], F32, kind="ExternalInput").ap()
        w["decayT_in"] = nc.dram_tensor("decayT_in", [64, 64], F32, kind="ExternalInput").ap()

    need_exch = any(st_ in ("l0ab", "l0aa", "l0ffn", "l1aa") for st_ in stages)
    if need_exch:
        meta_in = nc.dram_tensor("meta_in", [1, 2 + NFLAGS], I32, kind="ExternalInput").ap()
        r.flags_sh = nc.dram_tensor("flags_sh", [2 * NFLAGS, 1], I32, addr_space="Shared")
    need_attn = ("l0ab" in stages) or ("l0aa" in stages)
    if need_attn:
        r.qscr = nc.dram_tensor("qscr", [TOK, D], BF16)
    if "l0ab" in stages:
        r.ksh = nc.dram_tensor("ksh_l0ab", [S, D], BF16, addr_space="Shared")
        r.vsh = nc.dram_tensor("vsh_l0ab", [16, 2, TOK, 128], BF16, addr_space="Shared")
    if "l0aa" in stages:
        r.ksh_aa = nc.dram_tensor("ksh_l0aa", [S, 1024], BF16, addr_space="Shared")
        r.vsh_aa = nc.dram_tensor("vsh_l0aa", [8, 2, TOK, 128], BF16, addr_space="Shared")
    if "l0ffn" in stages:
        pass
        r.rowscr = nc.dram_tensor("rowscr", [1, D], BF16)
        r.lastrow_sh = nc.dram_tensor("lastrow_sh", [2, D], BF16, addr_space="Shared")
        r.delta_sh = nc.dram_tensor("delta_sh", [8, D], F32, addr_space="Shared")
        w["l0ffn_aggw"] = nc.dram_tensor("l0ffn_aggw", [D, D], F32, kind="ExternalInput").ap()
        w["l0ffn_bcw"] = nc.dram_tensor("l0ffn_bcw", [D, D], F32, kind="ExternalInput").ap()
        w["l0ffn_gatew"] = nc.dram_tensor("l0ffn_gatew", [D, D], F32, kind="ExternalInput").ap()
        w["l0ffn_d1h"] = nc.dram_tensor("l0ffn_d1h", [D, 4096], F32, kind="ExternalInput").ap()
        w["l0ffn_d2h"] = nc.dram_tensor("l0ffn_d2h", [4096, D], F32, kind="ExternalInput").ap()
        w["l0ffn_b1h_in"] = nc.dram_tensor("l0ffn_b1h_in", [1, 4096], F32, kind="ExternalInput").ap()
        w["l0ffn_rows_in"] = nc.dram_tensor("l0ffn_rows_in", [1, 9 * D], F32, kind="ExternalInput").ap()
        w["l0ffn_gb"] = nc.dram_tensor("l0ffn_gb", [128, 16], F32, kind="ExternalInput").ap()
    prep_plan_spec = []
    if "l0ffn" in stages:
        prep_plan_spec += [("l0ffn", "l0ffn_d1h", None), ("l0ffn", "l0ffn_d2h", None),
                           ("l0ffn", "l0ffn_aggw", None), ("l0ffn", "l0ffn_bcw", None),
                           ("l0ffn", "l0ffn_gatew", None)]
    if "l0ab" in stages:
        prep_plan_spec += [("l0ab", k2, None) for k2 in
                           ["l0ab_qwT", "l0ab_kwT", "l0ab_vwT", "l0ab_owT",
                            "l0ab_ropeC", "l0ab_ropeS"]]
    if "l1aa" in stages:
        prep_plan_spec += [("l1aa", k2, None) for k2 in
                           ["l1aa_qw", "l1aa_kw", "l1aa_vw", "l1aa_ow"]]
    if "moe" in stages:
        prep_plan_spec += [("moe", k2, None) for k2 in
                           ["wsgT", "wsuT", "wsdT", "wgT", "wuT", "wdT"]]
    if "l1ab" in stages:
        prep_plan_spec += [("l1ab", k2, None) for k2 in
                           ["l1ab_qwT", "l1ab_kwT", "l1ab_vwT", "l1ab_ogwT",
                            "l1ab_owT"]]
    prep_plan = []
    prep_marks = {}
    prep_caches = {}
    for pstage, pkey, _ in prep_plan_spec:
        src = w[pkey]
        shp = list(src.shape)
        cache = nc.dram_tensor(pkey + "_bc", shp, BF16).ap()
        if len(shp) == 2:
            rows, cols = shp
            rpc = max(1, (1 << 20) // (cols * 2))
            for r0 in range(0, rows, rpc):
                r1 = min(rows, r0 + rpc)
                prep_plan.append((cache[r0:r1, :], src[r0:r1, :]))
        else:
            ne, rows, cols = shp
            rpc = max(1, (1 << 20) // (cols * 2))
            for e_ in range(ne):
                for r0 in range(0, rows, rpc):
                    r1 = min(rows, r0 + rpc)
                    prep_plan.append((cache[e_, r0:r1, :], src[e_, r0:r1, :]))
        prep_caches[pkey] = cache
        if pkey not in ("l0ffn_d1h", "l0ffn_d2h"):
            w[pkey] = cache
        prep_marks[pstage] = len(prep_plan)
    if "l0ffn" in stages:
        r.d1c = prep_caches["l0ffn_d1h"]
        r.d2c = prep_caches["l0ffn_d2h"]

    es = contextlib.ExitStack()
    with es:
        def sb(name, shape, dt):
            h = es.enter_context(nc.sbuf_tensor(name, shape, dt))
            setattr(r, name, h)
            return h

        def ps(name, shape, dt=F32):
            h = es.enter_context(nc.psum_tensor(name, shape, dt))
            return h

        sb("x_sb", [128, NT, D], F32)
        sb("xnT", [128, ND, TOK], BF16)
        sb("xstage", [128, D], BF16)
        sb("scr2_bf", [128, D], BF16)
        sb("stat", [128, 4 * NT], F32)
        sb("ident_sb", [128, 128], BF16)
        sb("ones_row", [1, 512], BF16)
        sb("ones_col", [128, 1], BF16)
        sb("mtmp", [128, 32], F32)
        sb("big1", [128, 16384], BF16)
        if need_exch:
            sb("meta_sb", [1, 2 + NFLAGS], I32)
        if "triu_in" in w:
            sb("triu", [128, 128], BF16)
        if need_attn:
            sb("l0ab_mb", [128, 16], F32)
        if "l1aa" in stages:
            sb("csc", [128, 1], F32)
            nc.const_aps.register(F32, SC128, r.csc.ap()[:, 0:1])
        r.ident = r.ident_sb
        r.scr_bf = r.xstage
        r.ystage = r.xstage
        r.qT = r.xnT
        r.qstage = r.big1.ap().rearrange("p (t d) -> p t d", d=D)
        r.kstage = r.qstage
        r.attT = r.big1.ap().rearrange("p (c t) -> p c t", t=TOK)
        r.ps_tr = [ps("ps_tr0", [128, 128], BF16), ps("ps_tr1", [128, 128], BF16)]
        r.ps_a = [ps("ps_a0", [128, 512], F32), ps("ps_a1", [128, 512], F32)]
        r.ps_g = [ps("ps_g0", [128, 512], F32), ps("ps_g1", [128, 512], F32)]
        r.ps_d = [ps("ps_d0", [8, 512], F32), ps("ps_d1", [8, 512], F32)]
        r.ps_s = r.ps_a
        r.ps_o = r.ps_g
        r.ps_w = r.ps_tr[0]

        sb("ARN", [128, ARN_BYTES // 2], BF16)
        arn_bf = r.ARN.ap()
        arn_f32 = arn_bf.bitcast(F32)

        def AV(off, shape, dt=BF16):
            base = arn_f32 if dt == F32 else arn_bf
            esz = 4 if dt == F32 else 2
            assert off % esz == 0
            n = 1
            for d_ in shape[1:]:
                n *= d_
            v = base[0:shape[0], off // esz:off // esz + n]
            if len(shape) == 3:
                v = v.rearrange("p (a b) -> p a b", b=shape[2])
            return v

        def _views(st):
            d = {}
            if st == "moe":
                d["gbuf"] = AV(0, [128, NFE, TOK])
                d["wd"] = [AV(22528, [128, ND, 512]), AV(53760, [128, ND, 512])]
                d["wtok"] = d["wd"]
                d["wk"] = [AV(38912, [128, ND, 128]), AV(43008, [128, ND, 128])]
                d["wrT_sb"] = AV(47104, [128, ND, 8])
                d["wT8"] = AV(47360, [8, TOK])
                d["sel8_sb"] = AV(49408, [8, 8 * 128])
                d["gatebc"] = AV(51456, [128, TOK])
                d["wsm"] = AV(53504, [128, 8], F32)
                d["mstat"] = AV(53568, [128, 8], F32)
                d["wfb"] = AV(53632, [128, NT, 8])
                d["ubuf"] = r.big1.ap()[:, 0:NFE * TOK].rearrange(
                    "p (a b) -> p a b", b=TOK)
            elif st == "l1ab":
                d["gbuf"] = AV(0, [128, 8, TOK])
                d["vtok"] = AV(16384, [64, 2 * NT, 512])
                d["wd"] = [AV(32768, [128, ND, 512])]
                d["wtok"] = d["wd"]
                d["wk"] = [AV(49152, [128, ND, 128]), AV(53248, [128, ND, 128])]
                d["betab"] = AV(57344, [64, 2 * NT, 16], F32)
                d["bwT_sb"] = AV(58368, [128, ND, 16])
                d["bb16"] = AV(58880, [1, 16])
                d["l1ab_cols"] = AV(58944, [128, 48], F32)
                d["l1ab_vbrow"] = AV(59200, [1, D])
                d["decayT"] = AV(63296, [64, 64])
                d["attn64"] = AV(63424, [64, 64])
                d["nrm"] = AV(63552, [1, 512], F32)
                d["nrm_bf"] = AV(65600, [1, 512])
            elif st == "l1aa":
                d["gbuf"] = AV(0, [128, 8, TOK])
                d["amem"] = AV(0, [128, 16, 512])
                d["sgbuf"] = AV(16384, [128, 8, TOK])
                d["wtok"] = [AV(16384, [128, ND, 512])]
                d["vtk"] = AV(32768, [128, NT, 512])
                d["sktok"] = AV(40960, [128, NT, 512])
                d["wk"] = [AV(49152, [128, ND, 128]), AV(53248, [128, ND, 128])]
                d["mem_f"] = AV(57344, [128, 512], F32)
                d["mem_b"] = AV(59392, [128, 512])
                d["z_f"] = AV(60416, [128, 4], F32)
                d["z_b"] = AV(60480, [128, 4])
                d["delta"] = AV(60544, [128, 4, 128])
                d["betcol"] = AV(61568, [128, 16], F32)
                d["ombet"] = AV(61632, [128, 16], F32)
                d["l1aa_bet_sb"] = AV(61696, [128, 16], F32)
                d["selcol"] = AV(61760, [128, 2], F32)
                d["nrm"] = AV(61824, [1, 512], F32)
                d["nrm_bf"] = AV(63872, [1, 512])
                d["aat"] = r.scr2_bf.ap()[:, 0:1024].rearrange(
                    "p (c t) -> p c t", t=512)
            elif st in ("l0ab", "l0aa"):
                d["kt"] = [AV(0, [128, S]), AV(4096, [128, S])]
                d["vt"] = [AV(8192, [128, ND, 128]), AV(12288, [128, ND, 128])]
                d["at_sb"] = AV(16384, [128, 2, TOK])
                d["bcbuf"] = AV(20480, [128, TOK])
                d["nrm_bf2"] = AV(22528, [1, 512])
                d["wtok"] = [AV(23552, [128, ND, 512])]
                d["wk"] = [AV(39936, [128, ND, 128]), AV(44032, [128, ND, 128])]
                if st == "l0ab":
                    d["l0ab_qbrow"] = AV(48128, [1, D])
                    d["l0ab_kbrow"] = AV(52224, [1, D])
                    d["l0ab_vbrow"] = AV(56320, [1, D])
                    d["l0ab_ocol"] = AV(60416, [128, 16], F32)
                    d["ropeC"] = [AV(60544, [128, 512]), AV(61568, [128, 512])]
                    d["ropeS"] = [AV(62592, [128, 512]), AV(63616, [128, 512])]
                    d["ropetmp"] = AV(64640, [128, 512])
                    d["nrm"] = AV(65664, [1, 512], F32)
                    d["nrm_bf"] = AV(67712, [1, 512])
                else:
                    d["ropeCw"] = [AV(48128, [128, D])]
                    d["ropeSw"] = [AV(52224, [128, D])]
                    d["lA_sb"] = AV(56320, [128, ND, 8])
                    l1all = AV(56576, [8, 3 * TOK])
                    d["l1all"] = l1all
                    d["l1q"] = l1all[:, 0:TOK]
                    d["l1k"] = l1all[:, TOK:2 * TOK]
                    d["l1v"] = l1all[:, 2 * TOK:3 * TOK]
                    d["lB_sb"] = AV(62720, [8, D])
                    d["ropetmpw"] = r.scr2_bf
                    d["nrm"] = AV(66816, [1, 512], F32)
                    d["nrm_bf"] = AV(68864, [1, 512])
            elif st == "l0ffn":
                d["wk"] = [AV(0, [128, ND, 128]), AV(4096, [128, ND, 128])]
                d["wrow"] = [AV(8192, [128, 8, 512]), AV(0, [128, 8, 512])]
                d["bcb"] = AV(16384, [128, D])
                d["rowc"] = [AV(20480, [1, D])]
                rbf = AV(24576, [1, D])
                d["row_bf"] = rbf
                d["row_bfin"] = rbf
                d["row_a"] = AV(28672, [1, D], F32)
                d["row_b"] = AV(36864, [1, D], F32)
                d["thought"] = AV(45056, [1, D], F32)
                d["t1h_bf"] = AV(53248, [1, 4096])
                d["b1h"] = AV(61440, [1, 4096])
                d["tcolT"] = AV(69632, [128, ND])
                d["t1colT"] = AV(69696, [128, 32])
                d["l0ffn_gcol"] = AV(69760, [128, 16], F32)
            return d

        stage_views = {st_: _views(st_) for st_ in set(stages)}

        sems = {}
        for sname in ["s_x", "s_v", "s_s", "s_t", "s_w", "s_g", "s_f", "s_p", "s_kv"]:
            sems[sname] = es.enter_context(nc.semaphore(sname))
            setattr(r, sname, sems[sname])
        r.s_wp = [es.enter_context(nc.semaphore(f"s_wp{i}")) for i in range(NWK)]
        r.s_wdp = [es.enter_context(nc.semaphore(f"s_wdp{i}")) for i in range(2)]
        r.s_wtp = r.s_wdp
        r.s_prep = es.enter_context(nc.semaphore("s_prep"))
        r.prep_plan = prep_plan
        block = es.enter_context(nc.Block())

        def emit_stage_barrier(E):
            allsems = [r.s_v, r.s_s, r.s_t, r.s_g, r.s_x, r.s_w, r.s_f,
                       r.s_kv, r.s_p] + r.s_wp + r.s_wdp
            for eng in (E.tensor, E.vector, E.scalar, E.gpsimd, E.sync):
                for sem_ in allsems:
                    c_ = E.c(sem_)
                    if c_ > 0:
                        eng.wait_ge(sem_, c_)

        def emit_stage_consts(E, st):
            g = E.gpsimd
            if st == "moe":
                g.dma_start(out=r.sel8_sb[:, :], in_=w["sel8"]).then_inc(r.s_w, 16)
            elif st == "l0ab":
                g.dma_start(out=r.l0ab_qbrow[0:1, :], in_=w["l0ab_qb"]).then_inc(r.s_w, 16)
                g.dma_start(out=r.l0ab_kbrow[0:1, :], in_=w["l0ab_kb"]).then_inc(r.s_w, 16)
                g.dma_start(out=r.l0ab_vbrow[0:1, :], in_=w["l0ab_vb"]).then_inc(r.s_w, 16)
                g.dma_start(out=r.l0ab_ocol[:, :], in_=w["l0ab_ocol_in"]).then_inc(r.s_w, 16)
            elif st == "l0ffn":
                g.dma_start(out=r.b1h[0:1, :], in_=w["l0ffn_b1h_in"]).then_inc(r.s_w, 16)
                g.dma_start(out=r.l0ffn_gcol[:, :], in_=w["l0ffn_gb"]).then_inc(r.s_w, 16)
            elif st == "l1ab":
                g.dma_start(out=r.bwT_sb[:, :, :],
                            in_=w["l1ab_bwT"].rearrange("(c p) e -> p c e", p=128)
                            ).then_inc(r.s_w, 16)
                g.dma_start(out=r.bb16[0:1, :], in_=w["l1ab_bb"]).then_inc(r.s_w, 16)
                g.dma_start(out=r.l1ab_cols[:, :], in_=w["l1ab_cols_in"]).then_inc(r.s_w, 16)
                g.dma_start(out=r.l1ab_vbrow[0:1, :], in_=w["l1ab_vbrow_in"]).then_inc(r.s_w, 16)
                g.dma_start(out=r.decayT[:, :], in_=w["decayT_in"]).then_inc(r.s_w, 16)
            elif st == "l1aa":
                g.dma_start(out=r.l1aa_bet_sb[:, :], in_=w["l1aa_bet_in"]).then_inc(r.s_w, 16)
                g.dma_start(out=r.selcol[:, :], in_=w["l1aa_sel_in"]).then_inc(r.s_w, 16)

        def prog(E):
            import os as _os
            r.DBG_NOGATE = bool(_os.environ.get("KDBG_NOGATE"))
            E.selfsem = {"vector": r.s_v, "scalar": r.s_s, "gpsimd": r.s_g}
            r.wk_lastuse = {}
            r.wd_lastuse = {}
            r.wrow_lastuse = {}
            r.wrow_rot = 0
            r.prep_pos = 0
            r.wtok_lastuse = r.wd_lastuse
            r.rope_lastuse = {}
            r.kv_lastuse = {}
            E.vector.memset(r.ones_row[0:1, :], 1.0)
            E.vector.memset(r.ones_col[:, 0:1], 1.0)
            if "l1aa" in stages:
                E.vector.memset(r.csc[:, 0:1], SC128)
            if need_exch:
                emit_exch_init(E, r, meta_in)
            E.gpsimd.dma_start(out=r.ident[:, :], in_=ident).then_inc(r.s_w, 16)
            if "triu_in" in w:
                E.gpsimd.dma_start(out=r.triu[:, :], in_=w["triu_in"]).then_inc(r.s_w, 16)
            if need_attn:
                E.gpsimd.dma_start(out=r.l0ab_mb[:, :], in_=w["l0ab_mb_in"]).then_inc(r.s_w, 16)
            E.gpsimd.wait_ge(r.s_w, E.c(r.s_w))
            r.exch_idx = 0
            r.w2sb_done = {}
            emit_load_x(E, r, x_in)
            for si_, st_ in enumerate(stages):
              with nc.named_scope(f"stg{si_}_{st_}"):
                emit_stage_barrier(E)
                for k_, v_ in stage_views[st_].items():
                    setattr(r, k_, v_)
                mark_ = prep_marks.get(st_)
                if mark_ is not None:
                    emit_prep_pump(E, r, mark_ - r.prep_pos)
                    E.gpsimd.wait_ge(r.s_prep, mark_ * 16)
                emit_stage_consts(E, st_)
                if st_ == "l0ffn":
                    emit_ln_T(E, r, out_tok=r.qstage)
                    emit_l0ffn(E, r, w, r.exch_idx, r.qstage)
                    r.exch_idx += 5
                    continue
                emit_ln_T(E, r)
                if st_ == "moe":
                    emit_moe(E, r, w)
                elif st_ == "l1ab":
                    emit_l1ab(E, r, w)
                elif st_ == "l0ab":
                    emit_l0ab(E, r, w, r.exch_idx)
                    r.exch_idx += 1
                elif st_ == "l0aa":
                    emit_l0aa(E, r, w, r.exch_idx)
                    r.exch_idx += 1
                elif st_ == "l1aa":
                    emit_l1aa(E, r, w, r.exch_idx)
                    r.exch_idx += NHG
            emit_store_x(E, r, x_out)
            E.sync.wait_ge(r.s_x, E.c(r.s_x))

        Emit(nc, block).build(prog)

    return nc


_prog_cache = {}


def _get_prog(stages):
    key = tuple(stages)
    if key not in _prog_cache:
        _prog_cache[key] = build_core(list(stages))
    return _prog_cache[key]


STAGE_ORDER = ["l0aa", "l0ffn", "l0ab", "l0ffn", "l1aa", "moe", "l1ab", "moe"]
EXCH_STAGES = ("l0aa", "l0ffn", "l0ab", "l1aa")


def _stage_weights(inputs, st, c, cache):
    if st in ("moe", "l1ab"):
        key = (st, 0)
    elif st in ("l0ffn", "l1aa"):
        key = (st, c % 2)
    else:
        key = (st, c)
    if key not in cache:
        if st == "moe":
            d = dict(moe_host_weights(inputs))
            d["sel8"] = _sel8()
        elif st == "l1ab":
            d = l1ab_host_weights(inputs)
        elif st == "l0ab":
            d = l0ab_host_weights(inputs, c)
        elif st == "l0aa":
            d = l0aa_host_weights(inputs, c)
        elif st == "l0ffn":
            d = l0ffn_host_weights(inputs, c)
        elif st == "l1aa":
            d = l1aa_host_weights(inputs, c)
        cache[key] = d
    return cache[key]


def kernel(**inputs):
    x = np.asarray(inputs["L0_attn_a_x"], np.float32)
    shards = _shard_x(x)
    ident = _ident128()
    wcache = {}
    rng = np.random.default_rng()
    nc = _get_prog(STAGE_ORDER)
    toks = rng.integers(1, 2 ** 30, size=NFLAGS).astype(np.int32)
    ins = []
    for c in range(NCORES):
        m = {"x_in": shards[c], "ident_in": ident}
        for st in ("l0aa", "l0ffn", "l0ab", "l1aa", "moe", "l1ab"):
            m.update(_stage_weights(inputs, st, c, wcache))
        meta = np.zeros((1, 2 + NFLAGS), np.int32)
        meta[0, 0] = c % 2
        meta[0, 1] = c
        meta[0, 2:] = toks
        m["meta_in"] = meta
        ins.append(m)
    res = run_bass_kernel_spmd(nc, ins, core_ids=list(range(NCORES)))
    out = np.empty((B, S, D), np.float32)
    for c in range(NCORES):
        b, h = c // 2, c % 2
        out[b, h * TOK:(h + 1) * TOK, :] = \
            np.ascontiguousarray(res.results[c]["x_out"], dtype=np.float32)
    return out
